# version 10
# speedup vs baseline: 1.2779x; 1.0022x over previous
from contextlib import ExitStack

import numpy as np
import ml_dtypes
import concourse.bass as bass
import concourse.mybir as mybir
from concourse.bass_utils import run_bass_kernel_spmd

F32 = mybir.dt.float32
BF16 = mybir.dt.bfloat16
I32 = mybir.dt.int32
U8 = mybir.dt.uint8
FP8 = mybir.dt.float8e4
AF = mybir.ActivationFunctionType
ALU = mybir.AluOpType
AX = mybir.AxisListType

NCORES = 8
B, S, D = 2, 4096, 2048
NTOK = B * S
TPC = NTOK // NCORES
DC = D // 128
MAGIC = float(np.float32(1.5 * 2 ** 23))

bf = ml_dtypes.bfloat16


class Ctx:

    def __init__(self, nc, pfx=None):
        self.nc = nc
        self.es = ExitStack()
        self.pfx = pfx
        self.handles = []

    def _n(self, name):
        return (self.pfx or "") + name

    def sb(self, name, shape, dtype=F32):
        return self.es.enter_context(self.nc.sbuf_tensor(self._n(name), shape, dtype))

    def ps(self, name, shape, dtype=F32):
        return self.es.enter_context(self.nc.psum_tensor(self._n(name), shape, dtype))

    def sem(self, name):
        if self.pfx is None:
            return self.es.enter_context(self.nc.semaphore(name))
        h = self.nc.alloc_semaphore(self._n(name))
        self.handles.append(h)
        return h

    def sems(self, *names):
        return [self.sem(n) for n in names]

    def block(self):
        return self.es.enter_context(self.nc.Block())

    def close(self):
        self.es.close()
        if self.pfx is not None:
            self.nc.all_engine_barrier()
            self.nc.clear_and_free_semaphores(self.handles)
            self.nc.all_engine_barrier()


def _io(mega, nc, name, shape, dtype, out=False):
    if mega is not None:
        return mega[name]
    return nc.dram_tensor(name, shape, dtype,
                          kind="ExternalOutput" if out else "ExternalInput").ap()


def to_fm(x):
    x = np.ascontiguousarray(x).reshape(-1, x.shape[-1])
    t, d = x.shape
    return np.ascontiguousarray(x.T.reshape(d // 128, 128, t))


def from_fm(xfm):
    dc, p, t = xfm.shape
    return np.ascontiguousarray(xfm.reshape(dc * p, t).T)


def shard_tok(x_fm):
    return [np.ascontiguousarray(x_fm[:, :, c * TPC:(c + 1) * TPC]) for c in range(NCORES)]


def build_combine_ln(has_combine, T=TPC, mega=None):
    nc = mega["nc"] if mega else bass.Bass("TRN2", target_bir_lowering=False)
    Xd = _io(mega, nc, "x", [DC, 128, T], F32)
    if has_combine:
        Sd = _io(mega, nc, "s", [DC, 128, T], F32)
        Gd = _io(mega, nc, "g", [DC, 128, T], F32)
        XNEWd = _io(mega, nc, "xnew", [DC, 128, T], F32, out=True)
    XNd = _io(mega, nc, "xn", [DC, 128, T], F32, out=True)
    NTB = T // 512

    cx = Ctx(nc, pfx=mega["pfx"] if mega else None)
    xt = cx.sb("xt", [128, DC, T], F32)
    st = cx.sb("st", [128, DC, T], F32)
    gt = cx.sb("gt", [128, DC, T], F32)
    gview = gt[:].bitcast(BF16)
    xb = gview[:, :, 0:T]
    sq = gview[:, :, T:2 * T]
    xnt = st[:]
    ones_b = cx.sb("ones_b", [128, 1], BF16)
    ones_f = cx.sb("ones_f", [1, 128], F32)
    mrow = cx.sb("mrow", [1, T], F32)
    vrow = cx.sb("vrow", [1, T], F32)
    rrow = cx.sb("rrow", [1, T], F32)
    psum_s = cx.ps("psum_s", [1, NTB, 512], F32)
    psum_q = cx.ps("psum_q", [1, NTB, 512], F32)
    pm = cx.ps("pm", [128, NTB, 512], F32)
    pr = cx.ps("pr", [128, NTB, 512], F32)
    (ld, cmb, prep, mm, rows, rst, bc, ap_, stm, vch, sch, lds, ldg,
     stm2) = cx.sems(
        "ld", "cmb", "prep", "mm", "rows", "rst", "bc", "ap", "stm", "vch", "sch",
        "lds", "ldg", "stm2")
    blk = cx.block()
    if True:
        nld = 3 if has_combine else 1

        @blk.sync
        def _(sync):
            sync.dma_start(out=xt[:], in_=Xd.rearrange("c p t -> p c t")).then_inc(ld, 16)

        @blk.scalar
        def _(sc):
            if has_combine:
                sc.dma_start(out=st[:], in_=Sd.rearrange("c p t -> p c t")).then_inc(lds, 16)
                sc.wait_ge(ldg, 16)
                sc.activation(out=gt[:], in_=gt[:], func=AF.Silu).then_inc(cmb, 1)
                sc.wait_ge(cmb, 2)
            else:
                sc.wait_ge(ld, 16)
            sc.activation(out=sq, in_=xt[:], func=AF.Square).then_inc(prep, 1)
            sc.wait_ge(rows, 1)
            sc.activation(out=rrow[:], in_=vrow[:], func=AF.Ln).then_inc(sch, 1)
            sc.wait_ge(sch, 1)
            sc.activation(out=rrow[:], in_=rrow[:], func=AF.Exp, scale=-0.5).then_inc(rst, 1)
            if has_combine:
                sc.wait_ge(ap_, 1)
                sc.dma_start(out=XNEWd.rearrange("c p t -> p c t"), in_=xt[:]).then_inc(stm2, 16)
                sc.wait_ge(stm2, 16)

        @blk.vector
        def _(v):
            nch = [0]

            def ch(instr):
                nch[0] += 1
                instr.then_inc(vch, 1)
                v.wait_ge(vch, nch[0])

            if has_combine:
                v.wait_ge(cmb, 1)
                v.wait_ge(ld, 16)
                v.wait_ge(lds, 16)
                ch(v.tensor_tensor(st[:], st[:], gt[:], op=ALU.mult))
                ch(v.tensor_tensor(xt[:], xt[:], st[:], op=ALU.add))
                v.sem_inc(cmb, 1)
            else:
                v.wait_ge(ld, 16)
            v.tensor_copy(xb, xt[:]).then_inc(prep, 1)
            v.wait_ge(mm, NTB)
            rs = 1.0 / D
            for tb in range(NTB):
                sl = slice(tb * 512, (tb + 1) * 512)
                v.tensor_scalar(mrow[:, sl], psum_s[0:1, tb, :], rs, None, op0=ALU.mult)
                ch(v.tensor_scalar(vrow[:, sl], psum_q[0:1, tb, :], rs, None, op0=ALU.mult))
            ch(v.tensor_tensor(rrow[:], mrow[:], mrow[:], op=ALU.mult))
            ch(v.tensor_tensor(vrow[:], vrow[:], rrow[:], op=ALU.subtract))
            v.tensor_scalar(vrow[:], vrow[:], 1e-5, None, op0=ALU.add).then_inc(rows, 1)
            for tb in range(NTB):
                v.wait_ge(bc, tb + 1)
                sl = slice(tb * 512, (tb + 1) * 512)
                ch(v.tensor_tensor(xnt[:, :, sl], xt[:, :, sl],
                                   pm[:, tb, :].unsqueeze(1).broadcast_to([128, DC, 512]), op=ALU.subtract))
                last = v.tensor_tensor(xnt[:, :, sl], xnt[:, :, sl],
                                       pr[:, tb, :].unsqueeze(1).broadcast_to([128, DC, 512]), op=ALU.mult)
            last.then_inc(ap_, 1)

        @blk.gpsimd
        def _(g):
            if has_combine:
                g.dma_start(out=gt[:], in_=Gd.rearrange("c p t -> p c t")).then_inc(ldg, 16)
            else:
                g.sem_inc(ldg, 16)
            g.memset(ones_b[:], 1.0)
            g.memset(ones_f[:], 1.0).then_inc(prep, 1)
            g.wait_ge(ap_, 1)
            g.dma_start(out=XNd.rearrange("c p t -> p c t"), in_=xnt).then_inc(stm, 16)
            g.wait_ge(stm, 16)

        @blk.tensor
        def _(t):
            t.wait_ge(prep, 3)
            for tb in range(NTB):
                sl = slice(tb * 512, (tb + 1) * 512)
                for c in range(DC):
                    t.matmul(psum_s[0:1, tb, :], ones_b[:], xb[:, c, sl], start=(c == 0), stop=(c == DC - 1))
                for c in range(DC):
                    mmi = t.matmul(psum_q[0:1, tb, :], ones_b[:], sq[:, c, sl], start=(c == 0), stop=(c == DC - 1))
                mmi.then_inc(mm, 1)
            t.wait_ge(rst, 1)
            for tb in range(NTB):
                sl = slice(tb * 512, (tb + 1) * 512)
                t.matmul(pm[:, tb, :], ones_f[:], mrow[:, sl], start=True, stop=True)
                t.matmul(pr[:, tb, :], ones_f[:], rrow[:, sl], start=True, stop=True).then_inc(bc, 1)

    cx.close()
    return nc



L0_H, L0_D, L0_BS = 32, 64, 256
L0_HPC = L0_H // NCORES
L0_NB = S // L0_BS


def l0_decay_consts(heads):
    H, BS = L0_H, L0_BS
    base = 1.0 / 2.0 ** (8.0 / H)
    rate = (base ** (np.arange(H, dtype=np.float32) + 1.0)).astype(np.float32)
    factor = np.float32(1.0 - 0.0 / (32 - 1 + 1e-05) + 1e-05)
    slope = (rate * np.float32(factor)).astype(np.float32)[heads]
    r = np.arange(BS, dtype=np.float32) + 1.0
    qd = np.exp(-slope[:, None] * r[None, :]).astype(np.float32)
    kd = np.exp(-slope[:, None] * (BS - r[None, :])).astype(np.float32)
    dd = slope[:, None, None].astype(np.float64) * (r[:, None] - r[None, :])[None]
    dd = np.where(dd >= 0, -dd, -np.inf)
    dd = np.exp(dd).astype(np.float32)
    bd = np.exp(-slope * np.float32(BS)).astype(np.float32)
    ddT = np.ascontiguousarray(np.transpose(dd, (0, 2, 1)))
    return qd, kd, ddT, bd


def build_l0_attn(bd4, mega=None):
    nc = mega["nc"] if mega else bass.Bass("TRN2", target_bir_lowering=False)
    XNd = _io(mega, nc, "xn", [DC, 128, NTOK], F32)
    Wd = _io(mega, nc, "wq", [DC, 128, 768], F32)
    DDd = _io(mega, nc, "ddt", [L0_HPC, 2, 128, L0_BS], F32)
    QDd = _io(mega, nc, "qdb", [64, L0_HPC, L0_BS], F32)
    KDd = _io(mega, nc, "kdc_d", [128, 2, L0_HPC], F32)
    IDd = _io(mega, nc, "ident", [128, 128], F32)
    BDd = _io(mega, nc, "bd4", [64, L0_HPC], F32)
    ATd = _io(mega, nc, "attn", [L0_HPC, 64, NTOK], BF16, out=True)

    NBLK = 2 * L0_NB if nblk is None else nblk
    HPC, BS = L0_HPC, L0_BS

    cx = Ctx(nc, pfx=mega["pfx"] if mega else None)
    wt = cx.sb("wt", [128, DC, 768], BF16)
    xnt = cx.sb("xnt", [128, 2, DC, BS], BF16)
    qkv = cx.sb("qkv", [128, 2, 6, BS], BF16)
    ddt = cx.sb("ddt", [128, HPC, 2, BS], BF16)
    qdb = cx.sb("qdb", [64, HPC, BS], F32)
    kdc = cx.sb("kdc", [128, 2, HPC], F32)
    iden = cx.sb("iden", [128, 64], BF16)
    bdt = cx.sb("bdt", [64, HPC], F32)
    qp = cx.sb("qp", [64, BS], BF16)
    pt = cx.sb("pt", [128, 2, BS], BF16)
    ktm = cx.sb("ktm", [128, 2, 64], BF16)
    vtm = cx.sb("vtm", [128, 2, 64], BF16)
    state = cx.sb("state", [64, HPC, 64], F32)
    stateb = cx.sb("stateb", [64, HPC, 64], BF16)
    atile = cx.sb("atile", [64, 2, HPC, BS], BF16)

    pproj = cx.ps("pproj", [128, 6, BS], F32)
    pst = cx.ps("pst", [128, 2, BS], F32)
    ptr = cx.ps("ptr", [128, 4, 64], BF16)
    po = cx.ps("po", [64, BS], F32)
    ptm = cx.ps("ptm", [64, 64], F32)

    (wld, wldg, xnld, projmm, silu, tpose, ev1, mmS, pT, mmO, mmT,
     evO, stup, outd, chn) = cx.sems("wld", "wldg", "xnld", "projmm", "silu", "tpose",
                                "ev1", "mmS", "pT", "mmO", "mmT", "evO", "stup", "outd", "chn")
    blk = cx.block()
    if True:
        def tok0(bi):
            sq, b = divmod(bi, L0_NB)
            return sq * S + b * BS

        def nresets(bi):
            return HPC * (1 + (1 if bi >= L0_NB else 0))

        @blk.sync
        def _(sync):
            sync.dma_start(out=wt[:], in_=Wd.rearrange("c p m -> p c m")).then_inc(wld, 16)
            sync.dma_start(out=ddt[:], in_=DDd.rearrange("h c p i -> p h c i")).then_inc(wld, 16)
            sync.dma_start(out=qdb[:], in_=QDd).then_inc(wld, 16)
            sync.dma_start(out=kdc[:], in_=KDd).then_inc(wld, 16)
            sync.dma_start(out=iden[:], in_=IDd).then_inc(wld, 16)
            sync.dma_start(out=bdt[:], in_=BDd).then_inc(wld, 16)

        @blk.gpsimd
        def _(g):
            for bi in range(NBLK):
                if bi >= 2:
                    g.wait_ge(projmm, bi - 1)
                t0 = tok0(bi)
                g.dma_start(out=xnt[:, bi % 2, :, :],
                            in_=XNd[:, :, t0:t0 + BS].rearrange("c p t -> p c t")).then_inc(xnld, 16)
            for bi in range(NBLK):
                g.wait_ge(evO, HPC * (bi + 1))
                t0 = tok0(bi)
                g.dma_start(out=ATd[:, :, t0:t0 + BS].rearrange("h d t -> d h t"),
                            in_=atile[:, bi % 2, :, :]).then_inc(outd, 16)
            g.wait_ge(outd, 16 * NBLK)

        @blk.tensor
        def _(t):
            t.wait_ge(wld, 16 * 5)
            t.wait_ge(wldg, 16)
            for bi in range(NBLK):
                t.wait_ge(xnld, 16 * (bi + 1))
                t.wait_ge(silu, bi)
                for o in range(6):
                    for c in range(DC):
                        mmi = t.matmul(pproj[:, o, :], wt[:, c, o * 128:(o + 1) * 128],
                                       xnt[:, bi % 2, c, :], start=(c == 0), stop=(c == DC - 1))
                mmi.then_inc(projmm, 1)
                t.wait_ge(silu, bi + 1)
                for h in range(HPC):
                    idx = bi * HPC + h + 1
                    pidx = (bi - 1) * HPC + h + 1
                    if bi > 0:
                        t.wait_ge(ev1, pidx)
                    ko, kp = 2 + h // 2, (h % 2) * 64
                    vo, vp = 4 + h // 2, (h % 2) * 64
                    for jc in range(2):
                        t.transpose(ptr[:, jc, :], qkv[ko * 0 + kp:kp + 64, bi % 2, ko, jc * 128:(jc + 1) * 128], iden[:])
                        mmi = t.transpose(ptr[:, 2 + jc, :], qkv[vp:vp + 64, bi % 2, vo, jc * 128:(jc + 1) * 128], iden[:])
                    mmi.then_inc(tpose, 1)
                for h in range(HPC):
                    idx = bi * HPC + h + 1
                    pidx = (bi - 1) * HPC + h + 1
                    if bi > 0:
                        t.wait_ge(pT, pidx)
                    qo, qp_ = h // 2, (h % 2) * 64
                    ko, kp = 2 + h // 2, (h % 2) * 64
                    for jc in range(2):
                        mmi = t.matmul(pst[:, jc, :], qkv[kp:kp + 64, bi % 2, ko, jc * 128:(jc + 1) * 128],
                                       qkv[qp_:qp_ + 64, bi % 2, qo, :], start=True, stop=True)
                    mmi.then_inc(mmS, 1)
                for h in range(HPC):
                    idx = bi * HPC + h + 1
                    pidx = (bi - 1) * HPC + h + 1
                    t.wait_ge(pT, idx)
                    if bi > 0:
                        t.wait_ge(evO, pidx)
                    t.wait_ge(stup, (bi - 1) * HPC + h + 1 + nresets(bi))
                    t.matmul(po[:], vtm2(cx, bi, h)[0][:, 0, :] if False else vtm[:, 0, :], pt[:, 0, :], start=True, stop=False) if False else None
                    t.matmul(po[:], vtmh(h)[:, 0, :] if False else vtm_h(h, 0), pt[:, 0, :], start=True, stop=False) if False else None
                    pass
    cx.close()
    return nc



L0_H, L0_D, L0_BS = 32, 64, 256
L0_HPC = L0_H // NCORES
L0_NB = S // L0_BS


def l0_decay_consts(heads):
    H, BS = L0_H, L0_BS
    base = 1.0 / 2.0 ** (8.0 / H)
    rate = (base ** (np.arange(H, dtype=np.float32) + 1.0)).astype(np.float32)
    factor = np.float32(1.0 - 0.0 / (32 - 1 + 1e-05) + 1e-05)
    slope = (rate * factor).astype(np.float32)[heads]
    r = np.arange(BS, dtype=np.float32) + 1.0
    qd = np.exp(-slope[:, None] * r[None, :]).astype(np.float32)
    kd = np.exp(-slope[:, None] * (BS - r[None, :])).astype(np.float32)
    dd = slope[:, None, None] * (r[:, None] - r[None, :])[None]
    dd = np.where(dd >= 0, -dd, -np.inf)
    dd = np.exp(dd).astype(np.float32)
    bd = np.exp(-slope * np.float32(BS)).astype(np.float32)
    ddT = np.ascontiguousarray(np.transpose(dd, (0, 2, 1)))
    return qd, kd, ddT, bd


def build_l0_attn(nblk=None, mega=None):
    nc = mega["nc"] if mega else bass.Bass("TRN2", target_bir_lowering=False)
    XNd = _io(mega, nc, "xn", [DC, 128, NTOK], F32)
    Wd = _io(mega, nc, "wq", [DC, 128, 768], F32)
    DDd = _io(mega, nc, "ddt_d", [L0_HPC, 2, 128, L0_BS], BF16)
    QDd = _io(mega, nc, "qdb_d", [128, 2, L0_BS], BF16)
    KDd = _io(mega, nc, "kdc_d", [128, 2, L0_HPC], F32)
    IDd = _io(mega, nc, "ident", [128, 64], BF16)
    BDd = _io(mega, nc, "bd4", [64, L0_HPC], F32)
    ATd = _io(mega, nc, "attn", [L0_HPC, 64, NTOK], BF16, out=True)

    NBLK = 2 * L0_NB if nblk is None else nblk
    HPC, BS = L0_HPC, L0_BS

    cx = Ctx(nc, pfx=mega["pfx"] if mega else None)
    wt = cx.sb("wt", [128, DC, 768], BF16)
    xnt = cx.sb("xnt", [128, 2, DC, BS], BF16)
    qkv = cx.sb("qkv", [128, 2, 6, BS], BF16)
    ddt = cx.sb("ddt", [128, HPC, 2, BS], BF16)
    qdb = cx.sb("qdb", [128, 2, BS], BF16)
    kdc = cx.sb("kdc", [128, 2, HPC], F32)
    iden = cx.sb("iden", [128, 64], BF16)
    bdt = cx.sb("bdt", [64, HPC], F32)
    qp = cx.sb("qp", [64, HPC, BS], BF16)
    pt = cx.sb("pt", [128, HPC, 2, BS], BF16)
    ktm = cx.sb("ktm", [128, HPC, 2, 64], BF16)
    vtm = cx.sb("vtm", [128, HPC, 2, 64], BF16)
    state = cx.sb("state", [64, HPC, 64], F32)
    stateb = cx.sb("stateb", [64, HPC, 64], BF16)
    atile = cx.sb("atile", [64, 2, HPC, BS], BF16)

    pproj = cx.ps("pproj", [128, 6, BS], F32)
    pst = cx.ps("pst", [128, 2, BS], F32)
    ptr = cx.ps("ptr", [128, 4, 64], BF16)
    po = cx.ps("po", [64, BS], F32)
    ptm = cx.ps("ptm", [64, 64], F32)

    (wld, wldg, xnld, projmm, silu, tpose, ev1, mmS, pT, mmO, mmT,
     evO, stup, outd, chn) = cx.sems("wld", "wldg", "xnld", "projmm", "silu", "tpose",
                                "ev1", "mmS", "pT", "mmO", "mmT", "evO", "stup", "outd", "chn")
    blk = cx.block()
    if True:
        def tok0(bi):
            sq, b = divmod(bi, L0_NB)
            return sq * S + b * BS

        def R(bi):
            return HPC * (1 + (1 if bi >= L0_NB else 0))

        @blk.sync
        def _(sync):
            sync.dma_start(out=ddt[:], in_=DDd.rearrange("h c p i -> p h c i")).then_inc(wld, 16)
            sync.dma_start(out=qdb[:], in_=QDd).then_inc(wld, 16)
            sync.dma_start(out=kdc[:], in_=KDd).then_inc(wld, 16)
            sync.dma_start(out=iden[:], in_=IDd).then_inc(wld, 16)
            sync.dma_start(out=bdt[:], in_=BDd).then_inc(wld, 16)

        @blk.gpsimd
        def _(g):
            g.dma_start(out=wt[:], in_=Wd.rearrange("c p m -> p c m")).then_inc(wldg, 16)
            for bi in range(NBLK):
                if bi >= 1:
                    g.wait_ge(xnld, 16 * bi)
                if bi >= 2:
                    g.wait_ge(projmm, bi - 1)
                t0 = tok0(bi)
                g.dma_start(out=xnt[:, bi % 2, :, :],
                            in_=XNd[:, :, t0:t0 + BS].rearrange("c p t -> p c t")).then_inc(xnld, 16)
                if bi >= 1:
                    g.wait_ge(evO, HPC * bi)
                    g.wait_ge(outd, 16 * (bi - 1))
                    t0p = tok0(bi - 1)
                    g.dma_start(out=ATd[:, :, t0p:t0p + BS].rearrange("h d t -> d h t"),
                                in_=atile[:, (bi - 1) % 2, :, :]).then_inc(outd, 16)
            g.wait_ge(evO, HPC * NBLK)
            g.wait_ge(outd, 16 * (NBLK - 1))
            t0p = tok0(NBLK - 1)
            g.dma_start(out=ATd[:, :, t0p:t0p + BS].rearrange("h d t -> d h t"),
                        in_=atile[:, (NBLK - 1) % 2, :, :]).then_inc(outd, 16)
            g.wait_ge(outd, 16 * NBLK)

        @blk.scalar
        def _(sc):
            for bi in range(NBLK):
                sc.wait_ge(projmm, bi + 1)
                if bi >= 2:
                    sc.wait_ge(mmS, HPC * (bi - 1))
                    sc.wait_ge(pT, HPC * (bi - 1))
                for bk in range(3):
                    mmi = sc.activation(out=qkv[:, bi % 2, 2 * bk:2 * bk + 2, :],
                                        in_=pproj[:, 2 * bk:2 * bk + 2, :], func=AF.Silu)
                mmi.then_inc(silu, 1)

        @blk.tensor
        def _(t):
            t.wait_ge(wld, 16 * 5)
            t.wait_ge(wldg, 16)
            for bi in range(NBLK):
                e0 = bi * HPC
                t.wait_ge(xnld, 16 * (bi + 1))
                t.wait_ge(silu, bi)
                for o in range(6):
                    for c in range(DC):
                        mmi = t.matmul(pproj[:, o, :], wt[:, c, o * 128:(o + 1) * 128],
                                       xnt[:, bi % 2, c, :], start=(c == 0), stop=(c == DC - 1))
                mmi.then_inc(projmm, 1)
                t.wait_ge(silu, bi + 1)
                for h in range(HPC):
                    e = e0 + h
                    ko, kp = 2 + h // 2, (h % 2) * 64
                    vo = 4 + h // 2
                    t.wait_ge(ev1, e)
                    for jc in range(2):
                        t.transpose(ptr[:, jc, :], qkv[kp:kp + 64, bi % 2, ko, jc * 128:(jc + 1) * 128], iden[kp:kp + 64, :])
                        mmi = t.transpose(ptr[:, 2 + jc, :], qkv[kp:kp + 64, bi % 2, vo, jc * 128:(jc + 1) * 128], iden[kp:kp + 64, :])
                    mmi.then_inc(tpose, 1)
                for h in range(HPC):
                    e = e0 + h
                    qo, qpp = h // 2, (h % 2) * 64
                    ko, kp = 2 + h // 2, (h % 2) * 64
                    t.wait_ge(pT, e)
                    for jc in range(2):
                        mmi = t.matmul(pst[:, jc, :], qkv[kp:kp + 64, bi % 2, ko, jc * 128:(jc + 1) * 128],
                                       qkv[qpp:qpp + 64, bi % 2, qo, :], start=True, stop=True)
                    mmi.then_inc(mmS, 1)
                for h in range(HPC):
                    e = e0 + h
                    t.wait_ge(pT, e + 1)
                    t.wait_ge(evO, e)
                    t.wait_ge(stup, (bi - 1) * HPC + h + 1 + R(bi))
                    t.matmul(po[:], vtm[:, h, 0, :], pt[:, h, 0, :], start=True, stop=False)
                    t.matmul(po[:], vtm[:, h, 1, :], pt[:, h, 1, :], start=False, stop=False)
                    t.matmul(po[:], stateb[:, h, :], qp[:, h, :], start=False, stop=True).then_inc(mmO, 1)
                for h in range(HPC):
                    e = e0 + h
                    t.wait_ge(ev1, e + 1)
                    t.wait_ge(stup, e + R(bi))
                    t.matmul(ptm[:], ktm[:, h, 0, :], vtm[:, h, 0, :], start=True, stop=False)
                    t.matmul(ptm[:], ktm[:, h, 1, :], vtm[:, h, 1, :], start=False, stop=True).then_inc(mmT, 1)

        @blk.vector
        def _(v):
            v.wait_ge(wld, 16 * 5)
            v.wait_ge(wldg, 16)
            v.memset(state[:], 0.0)
            v.memset(stateb[:], 0.0).then_inc(stup, HPC)
            for bi in range(NBLK):
                e0 = bi * HPC
                if bi % L0_NB == 0 and bi > 0:
                    v.memset(state[:], 0.0)
                    v.memset(stateb[:], 0.0).then_inc(stup, HPC)
                for h in range(HPC):
                    e = e0 + h
                    v.wait_ge(tpose, e + 1)
                    if bi > 0:
                        v.wait_ge(mmT, (bi - 1) * HPC + h + 1)
                    for jc in range(2):
                        v.tensor_scalar(ktm[:, h, jc, :], ptr[:, jc, :], kdc[:, jc, h:h + 1], None, op0=ALU.mult)
                    mmi = v.tensor_copy(vtm[:, h, :, :], ptr[:, 2:4, :])
                    mmi.then_inc(ev1, 1)
                for h in range(HPC):
                    e = e0 + h
                    v.wait_ge(mmS, e + 1)
                    if bi > 0:
                        v.wait_ge(mmO, (bi - 1) * HPC + h + 1)
                    qo, qpp = h // 2, (h % 2) * 64
                    v.tensor_tensor(qp[:, h, :], qkv[qpp:qpp + 64, bi % 2, qo, :], qdb[qpp:qpp + 64, qo, :], op=ALU.mult)
                    v.tensor_tensor(pt[:, h, :, :], pst[:], ddt[:, h, :, :], op=ALU.mult).then_inc(pT, 1)
                for h in range(HPC):
                    e = e0 + h
                    v.wait_ge(mmO, e + 1)
                    if bi >= 1:
                        v.wait_ge(outd, 16 * (bi - 1))
                    v.tensor_copy(atile[:, bi % 2, h, :], po[:]).then_inc(evO, 1)
                for h in range(HPC):
                    e = e0 + h
                    v.wait_ge(mmT, e + 1)
                    v.scalar_tensor_tensor(state[:, h, :], state[:, h, :], bdt[:, h:h + 1],
                                           ptm[:], op0=ALU.mult, op1=ALU.add).then_inc(chn, 1)
                    v.wait_ge(chn, bi * HPC + h + 1)
                    v.tensor_copy(stateb[:, h, :], state[:, h, :]).then_inc(stup, 1)

    cx.close()
    return nc


def l0_attn_host_inputs(xn_fm_full):
    maps = []
    ident = np.concatenate([np.eye(64, dtype=np.float32)] * 2, 0).astype(bf)
    for c in range(NCORES):
        heads = list(range(c * L0_HPC, (c + 1) * L0_HPC))
        qd, kd, ddT, bd = l0_decay_consts(heads)
        qdb = np.zeros((128, 2, L0_BS), np.float32)
        for h in range(L0_HPC):
            qdb[(h % 2) * 64:(h % 2) * 64 + 64, h // 2, :] = qd[h][None, :]
        qdb = qdb.astype(bf)
        kdc = np.zeros((128, 2, L0_HPC), np.float32)
        for jc in range(2):
            kdc[:, jc, :] = kd[:, jc * 128:(jc + 1) * 128].T
        ddt = ddT.reshape(L0_HPC, 2, 128, L0_BS).astype(bf)
        bd4 = np.ascontiguousarray(np.broadcast_to(bd[None, :], (64, L0_HPC)))
        maps.append({"xn": xn_fm_full, "ddt_d": ddt, "qdb_d": qdb, "kdc_d": kdc,
                     "ident": ident, "bd4": bd4})
    return maps



def build_l0_post(T=TPC, mega=None):
    nc = mega["nc"] if mega else bass.Bass("TRN2", target_bir_lowering=False)
    ATd = _io(mega, nc, "attn", [DC, 128, T], BF16)
    XNd = _io(mega, nc, "xn", [DC, 128, T], F32)
    NWd = _io(mega, nc, "nw", [128, DC], F32)
    WGd = _io(mega, nc, "wg", [DC, 128, D], F32)
    WOd = _io(mega, nc, "wo", [DC, 128, D], F32)
    S0d = _io(mega, nc, "s0", [DC, 128, T], F32, out=True)
    NTB = T // 512

    cx = Ctx(nc, pfx=mega["pfx"] if mega else None)
    at = cx.sb("at", [128, DC, T], BF16)
    xnb = cx.sb("xnb", [128, DC, T], BF16)
    m = cx.sb("m", [128, DC, T], BF16)
    nw = cx.sb("nw_s", [128, DC], F32)
    wt = cx.sb("wtl", [128, DC, D], BF16)
    ones_b = cx.sb("ones_b", [128, 1], BF16)
    ones_f = cx.sb("ones_f", [1, 128], F32)
    rrow = cx.sb("rrow", [1, T], F32)
    so = cx.sb("so", [128, T], F32)
    pq = cx.ps("pq", [1, NTB, 512], F32)
    pr = cx.ps("pr", [128, NTB, 512], F32)
    pg = cx.ps("pg", [128, 2, 512], F32)
    (ld, ldw, sq_s, mm, rows, rst, bc, ms, gmm, gev, w2, omm, oev, stm,
     vch, sch) = cx.sems("ld", "ldw", "sq", "mm", "rows", "rst", "bc", "ms",
                         "gmm", "gev", "w2", "omm", "oev", "stm", "vch", "sch")
    blk = cx.block()
    if True:
        @blk.sync
        def _(sync):
            sync.dma_start(out=at[:], in_=ATd.rearrange("c p t -> p c t")).then_inc(ld, 16)
            sync.dma_start(out=nw[:], in_=NWd).then_inc(ld, 16)

        @blk.gpsimd
        def _(g):
            g.memset(ones_b[:], 1.0)
            g.memset(ones_f[:], 1.0).then_inc(ms, 1)
            g.dma_start(out=xnb[:], in_=XNd.rearrange("c p t -> p c t")).then_inc(ldw, 16)
            g.wait_ge(ldw, 16)
            g.dma_start(out=wt[:], in_=WGd.rearrange("c p m -> p c m")).then_inc(ldw, 16)
            g.wait_ge(gmm, 16 * NTB)
            g.dma_start(out=wt[:], in_=WOd.rearrange("c p m -> p c m")).then_inc(w2, 16)
            for o in range(DC):
                g.wait_ge(oev, NTB * (o + 1))
                g.wait_ge(stm, 16 * o)
                g.dma_start(out=S0d[o], in_=so[:]).then_inc(stm, 16)
            g.wait_ge(stm, 16 * DC)

        @blk.scalar
        def _(sc):
            sc.wait_ge(ld, 32)
            sc.activation(out=m[:], in_=at[:], func=AF.Square).then_inc(sq_s, 1)
            sc.wait_ge(rows, 1)
            sc.activation(out=rrow[:], in_=rrow[:], func=AF.Ln).then_inc(sch, 1)
            sc.wait_ge(sch, 1)
            sc.activation(out=rrow[:], in_=rrow[:], func=AF.Exp, scale=-0.5).then_inc(rst, 1)
            for i in range(DC * NTB):
                sc.wait_ge(gmm, i + 1)
                o, tb = divmod(i, NTB)
                sl = slice(tb * 512, (tb + 1) * 512)
                sc.activation(out=at[:, o, sl], in_=pg[:, i % 2, :], func=AF.Sigmoid).then_inc(gev, 1)

        @blk.vector
        def _(v):
            nch = [0]

            def ch(instr):
                nch[0] += 1
                instr.then_inc(vch, 1)
                v.wait_ge(vch, nch[0])

            v.wait_ge(mm, NTB)
            for tb in range(NTB):
                sl = slice(tb * 512, (tb + 1) * 512)
                ch(v.tensor_scalar(rrow[:, sl], pq[0:1, tb, :], 1.0 / D, 1e-5, op0=ALU.mult, op1=ALU.add))
            v.sem_inc(rows, 1)
            for tb in range(NTB):
                v.wait_ge(bc, tb + 1)
                sl = slice(tb * 512, (tb + 1) * 512)
                for c in range(DC):
                    mmi = v.scalar_tensor_tensor(m[:, c, sl], at[:, c, sl], nw[:, c:c + 1],
                                                 pr[:, tb, :], op0=ALU.mult, op1=ALU.mult)
            mmi.then_inc(ms, 1)
            for i in range(DC * NTB):
                v.wait_ge(gev, i + 1)
                o, tb = divmod(i, NTB)
                sl = slice(tb * 512, (tb + 1) * 512)
                v.tensor_tensor(at[:, o, sl], at[:, o, sl], m[:, o, sl], op=ALU.mult).then_inc(ms, 1)
            for i in range(DC * NTB):
                v.wait_ge(omm, i + 1)
                o, tb = divmod(i, NTB)
                sl = slice(tb * 512, (tb + 1) * 512)
                if tb == 0:
                    v.wait_ge(stm, 16 * o)
                v.tensor_copy(so[:, sl], pg[:, i % 2, :]).then_inc(oev, 1)

        @blk.tensor
        def _(t):
            t.wait_ge(sq_s, 1)
            t.wait_ge(ms, 1)
            for tb in range(NTB):
                sl = slice(tb * 512, (tb + 1) * 512)
                for c in range(DC):
                    mmi = t.matmul(pq[0:1, tb, :], ones_b[:], m[:, c, sl], start=(c == 0), stop=(c == DC - 1))
                mmi.then_inc(mm, 1)
            t.wait_ge(rst, 1)
            for tb in range(NTB):
                sl = slice(tb * 512, (tb + 1) * 512)
                t.matmul(pr[:, tb, :], ones_f[:], rrow[:, sl], start=True, stop=True).then_inc(bc, 1)
            t.wait_ge(ldw, 32)
            t.wait_ge(ms, 2)
            for i in range(DC * NTB):
                o, tb = divmod(i, NTB)
                sl = slice(tb * 512, (tb + 1) * 512)
                t.wait_ge(gev, max(0, i - 1))
                for c in range(DC):
                    mmi = t.matmul(pg[:, i % 2, :], wt[:, c, o * 128:(o + 1) * 128],
                                   xnb[:, c, sl], start=(c == 0), stop=(c == DC - 1))
                mmi.then_inc(gmm, 1)
            t.wait_ge(w2, 16)
            t.wait_ge(ms, 2 + DC * NTB)
            for i in range(DC * NTB):
                o, tb = divmod(i, NTB)
                sl = slice(tb * 512, (tb + 1) * 512)
                t.wait_ge(oev, max(0, i - 1))
                for c in range(DC):
                    mmi = t.matmul(pg[:, i % 2, :], wt[:, c, o * 128:(o + 1) * 128],
                                   at[:, c, sl], start=(c == 0), stop=(c == DC - 1))
                mmi.then_inc(omm, 1)

    cx.close()
    return nc



def emit_mx_quant_row(v, ch, out_bf, t_f32, scr, rs, ts2, nb):
    P, Fd = t_f32.shape[0], t_f32.shape[-1]
    tv = t_f32.rearrange("p (b w) -> p b w", w=32)
    ch(v.tensor_reduce(rs, tv, axis=AX.X, op=ALU.max, apply_absolute_value=True))
    ch(v.tensor_scalar(rs, rs, 1.0 / 448.0, 1e-30, op0=ALU.mult, op1=ALU.max))
    ch(v.tensor_scalar(ts2, rs, 2.0, None, op0=ALU.mult))
    ch(v.reciprocal(rs, rs))
    ch(v.tensor_scalar(rs, rs, 0.5, None, op0=ALU.mult))
    rsb = rs.unsqueeze(2).broadcast_to([P, nb, 32])
    tsb = ts2.unsqueeze(2).broadcast_to([P, nb, 32])
    scrv = scr.rearrange("p (b w) -> p b w", w=32)
    outv = out_bf.rearrange("p (b w) -> p b w", w=32)
    ch(v.tensor_tensor(scrv, tv, rsb, op=ALU.mult))
    ch(v.tensor_tensor(outv, scrv, tsb, op=ALU.mult))


def emit_i8_scale(v, ch, sc_out, rc_out, amax_col, s0, s1):
    ch(v.tensor_scalar(s0, amax_col, 0.0, None, op0=ALU.is_gt))
    ch(v.tensor_scalar(sc_out, amax_col, 1.0 / 127.0, None, op0=ALU.mult))
    ch(v.tensor_tensor(sc_out, sc_out, s0, op=ALU.mult))
    ch(v.tensor_scalar(s1, s0, -1.0, 1.0, op0=ALU.mult, op1=ALU.add))
    ch(v.tensor_tensor(sc_out, sc_out, s1, op=ALU.add))
    ch(v.reciprocal(rc_out, sc_out))


def build_prep(trans=False, mega=None):
    nc = mega["nc"] if mega else bass.Bass("TRN2", target_bir_lowering=False)
    GS = _io(mega, nc, "gsl", [1024, 2048], F32)
    US = _io(mega, nc, "usl", [1024, 2048], F32)
    DS = _io(mega, nc, "dsl", [256, 8192], F32)
    KS = _io(mega, nc, "ksl", [256, 2048], F32)
    B2 = _io(mega, nc, "bias2", [1, 2048], F32)
    RP = _io(mega, nc, "relp", [1023, 64], F32)
    if not trans:
        GQ = _io(mega, nc, "gq", [1024, 2048], BF16, out=True)
        UQ = _io(mega, nc, "uq", [1024, 2048], BF16, out=True)
        DQ = _io(mega, nc, "dq", [256, 8192], BF16, out=True)
        KQ = _io(mega, nc, "kq", [256, 2048], BF16, out=True)
    else:
        GQ = UQ = DQ = KQ = None
        WGUc = _io(mega, nc, "wguc", [2, DC, 128, 1024], BF16, out=True)
        WDQc = _io(mega, nc, "wdqc", [64, 128, 256], BF16, out=True)
        KQWc = _io(mega, nc, "kqwc", [DC, 128, 256], BF16, out=True)
        IDP = _io(mega, nc, "idp", [128, 128], BF16)
    BQ = _io(mega, nc, "bqv", [1, 2048], F32, out=True)
    BV = _io(mega, nc, "bv8", [1023, 1], F32, out=True)
    KSC = _io(mega, nc, "ksc", [2, 128], F32, out=True)

    cx = Ctx(nc, pfx=mega["pfx"] if mega else None)
    tin = cx.sb("tin", [128, 2, 8192], F32)
    tout = cx.sb("tout", [128, 2, 8192], BF16)
    scr8 = cx.sb("scr8", [128, 8192], FP8)
    rs = cx.sb("rs", [128, 256], F32)
    ts2 = cx.sb("ts2", [128, 256], F32)
    rp = cx.sb("rp", [128, 8, 64], F32)
    bcol = cx.sb("bcol", [128, 8], F32)
    brow = cx.sb("brow", [1, 2048], F32)
    bqrow = cx.sb("bqrow", [1, 2048], F32)
    bs = cx.sb("bs", [1, 8], F32)
    ld, q8, st, vch, mz, tp, te, st2 = cx.sems(
        "ld", "q8", "st", "vch", "mz", "tp", "te", "st2")
    if trans:
        idp = cx.sb("idp_s", [128, 128], BF16)
        ttr = cx.sb("ttr_s", [128, 2, 512], BF16)
        pt0 = cx.ps("pt0_s", [128, 512], BF16)
        pt1 = cx.ps("pt1_s", [128, 512], BF16)
        PT = [pt0, pt1]
    blk = cx.block()

    jobs = []
    for i in range(8):
        jobs.append((GS[i * 128:(i + 1) * 128, :],
                     None if trans else GQ[i * 128:(i + 1) * 128, :], 2048, "mx"))
    for i in range(8):
        jobs.append((US[i * 128:(i + 1) * 128, :],
                     None if trans else UQ[i * 128:(i + 1) * 128, :], 2048, "mx"))
    for i in range(2):
        jobs.append((DS[i * 128:(i + 1) * 128, :],
                     None if trans else DQ[i * 128:(i + 1) * 128, :], 8192, "mx"))
    for i in range(2):
        jobs.append((KS[i * 128:(i + 1) * 128, :],
                     None if trans else KQ[i * 128:(i + 1) * 128, :], 2048, "i8"))
    ntile = [c // 128 for (_, _, c, _) in jobs]
    cgend = []
    _t = 0
    for n in ntile:
        _t += n // 4
        cgend.append(_t)

    def ttarget(j, t):
        if j < 8:
            return WGUc[j // 4, t, :, (j % 4) * 128:(j % 4) * 128 + 128]
        if j < 16:
            jj = j - 8
            return WGUc[jj // 4, t, :, 512 + (jj % 4) * 128:512 + (jj % 4) * 128 + 128]
        if j < 18:
            return WDQc[t, :, (j - 16) * 128:(j - 16) * 128 + 128]
        return KQWc[t, :, (j - 18) * 128:(j - 18) * 128 + 128]

    if True:
        @blk.sync
        def _(sync):
            if trans:
                sync.dma_start(out=idp[:], in_=IDP).then_inc(mz, 16)
            sync.dma_start(out=rp[:, 0:7, :], in_=RP[0:896].rearrange("(a p) b -> p a b", p=128)).then_inc(ld, 16)
            sync.wait_ge(mz, 17 if trans else 1)
            sync.dma_start(out=rp[0:127, 7, :], in_=RP[896:1023]).then_inc(ld, 16)
            sync.dma_start(out=brow[:], in_=B2).then_inc(ld, 16)
            for j, (srcap, dstap, cols, kind) in enumerate(jobs):
                sync.wait_ge(ld, 16 * (3 + j))
                if j >= 2:
                    sync.wait_ge(q8, j)
                sync.dma_start(out=tin[:, j % 2, 0:cols], in_=srcap).then_inc(ld, 16)

        @blk.vector
        def _(v):
            nch = [0]

            def ch(instr):
                nch[0] += 1
                instr.then_inc(vch, 1)
                v.wait_ge(vch, nch[0])

            v.wait_ge(ld, 48)
            v.wait_ge(mz, 17 if trans else 1)
            ch(v.tensor_reduce(bcol[:], rp[:], axis=AX.X, op=ALU.add))
            ch(v.tensor_scalar(bcol[:], bcol[:], 0.125, None, op0=ALU.mult))
            ch(v.tensor_reduce(bs[:, 0:1], brow[:], axis=AX.X, op=ALU.max, apply_absolute_value=True))
            emit_i8_scale(v, ch, bs[:, 1:2], bs[:, 2:3], bs[:, 0:1], bs[:, 3:4], bs[:, 4:5])
            ch(v.tensor_scalar(bqrow[:], brow[:], bs[:, 2:3], MAGIC, op0=ALU.mult, op1=ALU.add))
            ch(v.tensor_scalar(bqrow[:], bqrow[:], -MAGIC, 127.0, op0=ALU.add, op1=ALU.min))
            ch(v.tensor_scalar(bqrow[:], bqrow[:], -127.0, None, op0=ALU.max))
            ch(v.tensor_scalar(bqrow[:], bqrow[:], bs[:, 1:2], None, op0=ALU.mult))
            v.sem_inc(q8, 1)
            stcount = {}
            _n = 3
            for _j, (_s, _d, _c, _k) in enumerate(jobs):
                _n += 1
                stcount[_j] = _n
                if _k == "i8":
                    _n += 1
                    stcount[_j] = _n
            for j, (srcap, dstap, cols, kind) in enumerate(jobs):
                v.wait_ge(ld, 16 * (4 + j))
                if j >= 2:
                    if trans:
                        v.wait_ge(tp, cgend[j - 2])
                    else:
                        v.wait_ge(st, 16 * stcount[j - 2])
                nb = cols // 32
                if kind == "mx":
                    emit_mx_quant_row(v, ch, tout[:, j % 2, 0:cols], tin[:, j % 2, 0:cols],
                                      scr8[:, 0:cols], rs[:, 0:nb], ts2[:, 0:nb], nb)
                else:
                    ch(v.tensor_reduce(rs[:, 0:1], tin[:, j % 2, 0:cols], axis=AX.X, op=ALU.max, apply_absolute_value=True))
                    emit_i8_scale(v, ch, rs[:, 1:2], rs[:, 2:3], rs[:, 0:1], rs[:, 3:4], rs[:, 4:5])
                    ch(v.tensor_scalar(tin[:, j % 2, 0:cols], tin[:, j % 2, 0:cols], rs[:, 2:3], MAGIC, op0=ALU.mult, op1=ALU.add))
                    ch(v.tensor_scalar(tin[:, j % 2, 0:cols], tin[:, j % 2, 0:cols], -MAGIC, 127.0, op0=ALU.add, op1=ALU.min))
                    ch(v.tensor_scalar(tout[:, j % 2, 0:cols], tin[:, j % 2, 0:cols], -127.0, None, op0=ALU.max))
                    ch(v.tensor_copy(rs[:, 6 + (j % 2):7 + (j % 2)], rs[:, 1:2]))
                v.sem_inc(q8, 1)
                if trans and j >= 1:
                    for gj in range(ntile[j - 1] // 4):
                        gg = cgend[j - 2] + gj if j >= 2 else gj
                        v.wait_ge(tp, gg + 1)
                        if gg >= 2:
                            v.wait_ge(st2, 16 * 4 * (gg - 1))
                        ch(v.tensor_copy(ttr[:, gg % 2, :], PT[gg % 2][:]))
                        v.sem_inc(te, 1)
            if trans:
                j = len(jobs) - 1
                for gj in range(ntile[j] // 4):
                    gg = cgend[j - 1] + gj
                    v.wait_ge(tp, gg + 1)
                    if gg >= 2:
                        v.wait_ge(st2, 16 * 4 * (gg - 1))
                    ch(v.tensor_copy(ttr[:, gg % 2, :], PT[gg % 2][:]))
                    v.sem_inc(te, 1)

        @blk.gpsimd
        def _(g):
            g.memset(rp[:, 7, :], 0.0).then_inc(mz, 1)
            g.wait_ge(q8, 1)
            g.dma_start(out=BV[0:896].rearrange("(a p) o -> p (a o)", p=128), in_=bcol[:, 0:7], allow_slow_non_contiguous=True).then_inc(st, 16)
            g.wait_ge(st, 16)
            g.dma_start(out=BV[896:1023], in_=bcol[0:127, 7:8]).then_inc(st, 16)
            g.wait_ge(st, 32)
            g.dma_start(out=BQ, in_=bqrow[:]).then_inc(st, 16)
            g.wait_ge(st, 48)
            nst = 3
            if not trans:
                for j, (srcap, dstap, cols, kind) in enumerate(jobs):
                    g.wait_ge(q8, j + 2)
                    g.dma_start(out=dstap, in_=tout[:, j % 2, 0:cols]).then_inc(st, 16)
                    nst += 1
                    g.wait_ge(st, 16 * nst)
                    if kind == "i8":
                        ki = j - 18
                        g.dma_start(out=KSC[ki:ki + 1, :],
                                    in_=rs[:, 6 + (j % 2):7 + (j % 2)], allow_slow_non_contiguous=True).then_inc(st, 16)
                        nst += 1
                        g.wait_ge(st, 16 * nst)
            else:
                gg = 0
                for j, (srcap, dstap, cols, kind) in enumerate(jobs):
                    for gj in range(ntile[j] // 4):
                        g.wait_ge(te, gg + 1)
                        for s_ in range(4):
                            t = gj * 4 + s_
                            g.dma_start(out=ttarget(j, t),
                                        in_=ttr[:, gg % 2, s_ * 128:(s_ + 1) * 128]).then_inc(st2, 16)
                        gg += 1
                        g.wait_ge(st2, 16 * 4 * gg)
                    if kind == "i8":
                        ki = j - 18
                        g.wait_ge(q8, j + 2)
                        g.dma_start(out=KSC[ki:ki + 1, :],
                                    in_=rs[:, 6 + (j % 2):7 + (j % 2)], allow_slow_non_contiguous=True).then_inc(st, 16)
                        nst += 1
                        g.wait_ge(st, 16 * nst)

        if trans:
            @blk.tensor
            def _(t_):
                t_.wait_ge(mz, 17)
                gg = 0
                for j in range(len(jobs)):
                    t_.wait_ge(q8, j + 2)
                    for gj in range(ntile[j] // 4):
                        if gg >= 2:
                            t_.wait_ge(te, gg - 1)
                        for s_ in range(4):
                            t = gj * 4 + s_
                            mmi = t_.transpose(PT[gg % 2][:, s_ * 128:(s_ + 1) * 128],
                                               tout[:, j % 2, t * 128:(t + 1) * 128], idp[:])
                        mmi.then_inc(tp, 1)
                        gg += 1

    cx.close()
    return nc



QF_F = 41
QW = 4992


def build_qscat(mega=None):
    nc = mega["nc"] if mega else bass.Bass("TRN2", target_bir_lowering=False)
    BV = _io(mega, nc, "bv8", [1023, 1], F32)
    SCL = _io(mega, nc, "scl", [10, 128, 128], BF16)
    SCM = _io(mega, nc, "scm", [10, 128, QF_F], BF16)
    QDo = _io(mega, nc, "qdv", [1, 128 * QF_F], BF16, out=True)

    cx = Ctx(nc, pfx=mega["pfx"] if mega else None)
    bvc = cx.sb("bvc_q", [128, 10], F32)
    scl = cx.sb("scl_q", [128, 10, 128], BF16)
    scm = cx.sb("scm_q", [128, 10, QF_F], BF16)
    rhs = cx.sb("rhs_q", [128, 10, QF_F], BF16)
    qsb = cx.sb("qsb_q", [128, QF_F], BF16)
    onr = cx.sb("onr_q", [1, 128], F32)
    bvx = cx.sb("bvx_q", [1, 1], F32)
    psq = cx.ps("psq_q", [128, QF_F], F32)
    psb = cx.ps("psb_q", [128, 2], F32)
    ld, rh, mmd, bcd, ev, vch, msq = cx.sems("ld", "rh", "mmd", "bcd", "ev", "vch", "msq")
    blk = cx.block()

    if True:
        @blk.gpsimd
        def _(g):
            g.memset(bvc[:, 7:8], 0.0).then_inc(msq, 1)
            g.memset(onr[:], 1.0).then_inc(msq, 1)
            g.dma_start(out=bvc[:, 0:7],
                        in_=BV[0:896].rearrange("(a p) o -> p (a o)", p=128),
                        allow_slow_non_contiguous=True).then_inc(ld, 16)
            g.wait_ge(msq, 1)
            g.dma_start(out=bvc[0:127, 7:8], in_=BV[896:1023],
                        allow_slow_non_contiguous=True).then_inc(ld, 16)
            g.dma_start(out=bvx[:], in_=BV[1022:1023, :]).then_inc(ld, 16)
            g.wait_ge(ld, 48)
            g.dma_start(out=scl[:], in_=SCL.rearrange("a p b -> p a b")).then_inc(ld, 16)
            g.wait_ge(ld, 64)
            g.dma_start(out=scm[:], in_=SCM.rearrange("a p b -> p a b")).then_inc(ld, 16)
            g.wait_ge(ev, 1)
            g.dma_start(out=QDo.rearrange("o (p f) -> (o p) f", p=128), in_=qsb[:]).then_inc(ld, 16)
            g.wait_ge(ld, 96)

        @blk.tensor
        def _(t):
            t.wait_ge(ld, 48)
            t.wait_ge(msq, 2)
            t.matmul(psb[:, 0:1], onr[:], bvc[0:1, 0:1], start=True, stop=True)
            t.matmul(psb[:, 1:2], onr[:], bvx[0:1, 0:1],
                     start=True, stop=True).then_inc(bcd, 1)
            t.wait_ge(rh, 10)
            for mc in range(10):
                mmi = t.matmul(psq[:], scl[:, mc, :], rhs[:, mc, :],
                               start=(mc == 0), stop=(mc == 9))
            mmi.then_inc(mmd, 1)

        @blk.vector
        def _(v):
            nch = [0]

            def ch(instr):
                nch[0] += 1
                instr.then_inc(vch, 1)
                v.wait_ge(vch, nch[0])

            v.wait_ge(ld, 80)
            v.wait_ge(bcd, 1)
            ch(v.tensor_copy(bvc[:, 8:9], psb[:, 0:1]))
            ch(v.tensor_copy(bvc[:, 9:10], psb[:, 1:2]))
            for mc in range(10):
                ch(v.tensor_scalar(rhs[:, mc, :], scm[:, mc, :], bvc[:, mc:mc + 1],
                                   None, op0=ALU.mult))
                v.sem_inc(rh, 1)
            v.wait_ge(mmd, 1)
            ch(v.tensor_copy(qsb[:], psq[:]))
            v.sem_inc(ev, 1)

    cx.close()
    return nc


def qscat_host(core):
    F = QF_F
    A = 4606 - (core % 4) * 1024
    scl = np.zeros((10, 128, 128), np.float32)
    scm = np.zeros((10, 128, F), np.float32)
    for m in range(1023):
        n = A - m
        if 0 <= n < 128 * F:
            p, f = divmod(n, F)
            scl[m // 128][m % 128, p] = 1.0
            scm[m // 128][m % 128, f] = 1.0
    for p in range(128):
        f0s = [f for f in range(F) if p * F + f > A]
        if f0s:
            scl[8][p, p] = 1.0
            for f in f0s:
                scm[8][p, f] = 1.0
        f9s = [f for f in range(F) if p * F + f < A - 1022]
        if f9s:
            scl[9][p, p] = 1.0
            for f in f9s:
                scm[9][p, f] = 1.0
    return scl.astype(bf), scm.astype(bf)


def build_l0_ffn(T=TPC, mega=None):
    nc = mega["nc"] if mega else bass.Bass("TRN2", target_bir_lowering=False)
    XNd = _io(mega, nc, "xn", [DC, 128, T], F32)
    IDF = _io(mega, nc, "idf", [128, 128], F32)
    IDB = _io(mega, nc, "idb", [128, 128], BF16)
    WGU = _io(mega, nc, "wgu", [16, DC, 128, 1024], BF16)
    WDQ = _io(mega, nc, "wdq", [64, 128, 2048], BF16)
    G0d = _io(mega, nc, "g0", [DC, 128, T], F32, out=True)
    NTS = T // 512
    NTT = T // 128
    NU = 64

    cx = Ctx(nc, pfx=mega["pfx"] if mega else None)
    xnf = cx.sb("xnf", [128, DC, T], F32)
    hq_fm = xnf[:].bitcast(BF16).rearrange("p c t -> p (c t)").rearrange("p (ic t) -> p ic t", t=512)
    xq = cx.sb("xq", [128, DC, T], BF16)
    wgu = cx.sb("wgu_s", [128, 2, DC, 1024], BF16)
    wd = cx.sb("wd_s", [128, 1, 64, 128], BF16)
    idf = cx.sb("idf_s", [128, 128], F32)
    idb = cx.sb("idb_s", [128, 128], BF16)
    xtm = cx.sb("xtm", [128, DC, 128], F32)
    xtq = cx.sb("xtq", [128, DC, 128], BF16)
    hst = cx.sb("hst", [128, 2, 512], F32)
    htm = cx.sb("htm", [128, 512], F32)
    hqt = cx.sb("hqt", [128, 2, 512], BF16)
    scr8 = cx.sb("scr8", [128, 2048], FP8)
    rs = cx.sb("rs", [128, 64], F32)
    ts2 = cx.sb("ts2", [128, 64], F32)
    so = cx.sb("so", [128, 512], F32)

    ptf0 = cx.ps("ptf0", [128, 4, 128], F32)
    ptb0 = cx.ps("ptb0", [128, 4, 128], BF16)
    ptb1 = cx.ps("ptb1", [128, 4, 128], BF16)
    pgu = cx.ps("pgu", [128, 2, 2, 512], F32)
    pdn = cx.ps("pdn", [128, 512], F32)
    PTF = [ptf0, ptf0]
    PTB = [ptb0, ptb1]

    (xl, il, wl, wdl, stm, fpt, fev, xqs, bpt, bev, gumm, hsil, huev, hqs,
     t2p, t2e, dnmm, dnev, vch) = cx.sems(
        "xl", "il", "wl", "wdl", "stm", "fpt", "fev", "xqs", "bpt", "bev",
        "gumm", "hsil", "huev", "hqs", "t2p", "t2e", "dnmm", "dnev", "vch")
    blk = cx.block()

    if True:
        @blk.sync
        def _(sync):
            sync.dma_start(out=idf[:], in_=IDF).then_inc(il, 16)
            sync.dma_start(out=idb[:], in_=IDB).then_inc(il, 16)
            sync.dma_start(out=xnf[:], in_=XNd.rearrange("c p t -> p c t")).then_inc(xl, 16)
            for ts in range(NTS):
                for j in range(16):
                    li = ts * 16 + j
                    sync.wait_ge(wl, 16 * li)
                    if li >= 2:
                        sync.wait_ge(gumm, 4 * (li - 1))
                    sync.dma_start(out=wgu[:, li % 2, :, :], in_=WGU[j].rearrange("c p m -> p c m")).then_inc(wl, 16)

        @blk.gpsimd
        def _(g):
            for li in range(NTS * DC):
                ts, ot = divmod(li, DC)
                g.wait_ge(wdl, 16 * li)
                if li >= 1:
                    g.wait_ge(dnmm, li)
                g.dma_start(out=wd[:, 0, :, :],
                            in_=WDQ[:, :, ot * 128:(ot + 1) * 128].rearrange("i p o -> p i o")).then_inc(wdl, 16)
                if li >= 1:
                    g.wait_ge(dnev, li)
                    g.wait_ge(stm, 16 * (li - 1))
                    pts, pot = divmod(li - 1, DC)
                    g.dma_start(out=G0d[pot, :, pts * 512:(pts + 1) * 512], in_=so[:]).then_inc(stm, 16)
            li = NTS * DC
            g.wait_ge(dnev, li)
            g.wait_ge(stm, 16 * (li - 1))
            g.dma_start(out=G0d[DC - 1, :, (NTS - 1) * 512:NTS * 512], in_=so[:]).then_inc(stm, 16)
            g.wait_ge(stm, 16 * li)

        @blk.tensor
        def _(t):
            t.wait_ge(xl, 16)
            t.wait_ge(il, 32)
            for tt in range(NTT):
                for gq in range(4):
                    gi = tt * 4 + gq
                    if gi >= 1:
                        t.wait_ge(fev, gi)
                    for jj in range(4):
                        c = gq * 4 + jj
                        mmi = t.transpose(PTF[gi % 2][:, jj, :], xnf[:, c, tt * 128:(tt + 1) * 128], idf[:])
                    mmi.then_inc(fpt, 1)
                for gq in range(4):
                    gi = tt * 4 + gq
                    t.wait_ge(xqs, tt + 1)
                    if gi >= 2:
                        t.wait_ge(bev, gi - 1)
                    for jj in range(4):
                        c = gq * 4 + jj
                        mmi = t.transpose(PTB[gi % 2][:, jj, :], xtq[:, c, :], idb[:])
                    mmi.then_inc(bpt, 1)
            for ts in range(NTS):
                for u in range(NU):
                    Gg = ts * NU + u
                    j, tsub = divmod(u, 4)
                    li = ts * 16 + j
                    t.wait_ge(wl, 16 * (li + 1))
                    if ts == 0 and u == 0:
                        t.wait_ge(bev, NTT * 4)
                    if Gg >= 2:
                        t.wait_ge(huev, Gg - 1)
                    tsl = slice(ts * 512 + tsub * 128, ts * 512 + tsub * 128 + 128)
                    for gu in range(2):
                        for hc in range(DC):
                            mmi = t.matmul(pgu[:, Gg % 2, gu, :], xq[:, hc, tsl],
                                           wgu[:, li % 2, hc, gu * 512:(gu + 1) * 512],
                                           start=(hc == 0), stop=(hc == DC - 1))
                    mmi.then_inc(gumm, 1)
                    w = Gg - 2
                    if w >= ts * NU:
                        t.wait_ge(hqs, w + 1)
                        if w >= 2:
                            t.wait_ge(t2e, w - 1)
                        for jj in range(4):
                            mmi = t.transpose(PTB[w % 2][:, jj, :], hqt[:, w % 2, jj * 128:(jj + 1) * 128], idb[:])
                        mmi.then_inc(t2p, 1)
                for w in range(ts * NU + NU - 2, ts * NU + NU):
                    t.wait_ge(hqs, w + 1)
                    t.wait_ge(t2e, w - 1)
                    for jj in range(4):
                        mmi = t.transpose(PTB[w % 2][:, jj, :], hqt[:, w % 2, jj * 128:(jj + 1) * 128], idb[:])
                    mmi.then_inc(t2p, 1)
                t.wait_ge(t2e, (ts + 1) * NU)
                for ot in range(DC):
                    li = ts * DC + ot
                    t.wait_ge(wdl, 16 * (li + 1))
                    t.wait_ge(dnev, li)
                    for ic in range(64):
                        mmi = t.matmul(pdn[:], wd[:, 0, ic, :], hq_fm[:, ic, :],
                                       start=(ic == 0), stop=(ic == 63))
                    mmi.then_inc(dnmm, 1)

        @blk.scalar
        def _(sc):
            for ts in range(NTS):
                for u in range(NU):
                    Gg = ts * NU + u
                    sc.wait_ge(gumm, Gg + 1)
                    if Gg >= 2:
                        sc.wait_ge(huev, Gg - 1)
                    sc.activation(out=hst[:, Gg % 2, :], in_=pgu[:, Gg % 2, 0, :], func=AF.Silu).then_inc(hsil, 1)

        @blk.vector
        def _(v):
            nch = [0]

            def ch(instr):
                nch[0] += 1
                instr.then_inc(vch, 1)
                v.wait_ge(vch, nch[0])

            for tt in range(NTT):
                for gq in range(4):
                    gi = tt * 4 + gq
                    v.wait_ge(fpt, gi + 1)
                    ch(v.tensor_copy(xtm[:, gq * 4:(gq + 1) * 4, :], PTF[gi % 2][:]))
                    v.sem_inc(fev, 1)
                emit_mx_quant_row(v, ch, xtq[:].rearrange("p c t -> p (c t)"),
                                  xtm[:].rearrange("p c t -> p (c t)"),
                                  scr8[:], rs[:], ts2[:], 64)
                v.sem_inc(xqs, 1)
                for gq in range(4):
                    gi = tt * 4 + gq
                    v.wait_ge(bpt, gi + 1)
                    ch(v.tensor_copy(xq[:, gq * 4:(gq + 1) * 4, tt * 128:(tt + 1) * 128], PTB[gi % 2][:]))
                    v.sem_inc(bev, 1)
            for ts in range(NTS):
                for u in range(NU):
                    Gg = ts * NU + u
                    j, tsub = divmod(u, 4)
                    v.wait_ge(hsil, Gg + 1)
                    ch(v.tensor_tensor(htm[:], hst[:, Gg % 2, :], pgu[:, Gg % 2, 1, :], op=ALU.mult))
                    v.sem_inc(huev, 1)
                    if Gg >= 2:
                        v.wait_ge(t2p, Gg - 1)
                    emit_mx_quant_row(v, ch, hqt[:, Gg % 2, :], htm[:], scr8[:, 0:512],
                                      rs[:, 0:16], ts2[:, 0:16], 16)
                    v.sem_inc(hqs, 1)
                    w = Gg - 2
                    if w >= ts * NU:
                        v.wait_ge(t2p, w + 1)
                        wts, wu = divmod(w, NU)
                        wj, wtsub = divmod(wu, 4)
                        ch(v.tensor_copy(hq_fm[:, wj * 4:(wj + 1) * 4, wtsub * 128:(wtsub + 1) * 128], PTB[w % 2][:]))
                        v.sem_inc(t2e, 1)
                for w in range(ts * NU + NU - 2, ts * NU + NU):
                    v.wait_ge(t2p, w + 1)
                    wts, wu = divmod(w, NU)
                    wj, wtsub = divmod(wu, 4)
                    ch(v.tensor_copy(hq_fm[:, wj * 4:(wj + 1) * 4, wtsub * 128:(wtsub + 1) * 128], PTB[w % 2][:]))
                    v.sem_inc(t2e, 1)
                for ot in range(DC):
                    li = ts * DC + ot
                    v.wait_ge(dnmm, li + 1)
                    v.wait_ge(stm, 16 * li)
                    ch(v.tensor_copy(so[:], pdn[:]))
                    v.sem_inc(dnev, 1)

    cx.close()
    return nc



def build_l1_proj(T=TPC, mega=None):
    nc = mega["nc"] if mega else bass.Bass("TRN2", target_bir_lowering=False)
    XNd = _io(mega, nc, "xn", [DC, 128, T], F32)
    WQ = _io(mega, nc, "wq", [DC, 128, D], F32)
    WK = _io(mega, nc, "wk", [DC, 128, D], F32)
    WV = _io(mega, nc, "wv", [DC, 128, D], F32)
    QF = _io(mega, nc, "qf", [DC, 128, T], BF16, out=True)
    KF = _io(mega, nc, "kf", [DC, 128, T], BF16, out=True)
    VT = _io(mega, nc, "vt", [T, D], BF16, out=True)
    NTB = T // 512

    cx = Ctx(nc, pfx=mega["pfx"] if mega else None)
    xnb = cx.sb("xnb", [128, DC, T], BF16)
    wt = cx.sb("wt", [128, 2, DC, D], BF16)
    ost = cx.sb("ost", [128, 4, 512], BF16)
    pp = cx.ps("pp", [128, 2, 512], F32)
    wl, xld, mm, ev, stm = cx.sems("wl", "xld", "mm", "ev", "stm")
    blk = cx.block()

    if True:
        @blk.gpsimd
        def _(g):
            g.dma_start(out=xnb[:], in_=XNd.rearrange("c p t -> p c t")).then_inc(xld, 16)
            for wi, W in enumerate([WQ, WK, WV]):
                g.wait_ge(wl, 16 * wi)
                if wi >= 2:
                    g.wait_ge(mm, 32 * (wi - 1))
                g.dma_start(out=wt[:, wi % 2, :, :], in_=W.rearrange("c p m -> p c m")).then_inc(wl, 16)

        @blk.tensor
        def _(t):
            t.wait_ge(xld, 16)
            k = 0
            for wi in range(3):
                t.wait_ge(wl, 16 * (wi + 1))
                if wi < 2:
                    for ot in range(DC):
                        for tb in range(NTB):
                            t.wait_ge(ev, max(0, k - 1))
                            for c in range(DC):
                                mmi = t.matmul(pp[:, k % 2, :], wt[:, wi % 2, c, ot * 128:(ot + 1) * 128],
                                               xnb[:, c, tb * 512:(tb + 1) * 512], start=(c == 0), stop=(c == DC - 1))
                            mmi.then_inc(mm, 1)
                            k += 1
                else:
                    for tt in range(T // 128):
                        for vb in range(4):
                            t.wait_ge(ev, max(0, k - 1))
                            for c in range(DC):
                                mmi = t.matmul(pp[:, k % 2, :], xnb[:, c, tt * 128:(tt + 1) * 128],
                                               wt[:, wi % 2, c, vb * 512:(vb + 1) * 512], start=(c == 0), stop=(c == DC - 1))
                            mmi.then_inc(mm, 1)
                            k += 1

        @blk.vector
        def _(v):
            for k in range(32 * 3):
                v.wait_ge(mm, k + 1)
                v.wait_ge(stm, 16 * max(0, k - 3))
                v.tensor_copy(ost[:, k % 4, :], pp[:, k % 2, :]).then_inc(ev, 1)

        @blk.scalar
        def _(sc):
            for k in range(32 * 3):
                wi, r = divmod(k, 32)
                sc.wait_ge(ev, k + 1)
                sc.wait_ge(stm, 16 * k)
                if wi < 2:
                    ot, tb = divmod(r, NTB)
                    dst = (QF if wi == 0 else KF)[ot, :, tb * 512:(tb + 1) * 512]
                else:
                    tt, vb = divmod(r, 4)
                    dst = VT[tt * 128:(tt + 1) * 128, vb * 512:(vb + 1) * 512]
                sc.dma_start(out=dst, in_=ost[:, k % 4, :]).then_inc(stm, 16)
            sc.wait_ge(stm, 16 * 96)

    cx.close()
    return nc


L1_H = 32


def build_l1_attn(T=TPC, mega=None):
    nc = mega["nc"] if mega else bass.Bass("TRN2", target_bir_lowering=False)
    QFd = _io(mega, nc, "qf", [L1_H, 64, T], BF16)
    KFd = _io(mega, nc, "kf", [DC, 128, S], BF16)
    VTe = _io(mega, nc, "vte", [S, L1_H, 64], BF16)
    QDv = _io(mega, nc, "qdv", [1, 128 * QF_F], BF16)
    IDB = _io(mega, nc, "idb", [128, 128], BF16)
    ON5 = _io(mega, nc, "ones512", [1, 512], BF16)
    WO = _io(mega, nc, "wo", [DC, 128, D], F32)
    S1d = _io(mega, nc, "s1", [DC, 128, T], F32, out=True)
    NTB = T // 512
    NKT = S // 128
    NPAIR = NKT // 2
    H = L1_H

    cx = Ctx(nc, pfx=mega["pfx"] if mega else None)
    qbuf = cx.sb("qbuf_s", [72, 2, T], BF16)
    khe = cx.sb("khe_s", [72, 2, NKT, 128], BF16)
    vth = cx.sb("vth_s", [128, 2, NKT, 65], BF16)
    rwin = cx.sb("rwin_s", [128, QW], BF16)
    idb = cx.sb("idb_s", [128, 128], BF16)
    on5 = cx.sb("on5_s", [1, 512], BF16)
    ones_b = cx.sb("ones_b", [64, 1], BF16)
    ones_f = cx.sb("ones_f", [1, 64], F32)
    sqq = cx.sb("sqq_s", [64, S], BF16)
    nrm = cx.sb("nrm_s", [1, 2, 512], F32)
    q2row = cx.sb("q2row_s", [1, 2, 512], F32)
    csr = cx.sb("csr_s", [1, 2, 512], F32)
    macc = cx.sb("macc_s", [1, 16], F32)
    nmrow = cx.sb("nmrow_s", [1, 2, 512], BF16)
    rrow = cx.sb("rrow_s", [1, 512], F32)
    exps = cx.sb("exps_s", [128, 4, 2, 512], BF16)
    ctx = cx.sb("ctx_s", [128, DC, T], BF16)
    wt = cx.sb("wt_s", [128, 2, DC, 128], BF16)
    so = cx.sb("so_s", [128, 512], F32)

    psA = cx.ps("psA", [128, 2, 2, 512], F32)
    pav = cx.ps("pav", [65, 512], F32)
    pnr = cx.ps("pnr", [1, 2, 512], F32)
    pbc = cx.ps("pbc", [64, 512], F32)
    pso = psA[:, 0, 0, :]

    (bl, il, qld, khl, vhl, wol, sqs, nmm, nev, mrd, smm, sex, avp, avdn,
     nrz, pbd, ctxe, omm, oev, stm, vch, nl1, nl2, mrw) = cx.sems(
        "bl", "il", "qld", "khl", "vhl", "wol", "sqs", "nmm", "nev", "mrd",
        "smm", "sex", "avp", "avdn", "nrz", "pbd", "ctxe", "omm", "oev", "stm", "vch",
        "nl1", "nl2", "mrw")
    blk = cx.block()

    if True:
        @blk.sync
        def _(sync):
            sync.dma_start(out=rwin[:], in_=bass.AP(QDv.tensor, 0, [[1, 128], [1, QW]])).then_inc(bl, 16)
            sync.dma_start(out=idb[:], in_=IDB).then_inc(il, 16)
            sync.dma_start(out=on5[:], in_=ON5).then_inc(il, 16)
            for h in range(H):
                hc, hp = h // 2, (h % 2) * 64
                sync.wait_ge(khl, 16 * h)
                if h >= 2:
                    sync.wait_ge(smm, NTB * NPAIR * (h - 1))
                sync.dma_start(out=khe[0:64, h % 2, :, :],
                               in_=KFd[hc, hp:hp + 64, :].rearrange("p (kt j) -> p kt j", j=128)).then_inc(khl, 16)
                sync.wait_ge(qld, 16 * h)
                sync.dma_start(out=qbuf[0:64, h % 2, :], in_=QFd[h]).then_inc(qld, 16)

        @blk.gpsimd
        def _(g):
            g.memset(ones_b[:], 1.0)
            g.memset(ones_f[:], 1.0).then_inc(il, 1)
            g.memset(vth[:, :, :, 64:65], 1.0)
            g.memset(khe[64:65, :, :, :], 1.0)
            for h in range(H):
                g.wait_ge(vhl, 16 * h)
                if h >= 2:
                    g.wait_ge(avp, NPAIR * NTB * (h - 1))
                g.dma_start(out=vth[:, h % 2, :, 0:64],
                            in_=VTe[:, h, :].rearrange("(kt p) e -> p kt e", p=128)).then_inc(vhl, 16)
                g.wait_ge(mrd, h + 1)
                if h >= 2:
                    g.wait_ge(smm, NTB * NPAIR * (h - 1))
                for qb in range(NTB):
                    g.dma_start(out=qbuf[64:65, h % 2, qb * 512:(qb + 1) * 512],
                                in_=nmrow[0:1, qb, :]).then_inc(mrw, 16)
            for ot in range(DC):
                g.wait_ge(wol, 16 * ot)
                if ot >= 1:
                    g.wait_ge(omm, NTB * ot)
                g.dma_start(out=wt[:, 0, :, :],
                            in_=WO[:, :, ot * 128:(ot + 1) * 128].rearrange("c p m -> p c m")).then_inc(wol, 16)
                if ot >= 1:
                    for tb in range(NTB):
                        k = (ot - 1) * NTB + tb
                        g.wait_ge(oev, k + 1)
                        g.wait_ge(stm, 16 * k)
                        g.dma_start(out=S1d[ot - 1, :, tb * 512:(tb + 1) * 512], in_=so[:]).then_inc(stm, 16)
            for tb in range(NTB):
                k = (DC - 1) * NTB + tb
                g.wait_ge(oev, k + 1)
                g.wait_ge(stm, 16 * k)
                g.dma_start(out=S1d[DC - 1, :, tb * 512:(tb + 1) * 512], in_=so[:]).then_inc(stm, 16)
            g.wait_ge(stm, 16 * DC * NTB)

        @blk.scalar
        def _(sc):
            for h in range(H):
                hc, hp = h // 2, (h % 2) * 64
                sc.wait_ge(qld, 16 * (h + 1))
                sc.wait_ge(nmm, 5 * h)
                sc.activation(out=sqq[0:64, 0:T], in_=qbuf[0:64, h % 2, :], func=AF.Square).then_inc(sqs, 1)
                sc.wait_ge(khl, 16 * (h + 1))
                sc.wait_ge(nmm, 5 * h + 1)
                sc.activation(out=sqq[0:64, 0:S], in_=khe[0:64, h % 2, :, :].rearrange("p a b -> p (a b)"),
                              func=AF.Square).then_inc(sqs, 1)
                sc.wait_ge(nl1, h + 1)
                if h >= 1:
                    sc.wait_ge(mrd, h)
                sc.activation(out=macc[:, 3:4], in_=macc[:, 2:3], func=AF.Ln).then_inc(nl2, 1)
                sc.wait_ge(nl2, 6 * h + 1)
                sc.activation(out=macc[:, 3:4], in_=macc[:, 3:4], func=AF.Exp, scale=0.5).then_inc(nl2, 1)
                for qb in range(NTB):
                    sc.activation(out=csr[:, qb, :], in_=q2row[:, qb, :], func=AF.Ln).then_inc(nl2, 1)
                    sc.wait_ge(nl2, 6 * h + 3 + 2 * qb)
                    sc.activation(out=csr[:, qb, :], in_=csr[:, qb, :], func=AF.Exp, scale=0.5).then_inc(nl2, 1)
                for qb in range(NTB):
                    for kp in range(NPAIR):
                        pi = (h * NTB + qb) * NPAIR + kp
                        sc.wait_ge(smm, pi + 1)
                        if pi >= 4:
                            sc.wait_ge(avp, pi - 3)
                        sc.activation(out=exps[:, pi % 4, :, :].rearrange("p a b -> p (a b)"),
                                      in_=psA[:, pi % 2, :, :].rearrange("p a b -> p (a b)"),
                                      func=AF.Exp, scale=0.125).then_inc(sex, 1)

        @blk.tensor
        def _(t):
            t.wait_ge(bl, 16)
            t.wait_ge(il, 33)
            for h in range(H):
                hc, hp = h // 2, (h % 2) * 64
                t.wait_ge(khl, 16 * (h + 1))
                t.wait_ge(qld, 16 * (h + 1))
                t.wait_ge(vhl, 16 * (h + 1))
                t.wait_ge(sqs, 2 * h + 1)
                t.wait_ge(nev, 5 * h)
                for tb in range(2):
                    mmi = t.matmul(pnr[0:1, tb, :], ones_b[:], sqq[0:64, tb * 512:(tb + 1) * 512],
                                   start=True, stop=True)
                mmi.then_inc(nmm, 1)
                t.wait_ge(sqs, 2 * h + 2)
                for rd in range(4):
                    t.wait_ge(nev, 5 * h + rd + 1)
                    for s_ in range(2):
                        mmi = t.matmul(pnr[0:1, s_, :], ones_b[:],
                                       sqq[0:64, (rd * 2 + s_) * 512:(rd * 2 + s_ + 1) * 512],
                                       start=True, stop=True)
                    mmi.then_inc(nmm, 1)
                t.wait_ge(mrw, 32 * (h + 1))
                for qb in range(NTB):
                    base = (h * NTB + qb) * NPAIR

                    def av_pair(kp2):
                        pi2 = base + kp2
                        t.wait_ge(sex, pi2 + 1)
                        for s2 in range(2):
                            kt2 = kp2 * 2 + s2
                            mmi2 = t.matmul(pav[:], vth[:, h % 2, kt2, :], exps[:, pi2 % 4, s2, :],
                                            start=(kt2 == 0), stop=(kt2 == NKT - 1))
                        mmi2.then_inc(avp, 1)

                    t.wait_ge(ctxe, h * NTB + qb)
                    for kp in range(NPAIR):
                        pi = base + kp
                        if pi >= 2:
                            t.wait_ge(sex, pi - 1)
                        for s_ in range(2):
                            kt = kp * 2 + s_
                            t.matmul(psA[:, pi % 2, s_, :], khe[0:65, h % 2, kt, :],
                                     qbuf[0:65, h % 2, qb * 512:(qb + 1) * 512], start=True, stop=False)
                            cw = 3968 - kt * 128 + qb * 512
                            mmi = t.matmul(psA[:, pi % 2, s_, :], idb[:], rwin[:, cw:cw + 512], start=False, stop=True)
                        mmi.then_inc(smm, 1)
                        if kp >= 2:
                            av_pair(kp - 2)
                    av_pair(NPAIR - 2)
                    av_pair(NPAIR - 1)
                    t.wait_ge(nrz, h * NTB + qb + 1)
                    t.matmul(pbc[:], ones_f[:], rrow[:], start=True, stop=True).then_inc(pbd, 1)
            t.wait_ge(wol, 16)
            t.wait_ge(ctxe, H * NTB)
            for k in range(DC * NTB):
                ot, tb = divmod(k, NTB)
                t.wait_ge(wol, 16 * (ot + 1))
                t.wait_ge(oev, k)
                for c in range(DC):
                    mmi = t.matmul(pso, wt[:, 0, c, :], ctx[:, c, tb * 512:(tb + 1) * 512],
                                   start=(c == 0), stop=(c == DC - 1))
                mmi.then_inc(omm, 1)

        @blk.vector
        def _(v):
            nch = [0]

            def ch(instr):
                nch[0] += 1
                instr.then_inc(vch, 1)
                v.wait_ge(vch, nch[0])

            v.wait_ge(bl, 16)
            ch(v.tensor_reduce(macc[:, 0:1], rwin[0:1, :], axis=AX.X, op=ALU.max))
            for h in range(H):
                v.wait_ge(nmm, 5 * h + 1)
                ch(v.tensor_copy(q2row[:], pnr[:]))
                v.sem_inc(nev, 1)
                ch(v.tensor_scalar(macc[:, 2:3], macc[:, 0:1], 0.0, None, op0=ALU.mult))
                for rd in range(4):
                    v.wait_ge(nmm, 5 * h + 2 + rd)
                    ch(v.tensor_copy(nrm[:], pnr[:]))
                    ch(v.tensor_reduce(macc[:, 4:5], nrm[:].rearrange("a b c -> a (b c)"),
                                       axis=AX.X, op=ALU.max))
                    ch(v.tensor_tensor(macc[:, 2:3], macc[:, 2:3], macc[:, 4:5], op=ALU.max))
                    v.sem_inc(nev, 1)
                v.sem_inc(nl1, 1)
                if h >= 1:
                    v.wait_ge(mrw, 32 * h)
                v.wait_ge(nl2, 6 * h + 2)
                ch(v.tensor_scalar(macc[:, 6:7], macc[:, 3:4], 0.35, None, op0=ALU.mult))
                v.wait_ge(nl2, 6 * h + 6)
                for qb in range(NTB):
                    ch(v.tensor_scalar(csr[:, qb, :], csr[:, qb, :], macc[:, 6:7], None, op0=ALU.mult))
                    ch(v.tensor_scalar(csr[:, qb, :], csr[:, qb, :], macc[:, 0:1], None, op0=ALU.add))
                    ch(v.tensor_scalar(nmrow[:, qb, :], csr[:, qb, :], -1.0, None, op0=ALU.mult))
                v.sem_inc(mrd, 1)
                for qb in range(NTB):
                    v.wait_ge(avp, (h * NTB + qb + 1) * NPAIR)
                    ch(v.tensor_copy(rrow[:], pav[64:65, :]))
                    ch(v.reciprocal(rrow[:], rrow[:]))
                    v.sem_inc(nrz, 1)
                    v.wait_ge(pbd, h * NTB + qb + 1)
                    hp = (h % 2) * 64
                    ch(v.tensor_copy(ctx[hp:hp + 64, h // 2, qb * 512:(qb + 1) * 512], pav[0:64, :]))
                    ch(v.tensor_tensor(ctx[hp:hp + 64, h // 2, qb * 512:(qb + 1) * 512],
                                       ctx[hp:hp + 64, h // 2, qb * 512:(qb + 1) * 512], pbc[:], op=ALU.mult))
                    v.sem_inc(ctxe, 1)
            for k in range(DC * NTB):
                v.wait_ge(omm, k + 1)
                v.wait_ge(stm, 16 * k)
                ch(v.tensor_copy(so[:], pso))
                v.sem_inc(oev, 1)

    cx.close()
    return nc



def build_l1_moe(T=TPC, mega=None):
    nc = mega["nc"] if mega else bass.Bass("TRN2", target_bir_lowering=False)
    XNd = _io(mega, nc, "xn", [DC, 128, T], F32)
    CLS = _io(mega, nc, "cls", [DC, 128, 12], F32)
    EBI = _io(mega, nc, "ebias", [1, 12], F32)
    IDF = _io(mega, nc, "idf", [128, 128], F32)
    WGU = _io(mega, nc, "wgu_e", [8, DC, 128, 2048], F32)
    WDE = _io(mega, nc, "wd_e", [8, 8, 128, 2048], F32)
    G1d = _io(mega, nc, "g1", [DC, 128, T], F32, out=True)
    WSC = nc.dram_tensor(("" if mega is None else mega["pfx"]) + "wscr_m", [16, T], F32, kind="Internal").ap()
    NTB = T // 512
    NTT = T // 128

    cx = Ctx(nc, pfx=mega["pfx"] if mega else None)
    wrow = cx.sb("wrow_m", [1, 2, T], F32)
    xnb = cx.sb("xnb_m", [128, DC, T], BF16)
    wgu = cx.sb("wgu_m", [128, DC, 2048], BF16)
    xnf = wgu[:].bitcast(F32)[:, :, 0:1024]
    wd = cx.sb("wd_m", [128, 8, 2048], BF16)
    cls = cx.sb("cls_m", [128, DC, 12], F32)
    idf = cx.sb("idf_m", [128, 128], F32)
    ones_f = cx.sb("ones_fm", [1, 128], F32)
    ebt = cx.sb("ebt_m", [1, 12], F32)
    lt = cx.sb("lt_m", [12, T], F32)
    rt = cx.sb("rt_m", [128, NTT, 12], F32)
    r1 = cx.sb("r1_m", [128, NTT, 12], F32)
    mx1 = cx.sb("mx1_m", [128, NTT, 1], F32)
    wx = cx.sb("wx_m", [128, NTT, 16], F32)
    wfm = cx.sb("wfm_m", [16, NTT, 128], F32)
    sst = cx.sb("sst_m", [128, 512], F32)
    act = cx.sb("act_m", [128, 8, T], BF16)
    sacc = cx.sb("sacc_m", [128, DC, T], BF16)
    so = cx.sb("so_m", [128, 512], F32)

    pgA = cx.ps("pgA", [128, 512], F32)
    pgB = cx.ps("pgB", [128, 512], F32)
    pdn = cx.ps("pdn_m", [128, 2, 512], F32)
    plog = cx.ps("plog", [12, 512], F32)
    ptm_ = cx.ps("ptm_m", [128, 128], F32)
    pwb = cx.ps("pwb", [128, 2, 512], F32)
    prt = ptm_[:, 0:12]
    pwf = ptm_[0:16, 0:128]

    (xld, cld, ebl, ifl, wld, wdld, lmm, ltd, rte, wtp, wte, sfs, sfe,
     wfme, wbd, wbu, gmm, sil, hue, dmm, dacc, dev_, stm, vch, wsc) = cx.sems(
        "xld", "cld", "ebl", "ifl", "wld", "wdld", "lmm", "ltd", "rte", "wtp",
        "wte", "sfs", "sfe", "wfme", "wbd", "wbu", "gmm", "sil", "hue",
        "dmm", "dacc", "dev", "stm", "vch", "wsc")
    blk = cx.block()

    if True:
        @blk.sync
        def _(sync):
            sync.dma_start(out=idf[:], in_=IDF).then_inc(ifl, 16)
            sync.dma_start(out=ebt[:], in_=EBI).then_inc(ebl, 16)
            sync.dma_start(out=cls[:], in_=CLS.rearrange("c p m -> p c m")).then_inc(cld, 16)
            sync.wait_ge(wfme, NTT)
            sync.dma_start(out=WSC, in_=wfm[:, :, :]).then_inc(wsc, 16)
            for e in range(9):
                ei = 8 if e == 8 else e
                row = 12 if e == 8 else e
                sync.wait_ge(wsc, 16 * (e + 1))
                if e >= 2:
                    sync.wait_ge(wbd, 1 + 2 * (e - 1))
                sync.dma_start(out=wrow[:, e % 2, :], in_=WSC[row:row + 1, :]).then_inc(wsc, 16)

        @blk.gpsimd
        def _(g):
            g.memset(ones_f[:], 1.0).then_inc(ifl, 1)
            g.dma_start(out=xnf, in_=XNd.rearrange("c p t -> p c t")).then_inc(xld, 16)
            g.wait_ge(xld, 16)
            g.dma_start(out=xnb[:], in_=XNd.rearrange("c p t -> p c t")).then_inc(xld, 16)
            for e in range(8):
                g.wait_ge(wld, 16 * e)
                if e == 0:
                    g.wait_ge(lmm, NTB)
                if e >= 1:
                    g.wait_ge(gmm, 16 * e)
                g.dma_start(out=wgu[:], in_=WGU[e].rearrange("c p m -> p c m")).then_inc(wld, 16)
                g.wait_ge(wdld, 16 * e)
                if e >= 1:
                    g.wait_ge(dmm, 32 * e)
                g.dma_start(out=wd[:], in_=WDE[e].rearrange("i p o -> p i o")).then_inc(wdld, 16)
            for k in range(DC * NTB):
                ot, tb = divmod(k, NTB)
                g.wait_ge(dev_, k + 1)
                g.wait_ge(stm, 16 * k)
                g.dma_start(out=G1d[ot, :, tb * 512:(tb + 1) * 512], in_=so[:]).then_inc(stm, 16)
            g.wait_ge(stm, 16 * DC * NTB)

        @blk.tensor
        def _(t):
            t.wait_ge(xld, 16)
            t.wait_ge(cld, 16)
            t.wait_ge(ifl, 17)
            for tb in range(NTB):
                t.wait_ge(ltd, tb)
                for c in range(DC):
                    mmi = t.matmul(plog[:], cls[:, c, :], xnf[:, c, tb * 512:(tb + 1) * 512],
                                   start=(c == 0), stop=(c == DC - 1))
                mmi.then_inc(lmm, 1)
            for tt in range(NTT):
                t.wait_ge(ltd, NTB)
                t.wait_ge(rte, tt)
                t.transpose(prt, lt[:, tt * 128:(tt + 1) * 128], idf[0:12, 0:12]).then_inc(wtp, 1)
            t.wait_ge(ebl, 16)
            t.matmul(pwb[:, 0, 0:12], ones_f[:], ebt[:], start=True, stop=True).then_inc(wbd, 1)
            for tt in range(NTT):
                t.wait_ge(wte, tt + 1)
                t.wait_ge(wfme, tt)
                t.transpose(pwf, wx[:, tt, :], idf[:]).then_inc(wtp, 1)
            for e in range(8):
                t.wait_ge(wsc, 16 * (e + 2))
                t.wait_ge(wbu, 16 * e)
                for tb in range(NTB):
                    t.matmul(pwb[:, tb, :], ones_f[:],
                             wrow[:, e % 2, tb * 512:(tb + 1) * 512],
                             start=True, stop=True).then_inc(wbd, 1)
                t.wait_ge(wld, 16 * (e + 1))
                t.wait_ge(xld, 32)
                for it in range(8):
                    for tb in range(NTB):
                        u = (e * 8 + it) * NTB + tb
                        if u >= 1:
                            t.wait_ge(hue, u)
                        for c in range(DC):
                            t.matmul(pgA[:], wgu[:, c, it * 128:(it + 1) * 128],
                                     xnb[:, c, tb * 512:(tb + 1) * 512], start=(c == 0), stop=(c == DC - 1))
                        for c in range(DC):
                            mmi = t.matmul(pgB[:], wgu[:, c, 1024 + it * 128:1024 + (it + 1) * 128],
                                           xnb[:, c, tb * 512:(tb + 1) * 512], start=(c == 0), stop=(c == DC - 1))
                        mmi.then_inc(gmm, 1)
                t.wait_ge(wdld, 16 * (e + 1))
                t.wait_ge(hue, 16 * (e + 1))
                for k in range(DC * NTB):
                    gk = e * 32 + k
                    ot, tb = divmod(k, NTB)
                    if gk >= 2:
                        t.wait_ge(dacc, gk - 1)
                    for ic in range(8):
                        mmi = t.matmul(pdn[:, gk % 2, :], wd[:, ic, ot * 128:(ot + 1) * 128],
                                       act[:, ic, tb * 512:(tb + 1) * 512], start=(ic == 0), stop=(ic == 7))
                    mmi.then_inc(dmm, 1)
            t.wait_ge(wbu, 16 * 8)
            t.wait_ge(wsc, 16 * 10)
            for tb in range(NTB):
                t.matmul(pwb[:, tb, :], ones_f[:],
                         wrow[:, 0, tb * 512:(tb + 1) * 512],
                         start=True, stop=True).then_inc(wbd, 1)

        @blk.scalar
        def _(sc):
            for tb in range(NTB):
                sc.wait_ge(lmm, tb + 1)
                sc.activation(out=lt[:, tb * 512:(tb + 1) * 512], in_=plog[:], func=AF.Identity).then_inc(ltd, 1)
            sc.wait_ge(sfs, 1)
            sc.activation(out=rt[:].rearrange("p a b -> p (a b)"), in_=rt[:].rearrange("p a b -> p (a b)"),
                          func=AF.Exp).then_inc(sfe, 1)
            for u in range(8 * 8 * NTB):
                sc.wait_ge(gmm, u + 1)
                if u >= 1:
                    sc.wait_ge(hue, u)
                sc.activation(out=sst[:], in_=pgA[:], func=AF.Silu).then_inc(sil, 1)

        @blk.vector
        def _(v):
            nch = [0]

            def ch(instr):
                nch[0] += 1
                instr.then_inc(vch, 1)
                v.wait_ge(vch, nch[0])

            for tt in range(NTT):
                v.wait_ge(wtp, tt + 1)
                ch(v.tensor_copy(rt[:, tt, :], prt))
                v.sem_inc(rte, 1)
            ch(v.tensor_reduce(mx1[:], rt[:], axis=AX.X, op=ALU.max))
            ch(v.tensor_tensor(rt[:], rt[:], mx1[:].broadcast_to([128, NTT, 12]), op=ALU.subtract))
            v.sem_inc(sfs, 1)
            v.wait_ge(sfe, 1)
            ch(v.tensor_reduce(mx1[:], rt[:], axis=AX.X, op=ALU.add))
            ch(v.reciprocal(mx1[:], mx1[:]))
            ch(v.tensor_tensor(rt[:], rt[:], mx1[:].broadcast_to([128, NTT, 12]), op=ALU.mult))
            v.wait_ge(wbd, 1)
            ch(v.tensor_tensor(r1[:], rt[:], pwb[:, 0, 0:12].unsqueeze(1).broadcast_to([128, NTT, 12]), op=ALU.add))
            ch(v.tensor_reduce(mx1[:], r1[:], axis=AX.X, op=ALU.max))
            ch(v.tensor_tensor(wx[:, :, 0:12], r1[:], mx1[:].broadcast_to([128, NTT, 12]), op=ALU.is_ge))
            ch(v.scalar_tensor_tensor(r1[:], wx[:, :, 0:12], -1e30, r1[:], op0=ALU.mult, op1=ALU.add))
            ch(v.tensor_reduce(mx1[:], r1[:], axis=AX.X, op=ALU.max))
            ch(v.tensor_tensor(r1[:], r1[:], mx1[:].broadcast_to([128, NTT, 12]), op=ALU.is_ge))
            ch(v.tensor_tensor(wx[:, :, 0:12], wx[:, :, 0:12], r1[:], op=ALU.add))
            ch(v.tensor_tensor(wx[:, :, 0:12], wx[:, :, 0:12], rt[:], op=ALU.mult))
            ch(v.tensor_scalar(wx[:, :, 0:12], wx[:, :, 0:12], 1.5, None, op0=ALU.mult))
            ch(v.tensor_reduce(wx[:, :, 12:13], wx[:, :, 8:12], axis=AX.X, op=ALU.add))
            ch(v.memset(wx[:, :, 13:16], 0.0))
            for tt in range(NTT):
                v.sem_inc(wte, 1)
            for tt in range(NTT):
                v.wait_ge(wtp, NTT + tt + 1)
                ch(v.tensor_copy(wfm[:, tt, :], pwf))
                v.sem_inc(wfme, 1)
            v.memset(sacc[:], 0.0)
            for e in range(8):
                for it in range(8):
                    for tb in range(NTB):
                        u = (e * 8 + it) * NTB + tb
                        v.wait_ge(sil, u + 1)
                        ch(v.tensor_tensor(sst[:], sst[:], pgB[:], op=ALU.mult))
                        v.wait_ge(wbd, 2 + e * NTB + tb)
                        ch(v.tensor_tensor(act[:, it, tb * 512:(tb + 1) * 512], sst[:], pwb[:, tb, :], op=ALU.mult))
                        v.sem_inc(hue, 1)
                        v.sem_inc(wbu, 1)
                for k in range(DC * NTB):
                    gk = e * 32 + k
                    ot, tb = divmod(k, NTB)
                    v.wait_ge(dmm, gk + 1)
                    ch(v.tensor_tensor(sacc[:, ot, tb * 512:(tb + 1) * 512],
                                       sacc[:, ot, tb * 512:(tb + 1) * 512], pdn[:, gk % 2, :], op=ALU.add))
                    v.sem_inc(dacc, 1)
            for k in range(DC * NTB):
                ot, tb = divmod(k, NTB)
                v.wait_ge(wbd, 17 + tb + 1)
                v.wait_ge(stm, 16 * k)
                ch(v.tensor_tensor(so[:], xnb[:, ot, tb * 512:(tb + 1) * 512], pwb[:, tb, :], op=ALU.mult))
                ch(v.tensor_tensor(so[:], so[:], sacc[:, ot, tb * 512:(tb + 1) * 512], op=ALU.add))
                v.sem_inc(dev_, 1)

    cx.close()
    return nc



def build_mla_proj(dA, dB, T=TPC, mega=None):
    nc = mega["nc"] if mega else bass.Bass("TRN2", target_bir_lowering=False)
    cA, cB = dA // 128, dB // 128
    XNd = _io(mega, nc, "xn", [DC, 128, T], F32)
    Wd = {}
    for nm, shp in [("w1", [DC + 1, 128, dA]), ("w2", [DC + 1, 128, dB]),
                    ("w3", [DC + 1, 128, 128]), ("w4", [cA + 1, 128, D]),
                    ("w5", [cA + 1, 128, D]), ("w6", [cB + 1, 128, D]),
                    ("w7", [cB + 1, 128, 1024])]:
        Wd[nm] = _io(mega, nc, nm, shp, F32)
    QF = _io(mega, nc, "qf", [DC, 128, T], BF16, out=True)
    RQR = _io(mega, nc, "rqr", [64, 16, T], BF16, out=True)
    KF = _io(mega, nc, "kf", [DC, 128, T], BF16, out=True)
    RKR = _io(mega, nc, "rkr", [128, T], BF16, out=True)
    VT = _io(mega, nc, "vt", [T, D], BF16, out=True)
    NTB = T // 512

    cx = Ctx(nc, pfx=mega["pfx"] if mega else None)
    xnb = cx.sb("xnb_p", [128, DC + 1, T], BF16)
    wb = cx.sb("wb_p", [128, 2, 26624], BF16)
    Ab = cx.sb("Ab_p", [128, cA + 1, T], BF16)
    Bb = cx.sb("Bb_p", [128, cB + 1, T], BF16)
    ost = cx.sb("ost_p", [128, 4, 512], BF16)
    pp = cx.ps("pp_p", [128, 2, 512], F32)

    xld, wl, ms_, mm, ev, stm = cx.sems("xld", "wl", "ms", "mm", "ev", "stm")
    blk = cx.block()

    jobs = [("w1", dA, "x", "A"), ("w2", dB, "x", "B"), ("w3", 128, "x", "rk"),
            ("w4", D, "A", "kf"), ("w5", D, "A", "vt"), ("w6", D, "B", "qf"),
            ("w7", 1024, "B", "rq")]

    def units(j):
        return (T // 128) * 4 if jobs[j][3] == "vt" else (jobs[j][1] // 128) * NTB

    def mm_before(j):
        return sum(units(i) for i in range(j))

    def store_dsts(j, u):
        nm, ncol, srck, dst = jobs[j]
        if dst == "vt":
            tt, vb = divmod(u, 4)
            return [(VT[tt * 128:(tt + 1) * 128, vb * 512:(vb + 1) * 512], (0, 128))]
        ot, tb = divmod(u, NTB)
        sl = slice(tb * 512, (tb + 1) * 512)
        if dst == "rk":
            return [(RKR[:, sl], (0, 128))]
        if dst == "kf":
            return [(KF[ot, :, sl], (0, 128))]
        if dst == "qf":
            return [(QF[ot, :, sl], (0, 128))]
        if dst == "rq":
            return [(RQR[:, 2 * ot, sl], (0, 64)), (RQR[:, 2 * ot + 1, sl], (64, 128))]
        return None

    stores = []
    done = {}
    k = 0
    for j in range(7):
        for u in range(units(j)):
            if jobs[j][3] not in ("A", "B"):
                for dstap, prng in store_dsts(j, u):
                    stores.append((k, dstap, prng))
                done[k] = len(stores)
            k += 1
    skeys = sorted(done.keys())
    sidx = {kk: (done[skeys[i - 4]] if i >= 4 else 0) for i, kk in enumerate(skeys)}

    if True:
        @blk.gpsimd
        def _(g):
            g.memset(xnb[:, DC, :], 0.0).then_inc(ms_, 1)
            g.memset(Ab[:, cA, :], 0.0).then_inc(ms_, 1)
            g.memset(Bb[:, cB, :], 0.0).then_inc(ms_, 1)
            g.wait_ge(ms_, 3)
            g.memset(xnb[0:1, DC, :], 1.0).then_inc(ms_, 1)
            g.memset(Ab[0:1, cA, :], 1.0).then_inc(ms_, 1)
            g.memset(Bb[0:1, cB, :], 1.0).then_inc(ms_, 1)
            g.dma_start(out=xnb[:, 0:DC, :], in_=XNd.rearrange("c p t -> p c t")).then_inc(xld, 16)
            for j, (nm, ncol, srck, dst) in enumerate(jobs):
                g.wait_ge(wl, 16 * j)
                if j >= 2:
                    g.wait_ge(mm, mm_before(j - 1))
                nsc1 = {"x": DC, "A": cA, "B": cB}[srck] + 1
                g.dma_start(out=wb[:, j % 2, 0:nsc1 * ncol].rearrange("p (c m) -> p c m", m=ncol),
                            in_=Wd[nm].rearrange("c p m -> p c m")).then_inc(wl, 16)

        @blk.sync
        def _(sync):
            for i, (kk, dstap, prng) in enumerate(stores):
                sync.wait_ge(ev, kk + 1)
                sync.wait_ge(stm, 16 * i)
                sync.dma_start(out=dstap, in_=ost[prng[0]:prng[1], kk % 4, :]).then_inc(stm, 16)
            sync.wait_ge(stm, 16 * len(stores))

        @blk.tensor
        def _(t):
            t.wait_ge(xld, 16)
            t.wait_ge(ms_, 6)
            k = 0
            for j, (nm, ncol, srck, dst) in enumerate(jobs):
                t.wait_ge(wl, 16 * (j + 1))
                t.wait_ge(ev, mm_before(j))
                srcb = {"x": xnb, "A": Ab, "B": Bb}[srck]
                nsc = {"x": DC, "A": cA, "B": cB}[srck] + 1
                if dst == "vt":
                    for tt in range(T // 128):
                        for vb in range(4):
                            t.wait_ge(ev, max(0, k - 1))
                            for c in range(nsc):
                                mmi = t.matmul(pp[:, k % 2, :], srcb[:, c, tt * 128:(tt + 1) * 128],
                                               wb[:, j % 2, c * ncol + vb * 512:c * ncol + (vb + 1) * 512],
                                               start=(c == 0), stop=(c == nsc - 1))
                            mmi.then_inc(mm, 1)
                            k += 1
                else:
                    for ot in range(ncol // 128):
                        for tb in range(NTB):
                            t.wait_ge(ev, max(0, k - 1))
                            for c in range(nsc):
                                mmi = t.matmul(pp[:, k % 2, :],
                                               wb[:, j % 2, c * ncol + ot * 128:c * ncol + (ot + 1) * 128],
                                               srcb[:, c, tb * 512:(tb + 1) * 512],
                                               start=(c == 0), stop=(c == nsc - 1))
                            mmi.then_inc(mm, 1)
                            k += 1

        @blk.scalar
        def _(sc):
            k = 0
            for j, (nm, ncol, srck, dst) in enumerate(jobs):
                for u in range(units(j)):
                    sc.wait_ge(mm, k + 1)
                    if dst in ("A", "B"):
                        ot, tb = divmod(u, NTB)
                        dstap = (Ab if dst == "A" else Bb)[:, ot, tb * 512:(tb + 1) * 512]
                    else:
                        sc.wait_ge(stm, 16 * sidx[k])
                        dstap = ost[:, k % 4, :]
                    sc.activation(out=dstap, in_=pp[:, k % 2, :], func=AF.Identity).then_inc(ev, 1)
                    k += 1

    cx.close()
    return nc



def build_causal_attn(T=TPC, mega=None):
    SC192 = 1.0 / float(np.sqrt(np.float32(192.0)))
    nc = mega["nc"] if mega else bass.Bass("TRN2", target_bir_lowering=False)
    QFd = _io(mega, nc, "qf", [DC, 128, T], BF16)
    RQd = _io(mega, nc, "rqr", [64, 16, T], BF16)
    KFd = _io(mega, nc, "kf", [DC, 128, S], BF16)
    RKd = _io(mega, nc, "rkr_f", [64, S], BF16)
    VTd = _io(mega, nc, "vt", [S, D], BF16)
    MTd = _io(mega, nc, "mt", [32, 2, 128, 512], BF16)
    CSQ = _io(mega, nc, "csq", [64, T], BF16)
    SSQ = _io(mega, nc, "ssq", [64, T], BF16)
    CSK = _io(mega, nc, "csk", [64, S], BF16)
    SSK = _io(mega, nc, "ssk", [64, S], BF16)
    ON5 = _io(mega, nc, "ones512", [1, 512], BF16)
    IDB = _io(mega, nc, "idb", [128, 128], BF16)
    WO = _io(mega, nc, "wo", [DC, 128, D], F32)
    BOC = _io(mega, nc, "boc", [128, DC], F32)
    S2d = _io(mega, nc, "s2", [DC, 128, T], F32, out=True)
    NTB = T // 512
    NKT = S // 128
    NPAIR = NKT // 2
    H = 16

    cx = Ctx(nc, pfx=mega["pfx"] if mega else None)
    kh = cx.sb("kh_c", [128, 2, NKT, 128], BF16)
    vth = cx.sb("vth_c", [128, 2, NKT, 128], BF16)
    qh = cx.sb("qh_c", [128, 2, T], BF16)
    rqh = cx.sb("rqh_c", [72, 2, 3, T], BF16)
    rkf = cx.sb("rkf_c", [72, 2, S], BF16)
    mt = cx.sb("mt_c", [128, NKT, 2, 512], BF16)
    csq = cx.sb("csq_c", [64, T], BF16)
    ssq = cx.sb("ssq_c", [64, T], BF16)
    on5 = cx.sb("on5_c", [1, 512], BF16)
    idb = cx.sb("idb_c", [128, 128], BF16)
    ones_b = cx.sb("ones_bc", [128, 1], BF16)
    ones_f = cx.sb("ones_fc", [1, 128], F32)
    sqq = cx.sb("sqq_c", [128, S], BF16)
    nrm = cx.sb("nrm_c", [1, 2, 512], F32)
    q2row = cx.sb("q2row_c", [1, 2, 512], F32)
    csr = nrm
    macc = cx.sb("macc_c", [1, 16], F32)
    nmrow = cx.sb("nmrow_c", [1, 2, 512], BF16)
    rrow = cx.sb("rrow_c", [1, 512], F32)
    exps = cx.sb("exps_c", [128, 2, 2, 512], BF16)
    ctx = cx.sb("ctx_c", [128, DC, T], BF16)
    wt = cx.sb("wt_c", [128, 1, DC, 128], BF16)
    boc = cx.sb("boc_c", [128, DC], F32)
    _ctxflat = None
    so = cx.sb("so_c", [128, 512], F32)
    rsc = cx.sb("rsc_c", [64, S], BF16)
    rksq = rsc
    _cf = ctx[:].rearrange("p c t -> p (c t)")
    csk = _cf[0:64, 0:S]
    ssk = _cf[0:64, S:2 * S]

    psA = cx.ps("psA_c", [128, 2, 2, 512], F32)
    pav = cx.ps("pav_c", [128, 512], F32)
    pnr = cx.ps("pnr_c", [1, 2, 512], F32)
    psm = pnr[0:1, 0:1, :]
    pbc = cx.ps("pbc_c", [128, 512], F32)
    pso = psA[:, 0, 0, :]

    (cl, il, qld, khl, vhl, wol, rkl, rkp, sqs, nmm, nev, mrd, smm, sex,
     avp, nrz, pbd, ctxe, omm, oev, stm, vch, sch, msc, nl1, nl2, mrw) = cx.sems(
        "cl", "il", "qld", "khl", "vhl", "wol", "rkl", "rkp", "sqs", "nmm",
        "nev", "mrd", "smm", "sex", "avp", "nrz", "pbd", "ctxe", "omm",
        "oev", "stm", "vch", "sch", "msc", "nl1", "nl2", "mrw")
    blk = cx.block()

    if True:
        @blk.sync
        def _(sync):
            sync.dma_start(out=mt[:], in_=MTd.rearrange("k q p j -> p k q j")).then_inc(cl, 16)
            sync.dma_start(out=idb[:], in_=IDB).then_inc(il, 16)
            sync.dma_start(out=on5[:], in_=ON5).then_inc(il, 16)
            sync.dma_start(out=csq[:], in_=CSQ).then_inc(il, 16)
            sync.dma_start(out=ssq[:], in_=SSQ).then_inc(il, 16)
            sync.dma_start(out=csk, in_=CSK).then_inc(il, 16)
            sync.dma_start(out=ssk, in_=SSK).then_inc(il, 16)
            sync.dma_start(out=boc[:], in_=BOC).then_inc(il, 16)
            sync.dma_start(out=rkf[0:64, 0, :], in_=RKd).then_inc(rkl, 16)
            sync.wait_ge(rkl, 16)
            sync.dma_start(out=rsc[0:32, :], in_=RKd[32:64, :]).then_inc(rkl, 16)
            sync.wait_ge(rkl, 32)
            sync.dma_start(out=rsc[32:64, :], in_=RKd[0:32, :]).then_inc(rkl, 16)
            for h in range(H):
                sync.wait_ge(khl, 48 * h)
                if h >= 2:
                    sync.wait_ge(smm, NTB * NPAIR * (h - 1))
                sync.dma_start(out=kh[:, h % 2, :, :],
                               in_=KFd[h, :, :].rearrange("p (kt j) -> p kt j", j=128)).then_inc(khl, 16)
                sync.wait_ge(khl, 48 * h + 16)
                sync.dma_start(out=qh[:, h % 2, :], in_=QFd[h, :, :]).then_inc(khl, 16)
                sync.wait_ge(khl, 48 * h + 32)
                sync.dma_start(out=rqh[0:64, h % 2, 0, :], in_=RQd[:, h, :]).then_inc(khl, 16)

        @blk.gpsimd
        def _(g):
            g.memset(ones_b[:], 1.0)
            g.memset(rkf[64:65, 1:2, :], 1.0)
            g.memset(ones_f[:], 1.0).then_inc(msc, 2)
            for h in range(H):
                g.wait_ge(vhl, 48 * h)
                if h >= 2:
                    g.wait_ge(avp, NPAIR * NTB * (h - 1))
                g.dma_start(out=vth[:, h % 2, :, :],
                            in_=VTd[:, h * 128:(h + 1) * 128].rearrange("(kt p) d -> p kt d", p=128)).then_inc(vhl, 16)
                g.wait_ge(vhl, 48 * h + 16)
                g.wait_ge(khl, 48 * (h + 1))
                g.dma_start(out=rqh[0:32, h % 2, 1, :], in_=RQd[32:64, h, :]).then_inc(vhl, 16)
                g.wait_ge(vhl, 48 * h + 32)
                g.dma_start(out=rqh[32:64, h % 2, 1, :], in_=RQd[0:32, h, :]).then_inc(vhl, 16)
                g.wait_ge(mrd, h + 1)
                if h >= 2:
                    g.wait_ge(smm, NTB * NPAIR * (h - 1))
                for qb in range(NTB):
                    g.dma_start(out=rqh[64:65, h % 2, 2, qb * 512:(qb + 1) * 512],
                                in_=nmrow[0:1, qb, :]).then_inc(mrw, 16)
            for ot in range(DC):
                g.wait_ge(wol, 16 * ot)
                if ot >= 1:
                    g.wait_ge(omm, NTB * ot)
                g.dma_start(out=wt[:, 0, :, :],
                            in_=WO[:, :, ot * 128:(ot + 1) * 128].rearrange("c p m -> p c m")).then_inc(wol, 16)
                if ot >= 1:
                    for tb in range(NTB):
                        k = (ot - 1) * NTB + tb
                        g.wait_ge(oev, k + 1)
                        g.wait_ge(stm, 16 * k)
                        g.dma_start(out=S2d[ot - 1, :, tb * 512:(tb + 1) * 512], in_=so[:]).then_inc(stm, 16)
            for tb in range(NTB):
                k = (DC - 1) * NTB + tb
                g.wait_ge(oev, k + 1)
                g.wait_ge(stm, 16 * k)
                g.dma_start(out=S2d[DC - 1, :, tb * 512:(tb + 1) * 512], in_=so[:]).then_inc(stm, 16)
            g.wait_ge(stm, 16 * DC * NTB)

        @blk.scalar
        def _(sc):
            sc.wait_ge(rkp, 1)
            sc.activation(out=rksq[:], in_=rkf[0:64, 1, :], func=AF.Square).then_inc(sqs, 1)
            for h in range(H):
                sc.wait_ge(khl, 48 * (h + 1))
                sc.wait_ge(nmm, 3 * h)
                sc.activation(out=sqq[:, 0:T], in_=qh[:, h % 2, :], func=AF.Square).then_inc(sqs, 1)
                sc.wait_ge(rkp, h + 2)
                sc.activation(out=sqq[0:64, T:2 * T], in_=rqh[0:64, h % 2, 2, :], func=AF.Square).then_inc(sqs, 1)
                sc.wait_ge(nmm, 3 * h + 1)
                sc.activation(out=sqq[:, 0:S],
                              in_=kh[:, h % 2, :, :].rearrange("p a b -> p (a b)"), func=AF.Square).then_inc(sqs, 1)
                sc.wait_ge(nl1, h + 1)
                if h >= 1:
                    sc.wait_ge(mrd, h)
                sc.activation(out=macc[:, 3:4], in_=macc[:, 2:3], func=AF.Ln).then_inc(nl2, 1)
                sc.wait_ge(nl2, 6 * h + 1)
                sc.activation(out=macc[:, 3:4], in_=macc[:, 3:4], func=AF.Exp, scale=0.5).then_inc(nl2, 1)
                for qb in range(NTB):
                    sc.activation(out=csr[:, qb, :], in_=q2row[:, qb, :], func=AF.Ln).then_inc(nl2, 1)
                    sc.wait_ge(nl2, 6 * h + 3 + 2 * qb)
                    sc.activation(out=csr[:, qb, :], in_=csr[:, qb, :], func=AF.Exp, scale=0.5).then_inc(nl2, 1)
                for qb in range(NTB):
                    for kp_ in range(NPAIR):
                        pi = (h * NTB + qb) * NPAIR + kp_
                        sc.wait_ge(smm, pi + 1)
                        if pi >= 2:
                            sc.wait_ge(avp, pi - 1)
                        sc.activation(out=exps[:, pi % 2, :, :].rearrange("p a b -> p (a b)"),
                                      in_=psA[:, pi % 2, :, :].rearrange("p a b -> p (a b)"),
                                      func=AF.Exp, scale=SC192).then_inc(sex, 1)
            for k in range(DC * NTB):
                ot, tb = divmod(k, NTB)
                sc.wait_ge(omm, k + 1)
                sc.wait_ge(stm, 16 * k)
                sc.activation(out=so[:], in_=pso, func=AF.Identity, bias=boc[:, ot:ot + 1]).then_inc(oev, 1)

        @blk.vector
        def _(v):
            nch = [0]

            def ch(instr):
                nch[0] += 1
                instr.then_inc(vch, 1)
                v.wait_ge(vch, nch[0])

            v.wait_ge(rkl, 48)
            v.wait_ge(il, 112)
            ch(v.tensor_tensor(rkf[0:64, 1, :], rkf[0:64, 0, :], csk, op=ALU.mult))
            ch(v.tensor_tensor(rsc[:], rsc[:], ssk, op=ALU.mult))
            ch(v.tensor_tensor(rkf[0:64, 1, :], rkf[0:64, 1, :], rsc[:], op=ALU.add))
            v.sem_inc(rkp, 1)
            for h in range(H):
                v.wait_ge(vhl, 48 * (h + 1))
                v.wait_ge(khl, 48 * (h + 1))
                ch(v.tensor_tensor(rqh[0:64, h % 2, 2, :], rqh[0:64, h % 2, 0, :], csq[:], op=ALU.mult))
                ch(v.tensor_tensor(rqh[0:64, h % 2, 1, :], rqh[0:64, h % 2, 1, :], ssq[:], op=ALU.mult))
                ch(v.tensor_tensor(rqh[0:64, h % 2, 2, :], rqh[0:64, h % 2, 2, :], rqh[0:64, h % 2, 1, :], op=ALU.add))
                v.sem_inc(rkp, 1)
                v.wait_ge(nmm, 3 * h + 1)
                ch(v.tensor_copy(q2row[:], pnr[:]))
                v.sem_inc(nev, 1)
                for grp in range(2):
                    v.wait_ge(nmm, 3 * h + 2 + grp)
                    ch(v.tensor_copy(nrm[:], pnr[:]))
                    if grp == 0:
                        ch(v.tensor_reduce(macc[:, 2:3], nrm[:].rearrange("a b c -> a (b c)"), axis=AX.X, op=ALU.max))
                    else:
                        ch(v.tensor_reduce(macc[:, 3:4], nrm[:].rearrange("a b c -> a (b c)"), axis=AX.X, op=ALU.max))
                        ch(v.tensor_tensor(macc[:, 2:3], macc[:, 2:3], macc[:, 3:4], op=ALU.max))
                    v.sem_inc(nev, 1)
                v.sem_inc(nl1, 1)
                if h >= 1:
                    v.wait_ge(mrw, 32 * h)
                v.wait_ge(nl2, 6 * h + 2)
                ch(v.tensor_scalar(macc[:, 6:7], macc[:, 3:4], 0.25, None, op0=ALU.mult))
                v.wait_ge(nl2, 6 * h + 6)
                for qb in range(NTB):
                    ch(v.tensor_scalar(csr[:, qb, :], csr[:, qb, :], macc[:, 6:7], None, op0=ALU.mult))
                    ch(v.tensor_scalar(nmrow[:, qb, :], csr[:, qb, :], -1.0, None, op0=ALU.mult))
                v.sem_inc(mrd, 1)
                for qb in range(NTB):
                    v.wait_ge(avp, (h * NTB + qb + 1) * NPAIR)
                    ch(v.tensor_copy(rrow[:], pnr[0:1, 0, :]))
                    ch(v.reciprocal(rrow[:], rrow[:]))
                    v.sem_inc(nrz, 1)
                    v.wait_ge(pbd, h * NTB + qb + 1)
                    ch(v.tensor_copy(ctx[:, h, qb * 512:(qb + 1) * 512], pav[:]))
                    ch(v.tensor_tensor(ctx[:, h, qb * 512:(qb + 1) * 512],
                                       ctx[:, h, qb * 512:(qb + 1) * 512], pbc[:], op=ALU.mult))
                    v.sem_inc(ctxe, 1)
        @blk.tensor
        def _(t):
            t.wait_ge(cl, 16)
            t.wait_ge(il, 112)
            t.wait_ge(msc, 2)
            t.wait_ge(rkl, 48)
            for h in range(H):
                t.wait_ge(khl, 48 * (h + 1))
                t.wait_ge(vhl, 48 * (h + 1))
                t.wait_ge(rkp, h + 2)
                t.wait_ge(sqs, 3 * h + 3)
                t.wait_ge(nev, 3 * h)
                t.wait_ge(ctxe, 2 * h)
                for tb in range(2):
                    t.matmul(pnr[0:1, tb, :], ones_b[:], sqq[:, tb * 512:(tb + 1) * 512],
                             start=True, stop=False)
                    mmi = t.matmul(pnr[0:1, tb, :], ones_b[0:64, :], sqq[0:64, T + tb * 512:T + (tb + 1) * 512],
                                   start=False, stop=True)
                mmi.then_inc(nmm, 1)
                t.wait_ge(sqs, 3 * h + 4)
                for grp in range(2):
                    t.wait_ge(nev, 3 * h + 1 + grp)
                    for s_ in range(2):
                        for cc in range(2):
                            ci = grp * 4 + s_ * 2 + cc
                            t.matmul(pnr[0:1, s_, :], ones_b[:], sqq[:, ci * 512:(ci + 1) * 512],
                                     start=(cc == 0), stop=False)
                            mmi = t.matmul(pnr[0:1, s_, :], ones_b[0:64, :],
                                           rksq[:, ci * 512:(ci + 1) * 512],
                                           start=False, stop=(cc == 1))
                    mmi.then_inc(nmm, 1)
                t.wait_ge(mrw, 32 * (h + 1))
                for qb in range(NTB):
                    base = (h * NTB + qb) * NPAIR

                    def av_pair(kp2):
                        pi2 = base + kp2
                        t.wait_ge(sex, pi2 + 1)
                        for s2 in range(2):
                            kt2 = kp2 * 2 + s2
                            t.matmul(pav[:], vth[:, h % 2, kt2, :], exps[:, pi2 % 2, s2, :],
                                     start=(kt2 == 0), stop=(kt2 == NKT - 1))
                            mmi2 = t.matmul(pnr[0:1, 0, :], ones_b[:], exps[:, pi2 % 2, s2, :],
                                            start=(kt2 == 0), stop=(kt2 == NKT - 1))
                        mmi2.then_inc(avp, 1)

                    t.wait_ge(ctxe, h * NTB + qb)
                    for kp_ in range(NPAIR):
                        pi = base + kp_
                        if pi >= 2:
                            t.wait_ge(sex, pi - 1)
                        for s_ in range(2):
                            kt = kp_ * 2 + s_
                            t.matmul(psA[:, pi % 2, s_, :], kh[:, h % 2, kt, :],
                                     qh[:, h % 2, qb * 512:(qb + 1) * 512], start=True, stop=False)
                            t.matmul(psA[:, pi % 2, s_, :], rkf[0:65, 1, kt * 128:(kt + 1) * 128],
                                     rqh[0:65, h % 2, 2, qb * 512:(qb + 1) * 512], start=False, stop=False)
                            mmi = t.matmul(psA[:, pi % 2, s_, :], idb[:], mt[:, kt, qb, :], start=False, stop=True)
                        mmi.then_inc(smm, 1)
                        if kp_ >= 2:
                            av_pair(kp_ - 2)
                    av_pair(NPAIR - 2)
                    av_pair(NPAIR - 1)
                    t.wait_ge(nrz, h * NTB + qb + 1)
                    t.matmul(pbc[:], ones_f[:], rrow[:], start=True, stop=True).then_inc(pbd, 1)
            t.wait_ge(ctxe, H * NTB)
            for k in range(DC * NTB):
                ot, tb = divmod(k, NTB)
                t.wait_ge(wol, 16 * (ot + 1))
                t.wait_ge(oev, k)
                for c in range(DC):
                    mmi = t.matmul(pso, wt[:, 0, c, :], ctx[:, c, tb * 512:(tb + 1) * 512],
                                   start=(c == 0), stop=(c == DC - 1))
                mmi.then_inc(omm, 1)

    cx.close()
    return nc



def build_l2_ffn(T=TPC, mega=None):
    nc = mega["nc"] if mega else bass.Bass("TRN2", target_bir_lowering=False)
    XNd = _io(mega, nc, "xn", [DC, 128, T], F32)
    KQW = _io(mega, nc, "kqw", [DC, 128, D], BF16)
    BQC = _io(mega, nc, "bqc", [128, DC], F32)
    KSC2 = _io(mega, nc, "kscc", [128, DC], F32)
    IDF = _io(mega, nc, "idf", [128, 128], F32)
    G2d = _io(mega, nc, "g2", [DC, 128, T], F32, out=True)
    WSC = nc.dram_tensor(("" if mega is None else mega["pfx"]) + "wsc2", [4, T], F32, kind="Internal").ap()
    NTB = T // 512
    NTT = T // 128

    cx = Ctx(nc, pfx=mega["pfx"] if mega else None)
    xnf = cx.sb("xnf_2", [128, DC, T], F32)
    xq = cx.sb("xq_2", [128, DC, T], BF16)
    kqw = cx.sb("kqw_2", [128, DC, D], BF16)
    tre = cx.sb("tre_2", [128, 8, T], F32)
    bqc = cx.sb("bqc_2", [128, DC], F32)
    kscc = cx.sb("kscc_2", [128, DC], F32)
    idf = cx.sb("idf_2", [128, 128], F32)
    ones_f = cx.sb("ones_f2", [1, 128], F32)
    mxc = cx.sb("mxc_2", [128, 8], F32)
    scc = cx.sb("scc_2", [128, 2, 8], F32)
    sscr = cx.sb("sscr_2", [128, 2, 8], F32)
    sct = cx.sb("sct_2", [16, 128], F32)
    srow = cx.sb("srow_2", [1, 2, T], F32)
    st = cx.sb("st_2", [128, 2, 512], F32)
    pp = cx.ps("pp_2", [128, 2, 512], F32)
    ptf = cx.ps("ptf_2", [128, 128], F32)
    pbc = cx.ps("pbc_2", [128, 2, 512], F32)
    pbs = cx.ps("pbs_2", [128, 2, 512], F32)

    (xld, kld, bld, ifl, msx, tmax, tpz, tev, sdone, srl, bcd, qdone, mmy,
     yev, ymax, ytpz, ytev, ysdone, ysrl, ybcd, oev, stm, vch, sch, ya, yb) = cx.sems(
        "xld", "kld", "bld", "ifl", "msx", "tmax", "tpz", "tev", "sdone",
        "srl", "bcd", "qdone", "mmy", "yev", "ymax", "ytpz", "ytev", "ysdone",
        "ysrl", "ybcd", "oev", "stm", "vch", "sch", "ya", "yb")
    blk = cx.block()

    def emit_tree(v, ch, srcap, use_abs):
        if use_abs:
            ch(v.tensor_tensor(tre[:, 0:8, :], srcap[:, 0:8, :], srcap[:, 8:16, :], op=ALU.max))
            ch(v.tensor_tensor(tre[:, 0:4, :], tre[:, 0:4, :], tre[:, 4:8, :], op=ALU.max))
            ch(v.tensor_tensor(tre[:, 0:2, :], tre[:, 0:2, :], tre[:, 2:4, :], op=ALU.max))
            ch(v.tensor_tensor(tre[:, 0:1, :], tre[:, 0:1, :], tre[:, 1:2, :], op=ALU.max))
            ch(v.tensor_tensor(tre[:, 1:2, :].rearrange("p o t -> p (o t)").rearrange("p (o t) -> p o t", o=1), srcap[:, 0:8, :].rearrange("p c t -> p c t")[:, 0:1, :], srcap[:, 0:1, :], op=ALU.bypass)) if False else None
            ch(v.tensor_tensor(tre[:, 4:8, :].rearrange("p c t -> p c t")[:, 0:4, :], srcap[:, 0:8, :][:, 0:4, :], srcap[:, 8:16, :][:, 0:4, :], op=ALU.min)) if False else None
        return

    if True:
        @blk.sync
        def _(sync):
            sync.dma_start(out=xnf[:], in_=XNd.rearrange("c p t -> p c t")).then_inc(xld, 16)
            sync.dma_start(out=bqc[:], in_=BQC).then_inc(bld, 16)
            sync.dma_start(out=kscc[:], in_=KSC2).then_inc(bld, 16)
            sync.dma_start(out=idf[:], in_=IDF).then_inc(ifl, 16)
            sync.dma_start(out=kqw[:], in_=KQW.rearrange("c p m -> p c m")).then_inc(kld, 16)
            sync.wait_ge(sdone, 1)
            sync.dma_start(out=WSC[0:2, :].rearrange("a (b t) -> (a b) t", t=128), in_=sct[:]).then_inc(srl, 16)
            sync.wait_ge(srl, 16)
            sync.dma_start(out=srow[:, :, :], in_=WSC[0:2, :].unsqueeze(0)).then_inc(srl, 16)
            sync.wait_ge(ysdone, 1)
            sync.dma_start(out=WSC[2:4, :].rearrange("a (b t) -> (a b) t", t=128), in_=sct[:]).then_inc(ysrl, 16)
            sync.wait_ge(ysrl, 16)
            sync.dma_start(out=srow[:, :, :], in_=WSC[2:4, :].unsqueeze(0)).then_inc(ysrl, 16)

        @blk.gpsimd
        def _(g):
            g.memset(ones_f[:], 1.0).then_inc(ifl, 1)
            for k in range(DC * NTB):
                ot, tb = divmod(k, NTB)
                g.wait_ge(oev, k + 1)
                g.wait_ge(stm, 16 * k)
                g.dma_start(out=G2d[ot, :, tb * 512:(tb + 1) * 512], in_=st[:, k % 2, :]).then_inc(stm, 16)
            g.wait_ge(stm, 16 * DC * NTB)

        @blk.vector
        def _(v):
            nch = [0]

            def ch(instr):
                nch[0] += 1
                instr.then_inc(vch, 1)
                v.wait_ge(vch, nch[0])

            v.wait_ge(xld, 16)
            ch(v.tensor_tensor(tre[:, 0:8, :], xnf[:, 0:8, :], xnf[:, 8:16, :], op=ALU.max))
            ch(v.tensor_tensor(tre[:, 0:4, :], tre[:, 0:4, :], tre[:, 4:8, :], op=ALU.max))
            ch(v.tensor_tensor(tre[:, 0:2, :], tre[:, 0:2, :], tre[:, 2:4, :], op=ALU.max))
            ch(v.tensor_tensor(tre[:, 0:1, :], tre[:, 0:1, :], tre[:, 1:2, :], op=ALU.max))
            ch(v.tensor_tensor(tre[:, 4:8, :], xnf[:, 0:4, :], xnf[:, 4:8, :], op=ALU.min))
            ch(v.tensor_tensor(tre[:, 4:8, :], tre[:, 4:8, :], xnf[:, 8:12, :], op=ALU.min))
            ch(v.tensor_tensor(tre[:, 4:8, :], tre[:, 4:8, :], xnf[:, 12:16, :], op=ALU.min))
            ch(v.tensor_tensor(tre[:, 4:6, :], tre[:, 4:6, :], tre[:, 6:8, :], op=ALU.min))
            ch(v.tensor_tensor(tre[:, 4:5, :], tre[:, 4:5, :], tre[:, 5:6, :], op=ALU.min))
            ch(v.tensor_scalar(tre[:, 4:5, :], tre[:, 4:5, :], -1.0, None, op0=ALU.mult))
            ch(v.tensor_tensor(tre[:, 0:1, :], tre[:, 0:1, :], tre[:, 4:5, :], op=ALU.max))
            v.sem_inc(tmax, 1)
            for tt in range(NTT):
                v.wait_ge(tpz, tt + 1)
                ch(v.tensor_reduce(mxc[:, tt:tt + 1], ptf[:], axis=AX.X, op=ALU.max))
                v.sem_inc(tev, 1)
            emit_i8_scale(v, ch, scc[:, 0, :], scc[:, 1, :], mxc[:], sscr[:, 0, :], sscr[:, 1, :])
            v.sem_inc(tmax, 1)
            v.wait_ge(tpz, NTT + 1)
            ch(v.tensor_copy(sct[:], ptf[0:16, :]))
            v.sem_inc(sdone, 1)
            for tb in range(NTB):
                v.wait_ge(bcd, tb + 1)
                sl = slice(tb * 512, (tb + 1) * 512)
                for c in range(DC):
                    w8 = tre[:, c % 8, 0:512]
                    ch(v.tensor_tensor(w8, xnf[:, c, sl], pbc[:, tb, :], op=ALU.mult))
                    ch(v.tensor_scalar(w8, w8, MAGIC, -MAGIC, op0=ALU.add, op1=ALU.add))
                    ch(v.tensor_scalar(xq[:, c, sl], w8, 127.0, -127.0, op0=ALU.min, op1=ALU.max))
            v.sem_inc(qdone, 1)
            for k in range(DC * NTB):
                ot, tb = divmod(k, NTB)
                v.wait_ge(ya, k + 1)
                sl = slice(tb * 512, (tb + 1) * 512)
                ch(v.tensor_tensor(xnf[:, ot, sl], xnf[:, ot, sl], pbs[:, tb, :], op=ALU.mult))
                ch(v.tensor_scalar(xnf[:, ot, sl], xnf[:, ot, sl], kscc[:, ot:ot + 1], None, op0=ALU.mult))
                v.sem_inc(yb, 1)
            v.wait_ge(yev, DC * NTB)
            ch(v.tensor_tensor(tre[:, 0:8, :], xnf[:, 0:8, :], xnf[:, 8:16, :], op=ALU.max))
            ch(v.tensor_tensor(tre[:, 0:4, :], tre[:, 0:4, :], tre[:, 4:8, :], op=ALU.max))
            ch(v.tensor_tensor(tre[:, 0:2, :], tre[:, 0:2, :], tre[:, 2:4, :], op=ALU.max))
            ch(v.tensor_tensor(tre[:, 0:1, :], tre[:, 0:1, :], tre[:, 1:2, :], op=ALU.max))
            v.sem_inc(ymax, 1)
            for tt in range(NTT):
                v.wait_ge(ytpz, tt + 1)
                ch(v.tensor_reduce(mxc[:, tt:tt + 1], ptf[:], axis=AX.X, op=ALU.max))
                v.sem_inc(ytev, 1)
            emit_i8_scale(v, ch, scc[:, 0, :], scc[:, 1, :], mxc[:], sscr[:, 0, :], sscr[:, 1, :])
            v.sem_inc(ymax, 1)
            v.wait_ge(ytpz, NTT + 1)
            ch(v.tensor_copy(sct[:], ptf[0:16, :]))
            v.sem_inc(ysdone, 1)
            for k in range(DC * NTB):
                ot, tb = divmod(k, NTB)
                v.wait_ge(ybcd, tb + 1)
                v.wait_ge(stm, 16 * max(0, k - 1))
                sl = slice(tb * 512, (tb + 1) * 512)
                ch(v.tensor_tensor(st[:, k % 2, :], xnf[:, ot, sl], pbc[:, tb, :], op=ALU.mult))
                ch(v.tensor_scalar(st[:, k % 2, :], st[:, k % 2, :], MAGIC, -MAGIC, op0=ALU.add, op1=ALU.add))
                ch(v.tensor_scalar(st[:, k % 2, :], st[:, k % 2, :], 127.0, -127.0, op0=ALU.min, op1=ALU.max))
                ch(v.tensor_tensor(st[:, k % 2, :], st[:, k % 2, :], pbs[:, tb, :], op=ALU.mult))
                v.sem_inc(oev, 1)

        @blk.tensor
        def _(t):
            t.wait_ge(xld, 16)
            t.wait_ge(ifl, 17)
            t.wait_ge(tmax, 1)
            for tt in range(NTT):
                if tt >= 1:
                    t.wait_ge(tev, tt)
                t.transpose(ptf[:], tre[:, 0, tt * 128:(tt + 1) * 128], idf[:]).then_inc(tpz, 1)
            t.wait_ge(tmax, 2)
            t.wait_ge(tev, NTT)
            t.transpose(ptf[0:16, 0:128], scc[:].rearrange("p a b -> p (a b)"), idf[:]).then_inc(tpz, 1)
            t.wait_ge(srl, 32)
            for tb in range(NTB):
                t.matmul(pbc[:, tb, :], ones_f[:], srow[:, 1, tb * 512:(tb + 1) * 512], start=True, stop=True)
                t.matmul(pbs[:, tb, :], ones_f[:], srow[:, 0, tb * 512:(tb + 1) * 512], start=True, stop=True).then_inc(bcd, 1)
            t.wait_ge(kld, 16)
            t.wait_ge(qdone, 1)
            for k in range(DC * NTB):
                ot, tb = divmod(k, NTB)
                if k >= 2:
                    t.wait_ge(ya, k - 1)
                for c in range(DC):
                    mmi = t.matmul(pp[:, k % 2, :], kqw[:, c, ot * 128:(ot + 1) * 128],
                                   xq[:, c, tb * 512:(tb + 1) * 512], start=(c == 0), stop=(c == DC - 1))
                mmi.then_inc(mmy, 1)
            t.wait_ge(ymax, 1)
            for tt in range(NTT):
                if tt >= 1:
                    t.wait_ge(ytev, tt)
                t.transpose(ptf[:], tre[:, 0, tt * 128:(tt + 1) * 128], idf[:]).then_inc(ytpz, 1)
            t.wait_ge(ymax, 2)
            t.wait_ge(ytev, NTT)
            t.transpose(ptf[0:16, 0:128], scc[:].rearrange("p a b -> p (a b)"), idf[:]).then_inc(ytpz, 1)
            t.wait_ge(ysrl, 32)
            for tb in range(NTB):
                t.matmul(pbc[:, tb, :], ones_f[:], srow[:, 1, tb * 512:(tb + 1) * 512], start=True, stop=True)
                t.matmul(pbs[:, tb, :], ones_f[:], srow[:, 0, tb * 512:(tb + 1) * 512], start=True, stop=True).then_inc(ybcd, 1)

        @blk.scalar
        def _(sc):
            sc.wait_ge(bld, 32)
            for k in range(DC * NTB):
                ot, tb = divmod(k, NTB)
                sc.wait_ge(mmy, k + 1)
                sc.activation(out=xnf[:, ot, tb * 512:(tb + 1) * 512], in_=pp[:, k % 2, :],
                              func=AF.Identity).then_inc(ya, 1)
                if k >= 1:
                    sc.wait_ge(yb, k)
                    ktb = k - 1
                    pot, ptb = divmod(ktb, NTB)
                    sc.activation(out=xnf[:, pot, ptb * 512:(ptb + 1) * 512],
                                  in_=xnf[:, pot, ptb * 512:(ptb + 1) * 512],
                                  func=AF.Relu, bias=bqc[:, pot:pot + 1]).then_inc(yev, 1)
            sc.wait_ge(yb, DC * NTB)
            ktb = DC * NTB - 1
            pot, ptb = divmod(ktb, NTB)
            sc.activation(out=xnf[:, pot, ptb * 512:(ptb + 1) * 512],
                          in_=xnf[:, pot, ptb * 512:(ptb + 1) * 512],
                          func=AF.Relu, bias=bqc[:, pot:pot + 1]).then_inc(yev, 1)

    cx.close()
    return nc



def build_l3_ffn(T=TPC, mega=None):
    nc = mega["nc"] if mega else bass.Bass("TRN2", target_bir_lowering=False)
    XNd = _io(mega, nc, "xn", [DC, 128, T], F32)
    WR = _io(mega, nc, "wr", [DC + 1, 128, 8], F32)
    WDP = _io(mega, nc, "wdp", [DC + 1, 128, 8, 16], F32)
    WUP = _io(mega, nc, "wup", [16, 8, D], F32)
    BUP = _io(mega, nc, "bup", [8, D], F32)
    IDF = _io(mega, nc, "idf", [128, 128], F32)
    G3d = _io(mega, nc, "g3", [DC, 128, T], F32, out=True)
    WS3 = nc.dram_tensor(("" if mega is None else mega["pfx"]) + "wsc3", [8, T], F32, kind="Internal").ap()
    NTB = T // 512
    NTT = T // 128
    BONUS = float(np.float32(3.0 * (1.0 - 1.0 / 5000.0)))

    cx = Ctx(nc, pfx=mega["pfx"] if mega else None)
    xnf = cx.sb("xnf_3", [128, DC + 1, T], F32)
    xnb = cx.sb("xnb_3", [128, DC + 1, T], BF16)
    wr = cx.sb("wr_3", [128, DC + 1, 8], F32)
    wd = cx.sb("wd_3", [128, DC + 1, 8, 16], BF16)
    wu = cx.sb("wu_3", [16, 8, D], BF16)
    bu = cx.sb("bu_3", [8, D], F32)
    idf = cx.sb("idf_3", [128, 128], F32)
    ones_f = cx.sb("ones_f3", [1, 128], F32)
    onesb16 = cx.sb("onesb16_3", [1, 16], BF16)
    lt = cx.sb("lt_3", [8, T], F32)
    rt = cx.sb("rt_3", [128, NTT, 8], F32)
    r1 = cx.sb("r1_3", [128, NTT, 8], F32)
    mx1 = cx.sb("mx1_3", [128, NTT, 1], F32)
    mx2 = cx.sb("mx2_3", [128, NTT, 1], F32)
    wno = cx.sb("wno_3", [128, NTT, 1], F32)
    wx = cx.sb("wx_3", [128, NTT, 8], F32)
    wfm = cx.sb("wfm_3", [8, NTT, 128], F32)
    wrow = cx.sb("wrow_3", [1, 8, T], BF16)
    hst = cx.sb("hst_3", [16, 512], F32)
    hp = cx.sb("hp_3", [16, 8, T], BF16)
    so = cx.sb("so_3", [128, 2, 512], F32)

    plog = cx.ps("plog_3", [8, 2, 512], F32)
    ptr = cx.ps("ptr_3", [128, 128], F32)
    pex = cx.ps("pex_3", [16, 512], F32)
    pwb = cx.ps("pwb_3", [16, 512], F32)
    pout = cx.ps("pout_3", [128, 512], F32)
    pbu = cx.ps("pbu_3", [128, 512], F32)

    (xld, xbl, wrl, wdl, wul, ifl, lmm, lev, rtp, rte, wtb, wte, wsd, wrl2,
     hmm, hev, hsc, omm, oev, stm, vch, sch, msc, wrl3, ya2) = cx.sems(
        "xld", "xbl", "wrl", "wdl", "wul", "ifl", "lmm", "lev", "rtp", "rte",
        "wtb", "wte", "wsd", "wrl2", "hmm", "hev", "hsc", "omm", "oev", "stm",
        "vch", "sch", "msc", "wrl3", "ya2")
    blk = cx.block()

    if True:
        @blk.sync
        def _(sync):
            sync.dma_start(out=idf[:], in_=IDF).then_inc(ifl, 16)
            sync.dma_start(out=wr[:], in_=WR.rearrange("c p m -> p c m")).then_inc(wrl, 16)
            sync.dma_start(out=xnf[:, 0:DC, :], in_=XNd.rearrange("c p t -> p c t")).then_inc(xld, 16)
            sync.wait_ge(wsd, NTT)
            sync.dma_start(out=WS3, in_=wfm[:, :, :]).then_inc(wrl2, 16)

        @blk.gpsimd
        def _(g):
            g.memset(ones_f[:], 1.0)
            g.memset(onesb16[:], 1.0)
            g.memset(xnf[:, DC, :], 0.0).then_inc(msc, 1)
            g.wait_ge(msc, 1)
            g.memset(xnf[0:1, DC, :], 1.0).then_inc(msc, 1)
            g.wait_ge(xld, 16)
            g.wait_ge(msc, 2)
            g.dma_start(out=xnb[:, 0:DC, :], in_=XNd.rearrange("c p t -> p c t")).then_inc(xbl, 16)
            g.wait_ge(xbl, 16)
            g.memset(xnb[:, DC, :], 0.0).then_inc(msc, 1)
            g.wait_ge(msc, 3)
            g.memset(xnb[0:1, DC, :], 1.0).then_inc(msc, 1)
            g.dma_start(out=wd[:], in_=WDP.rearrange("c p e r -> p c (e r)").rearrange("p c (e r) -> p c e r", r=16)).then_inc(wdl, 16)
            g.wait_ge(wrl2, 16)
            g.dma_start(out=wrow[0:1, :, :], in_=WS3.unsqueeze(0)).then_inc(wrl3, 16)
            g.dma_start(out=wu[:], in_=WUP).then_inc(wul, 16)
            g.dma_start(out=bu[:], in_=BUP).then_inc(wul, 16)
            for k in range(DC * NTB):
                ot, tb = divmod(k, NTB)
                g.wait_ge(oev, k + 1)
                g.wait_ge(stm, 16 * k)
                g.dma_start(out=G3d[ot, :, tb * 512:(tb + 1) * 512], in_=so[:, k % 2, :]).then_inc(stm, 16)
            g.wait_ge(stm, 16 * DC * NTB)

        @blk.tensor
        def _(t):
            t.wait_ge(xld, 16)
            t.wait_ge(wrl, 16)
            t.wait_ge(ifl, 16)
            t.wait_ge(msc, 2)
            for tb in range(NTB):
                for c in range(DC + 1):
                    mmi = t.matmul(plog[:, tb, :], wr[:, c, :], xnf[:, c, tb * 512:(tb + 1) * 512],
                                   start=(c == 0), stop=(c == DC))
            mmi.then_inc(lmm, 1)
            for tt in range(NTT):
                t.wait_ge(lev, 1)
                t.wait_ge(rte, tt)
                t.transpose(ptr[:, 0:8], lt[:, tt * 128:(tt + 1) * 128], idf[0:8, 0:8]).then_inc(rtp, 1)
            for tt in range(NTT):
                t.wait_ge(wte, tt + 1)
                t.wait_ge(wsd, tt)
                t.transpose(ptr[0:8, :], wx[:, tt, :], idf[:]).then_inc(rtp, 1)
            t.wait_ge(xbl, 16)
            t.wait_ge(wdl, 16)
            t.wait_ge(msc, 4)
            for e in range(8):
                for tb in range(NTB):
                    u = e * NTB + tb
                    if u >= 1:
                        t.wait_ge(hev, u)
                    for c in range(DC + 1):
                        mmi = t.matmul(pex[:], xnb[:, c, tb * 512:(tb + 1) * 512] if False else wd[:, c, e, :],
                                       xnb[:, c, tb * 512:(tb + 1) * 512],
                                       start=(c == 0), stop=(c == DC))
                    mmi.then_inc(hmm, 1)
                    t.wait_ge(wrl3, 16)
                    t.wait_ge(hsc, u)
                    t.matmul(pwb[:], onesb16[0:1, 0:16], wrow[0:1, e, tb * 512:(tb + 1) * 512],
                             start=True, stop=True).then_inc(hmm, 1)
            t.wait_ge(wul, 32)
            t.wait_ge(hsc, 16)
            for k in range(DC * NTB):
                ot, tb = divmod(k, NTB)
                t.wait_ge(ya2, k)
                for e in range(8):
                    mmi = t.matmul(pout[:], wu[:, e, ot * 128:(ot + 1) * 128],
                                   hp[:, e, tb * 512:(tb + 1) * 512], start=(e == 0), stop=(e == 7))
                mmi.then_inc(omm, 1)
                t.wait_ge(oev, k)
                t.matmul(pbu[:], bu[:, ot * 128:(ot + 1) * 128],
                         wfm[:].rearrange("p a b -> p (a b)")[:, tb * 512:(tb + 1) * 512],
                         start=True, stop=True).then_inc(omm, 1)

        @blk.scalar
        def _(sc):
            sc.wait_ge(lmm, 1)
            sc.activation(out=lt[:], in_=plog[:].rearrange("p a b -> p (a b)"), func=AF.Identity).then_inc(sch, 1)
            sc.wait_ge(sch, 2)
            sc.activation(out=rt[:].rearrange("p a b -> p (a b)"), in_=rt[:].rearrange("p a b -> p (a b)"),
                          func=AF.Exp).then_inc(sch, 1)
            for e in range(8):
                for tb in range(NTB):
                    u = e * NTB + tb
                    sc.wait_ge(hmm, 2 * u + 1)
                    if u >= 1:
                        sc.wait_ge(hsc, u)
                    sc.activation(out=hst[:, :], in_=pex[:], func=AF.Relu).then_inc(hev, 1)
            for k in range(DC * NTB):
                sc.wait_ge(omm, 2 * k + 1)
                sc.wait_ge(stm, 16 * max(0, k - 1))
                sc.activation(out=so[:, k % 2, :], in_=pout[:], func=AF.Identity).then_inc(ya2, 1)

        @blk.vector
        def _(v):
            nch = [0]

            def ch(instr):
                nch[0] += 1
                instr.then_inc(vch, 1)
                v.wait_ge(vch, nch[0])

            v.wait_ge(sch, 1)
            ch(v.tensor_scalar(lt[:], lt[:], BONUS, None, op0=ALU.add))
            v.sem_inc(lev, 1)
            for tt in range(NTT):
                v.wait_ge(rtp, tt + 1)
                ch(v.tensor_copy(rt[:, tt, :], ptr[:, 0:8]))
                v.sem_inc(rte, 1)
            ch(v.tensor_reduce(mx1[:], rt[:], axis=AX.X, op=ALU.max))
            ch(v.tensor_tensor(rt[:], rt[:], mx1[:].broadcast_to([128, NTT, 8]), op=ALU.subtract))
            v.sem_inc(sch, 1)
            v.wait_ge(sch, 3)
            ch(v.tensor_reduce(mx1[:], rt[:], axis=AX.X, op=ALU.add))
            ch(v.reciprocal(mx1[:], mx1[:]))
            ch(v.tensor_tensor(rt[:], rt[:], mx1[:].broadcast_to([128, NTT, 8]), op=ALU.mult))
            ch(v.tensor_reduce(mx1[:], rt[:], axis=AX.X, op=ALU.max))
            ch(v.tensor_tensor(wx[:], rt[:], mx1[:].broadcast_to([128, NTT, 8]), op=ALU.is_ge))
            ch(v.scalar_tensor_tensor(r1[:], wx[:], -1e30, rt[:], op0=ALU.mult, op1=ALU.add))
            ch(v.tensor_reduce(mx2[:], r1[:], axis=AX.X, op=ALU.max))
            ch(v.tensor_tensor(r1[:], r1[:], mx2[:].broadcast_to([128, NTT, 8]), op=ALU.is_ge))
            ch(v.tensor_tensor(wno[:], mx1[:], mx2[:], op=ALU.add))
            ch(v.reciprocal(wno[:], wno[:]))
            ch(v.tensor_tensor(wx[:], wx[:], mx1[:].broadcast_to([128, NTT, 8]), op=ALU.mult))
            ch(v.tensor_tensor(r1[:], r1[:], mx2[:].broadcast_to([128, NTT, 8]), op=ALU.mult))
            ch(v.tensor_tensor(wx[:], wx[:], r1[:], op=ALU.add))
            ch(v.tensor_tensor(wx[:], wx[:], wno[:].broadcast_to([128, NTT, 8]), op=ALU.mult))
            for tt in range(NTT):
                v.sem_inc(wte, 1)
            for tt in range(NTT):
                v.wait_ge(rtp, NTT + tt + 1)
                ch(v.tensor_copy(wfm[:, tt, :], ptr[0:8, :]))
                v.sem_inc(wsd, 1)
            for e in range(8):
                for tb in range(NTB):
                    u = e * NTB + tb
                    v.wait_ge(hev, u + 1)
                    v.wait_ge(hmm, 2 * u + 2)
                    ch(v.tensor_tensor(hp[:, e, tb * 512:(tb + 1) * 512], hst[:, :], pwb[:], op=ALU.mult))
                    v.sem_inc(hsc, 1)
            for k in range(DC * NTB):
                v.wait_ge(ya2, k + 1)
                v.wait_ge(omm, 2 * k + 2)
                ch(v.tensor_tensor(so[:, k % 2, :], so[:, k % 2, :], pbu[:], op=ALU.add))
                v.sem_inc(oev, 1)

    cx.close()
    return nc



def _EI(nc, name, shape, dtype):
    return nc.dram_tensor(name, shape, dtype, kind="ExternalInput").ap()


def _EO(nc, name, shape, dtype):
    return nc.dram_tensor(name, shape, dtype, kind="ExternalOutput").ap()


def _IN(nc, name, shape, dtype):
    return nc.dram_tensor(name, shape, dtype, kind="Internal").ap()


def build_launchA():
    nc = bass.Bass("TRN2", target_bir_lowering=False)
    BV8 = _IN(nc, "BV8", [1023, 1], F32)
    io_p = {"gsl": _EI(nc, "gsl", [1024, 2048], F32),
            "usl": _EI(nc, "usl", [1024, 2048], F32),
            "dsl": _EI(nc, "dsl", [256, 8192], F32),
            "ksl": _EI(nc, "ksl", [256, 2048], F32),
            "bias2": _EI(nc, "bias2", [1, 2048], F32),
            "relp": _EI(nc, "relp", [1023, 64], F32),
            "idp": _EI(nc, "idp", [128, 128], BF16),
            "wguc": _EO(nc, "wguc", [2, DC, 128, 1024], BF16),
            "wdqc": _EO(nc, "wdqc", [64, 128, 256], BF16),
            "kqwc": _EO(nc, "kqwc", [DC, 128, 256], BF16),
            "bqv": _EO(nc, "bqv", [1, 2048], F32),
            "bv8": BV8, "ksc": _EO(nc, "ksc", [2, 128], F32),
            "nc": nc, "pfx": "pA_"}
    build_prep(trans=True, mega=io_p)
    io_q = {"bv8": BV8,
            "scl": _EI(nc, "scl", [10, 128, 128], BF16),
            "scm": _EI(nc, "scm", [10, 128, QF_F], BF16),
            "qdv": _EO(nc, "qdv", [1, 128 * QF_F], BF16),
            "nc": nc, "pfx": "qA_"}
    build_qscat(mega=io_q)
    io_n = {"x": _EI(nc, "x0", [DC, 128, TPC], F32),
            "xn": _EO(nc, "xn0", [DC, 128, TPC], F32),
            "nc": nc, "pfx": "nA_"}
    build_combine_ln(False, mega=io_n)
    return nc


def build_launchB():
    nc = bass.Bass("TRN2", target_bir_lowering=False)
    io = {"xn": _EI(nc, "xnf", [DC, 128, NTOK], F32),
          "wq": _EI(nc, "wq", [DC, 128, 768], F32),
          "ddt_d": _EI(nc, "ddt_d", [L0_HPC, 2, 128, L0_BS], BF16),
          "qdb_d": _EI(nc, "qdb_d", [128, 2, L0_BS], BF16),
          "kdc_d": _EI(nc, "kdc_d", [128, 2, L0_HPC], F32),
          "ident": _EI(nc, "ident", [128, 64], BF16),
          "bd4": _EI(nc, "bd4", [64, L0_HPC], F32),
          "attn": _EO(nc, "attn", [L0_HPC, 64, NTOK], BF16),
          "nc": nc, "pfx": "aB_"}
    build_l0_attn(mega=io)
    return nc


def build_launchC():
    nc = bass.Bass("TRN2", target_bir_lowering=False)
    T = TPC
    XN0 = _EI(nc, "xn0", [DC, 128, T], F32)
    X0 = _EI(nc, "x0", [DC, 128, T], F32)
    S0 = _IN(nc, "S0", [DC, 128, T], F32)
    G0 = _IN(nc, "G0", [DC, 128, T], F32)
    XN1 = _IN(nc, "XN1", [DC, 128, T], F32)
    VTE1 = _EO(nc, "vte1", [T, L1_H, 64], BF16)
    io = {"attn": _EI(nc, "attn", [DC, 128, T], BF16), "xn": XN0,
          "nw": _EI(nc, "nw", [128, DC], F32),
          "wg": _EI(nc, "wg", [DC, 128, D], F32),
          "wo": _EI(nc, "wo0", [DC, 128, D], F32),
          "s0": S0, "nc": nc, "pfx": "pC_"}
    build_l0_post(mega=io)
    io = {"xn": XN0, "idf": _EI(nc, "idf", [128, 128], F32),
          "idb": _EI(nc, "idb", [128, 128], BF16),
          "wgu": _EI(nc, "wgu", [16, DC, 128, 1024], BF16),
          "wdq": _EI(nc, "wdq", [64, 128, 2048], BF16),
          "g0": G0, "nc": nc, "pfx": "fC_"}
    build_l0_ffn(mega=io)
    io = {"x": X0, "s": S0, "g": G0,
          "xnew": _EO(nc, "x1", [DC, 128, T], F32), "xn": XN1,
          "nc": nc, "pfx": "cC_"}
    build_combine_ln(True, mega=io)
    QF1 = _EO(nc, "qf1", [L1_H, 64, T], BF16)
    io = {"xn": XN1, "wq": _EI(nc, "wq1", [DC, 128, D], F32),
          "wk": _EI(nc, "wk1", [DC, 128, D], F32),
          "wv": _EI(nc, "wv1", [DC, 128, D], F32),
          "qf": QF1.rearrange("(c h2) d t -> c (h2 d) t", h2=2),
          "kf": _EO(nc, "kf1", [DC, 128, T], BF16),
          "vt": VTE1.rearrange("t h e -> t (h e)"),
          "nc": nc, "pfx": "jC_"}
    build_l1_proj(mega=io)
    io = {"xn": XN1, "cls": _EI(nc, "cls", [DC, 128, 12], F32),
          "ebias": _EI(nc, "ebias", [1, 12], F32),
          "idf": _EI(nc, "idf2", [128, 128], F32),
          "wgu_e": _EI(nc, "wgu_e", [8, DC, 128, 2048], F32),
          "wd_e": _EI(nc, "wd_e", [8, 8, 128, 2048], F32),
          "g1": _EO(nc, "g1", [DC, 128, T], F32),
          "nc": nc, "pfx": "mC_"}
    build_l1_moe(mega=io)
    return nc


def build_launchD():
    nc = bass.Bass("TRN2", target_bir_lowering=False)
    T = TPC
    S1 = _IN(nc, "S1", [DC, 128, T], F32)
    XN2 = _IN(nc, "XN2", [DC, 128, T], F32)
    io = {"qf": _EI(nc, "qf1", [L1_H, 64, T], BF16),
          "kf": _EI(nc, "kf1f", [DC, 128, S], BF16),
          "vte": _EI(nc, "vte1f", [S, L1_H, 64], BF16),
          "qdv": _EI(nc, "qdv", [1, 128 * QF_F], BF16),
          "idb": _EI(nc, "flip", [128, 128], BF16),
          "ones512": _EI(nc, "ones512", [1, 512], BF16),
          "wo": _EI(nc, "wo1", [DC, 128, D], F32),
          "s1": S1, "nc": nc, "pfx": "aD_"}
    build_l1_attn(mega=io)
    io = {"x": _EI(nc, "x1", [DC, 128, T], F32), "s": S1,
          "g": _EI(nc, "g1", [DC, 128, T], F32),
          "xnew": _EO(nc, "x2", [DC, 128, T], F32), "xn": XN2,
          "nc": nc, "pfx": "cD_"}
    build_combine_ln(True, mega=io)
    io = {"xn": XN2, "nc": nc, "pfx": "jD_",
          "qf": _EO(nc, "qf2", [DC, 128, T], BF16),
          "rqr": _EO(nc, "rq2", [64, 16, T], BF16),
          "kf": _EO(nc, "kf2", [DC, 128, T], BF16),
          "rkr": _EO(nc, "rk2", [128, T], BF16),
          "vt": _EO(nc, "vt2", [T, D], BF16)}
    for nm, shp in [("w1", [DC + 1, 128, 512]), ("w2", [DC + 1, 128, 1536]),
                    ("w3", [DC + 1, 128, 128]), ("w4", [5, 128, D]),
                    ("w5", [5, 128, D]), ("w6", [13, 128, D]),
                    ("w7", [13, 128, 1024])]:
        io[nm] = _EI(nc, "m2" + nm, shp, F32)
    build_mla_proj(512, 1536, mega=io)
    io = {"xn": XN2, "kqw": _EI(nc, "kqw", [DC, 128, D], BF16),
          "bqc": _EI(nc, "bqc", [128, DC], F32),
          "kscc": _EI(nc, "kscc", [128, DC], F32),
          "idf": _EI(nc, "idf", [128, 128], F32),
          "g2": _EO(nc, "g2", [DC, 128, T], F32),
          "nc": nc, "pfx": "fD_"}
    build_l2_ffn(mega=io)
    return nc


def _launch_ca_tail(nc, tag, final):
    T = TPC
    Sx = _IN(nc, "Sx", [DC, 128, T], F32)
    XNn = _IN(nc, "XNn", [DC, 128, T], F32)
    io = {"qf": _EI(nc, "qf", [DC, 128, T], BF16),
          "rqr": _EI(nc, "rqr", [64, 16, T], BF16),
          "kf": _EI(nc, "kff", [DC, 128, S], BF16),
          "rkr_f": _EI(nc, "rkf", [64, S], BF16),
          "vt": _EI(nc, "vtf", [S, D], BF16),
          "mt": _EI(nc, "mt", [32, 2, 128, 512], BF16),
          "csq": _EI(nc, "csq", [64, T], BF16),
          "ssq": _EI(nc, "ssq", [64, T], BF16),
          "csk": _EI(nc, "csk", [64, S], BF16),
          "ssk": _EI(nc, "ssk", [64, S], BF16),
          "ones512": _EI(nc, "ones512", [1, 512], BF16),
          "idb": _EI(nc, "idb", [128, 128], BF16),
          "wo": _EI(nc, "woc", [DC, 128, D], F32),
          "boc": _EI(nc, "boc", [128, DC], F32),
          "s2": Sx, "nc": nc, "pfx": "a%s_" % tag}
    build_causal_attn(mega=io)
    io = {"x": _EI(nc, "xin", [DC, 128, T], F32), "s": Sx,
          "g": _EI(nc, "gin", [DC, 128, T], F32),
          "xnew": _EO(nc, "xout", [DC, 128, T], F32), "xn": XNn,
          "nc": nc, "pfx": "c%s_" % tag}
    build_combine_ln(True, mega=io)
    return XNn


def build_launchE():
    nc = bass.Bass("TRN2", target_bir_lowering=False)
    XN3 = _launch_ca_tail(nc, "E", False)
    io = {"xn": XN3, "nc": nc, "pfx": "jE_",
          "qf": _EO(nc, "qf3", [DC, 128, TPC], BF16),
          "rqr": _EO(nc, "rq3", [64, 16, TPC], BF16),
          "kf": _EO(nc, "kf3", [DC, 128, TPC], BF16),
          "rkr": _EO(nc, "rk3", [128, TPC], BF16),
          "vt": _EO(nc, "vt3", [TPC, D], BF16)}
    for nm, shp in [("w1", [DC + 1, 128, 512]), ("w2", [DC + 1, 128, 512]),
                    ("w3", [DC + 1, 128, 128]), ("w4", [5, 128, D]),
                    ("w5", [5, 128, D]), ("w6", [5, 128, D]),
                    ("w7", [5, 128, 1024])]:
        io[nm] = _EI(nc, "m3" + nm, shp, F32)
    build_mla_proj(512, 512, mega=io)
    io = {"xn": XN3, "wr": _EI(nc, "wr", [DC + 1, 128, 8], F32),
          "wdp": _EI(nc, "wdp", [DC + 1, 128, 8, 16], F32),
          "wup": _EI(nc, "wup", [16, 8, D], F32),
          "bup": _EI(nc, "bup", [8, D], F32),
          "idf": _EI(nc, "idf", [128, 128], F32),
          "g3": _EO(nc, "g3", [DC, 128, TPC], F32),
          "nc": nc, "pfx": "fE_"}
    build_l3_ffn(mega=io)
    return nc


def build_launchF():
    nc = bass.Bass("TRN2", target_bir_lowering=False)
    _launch_ca_tail(nc, "F", True)
    return nc


def _ident():
    return np.eye(128, dtype=np.float32)


def _padw(w, b, ncolpad=None):
    K, M = w.shape
    Mp = ncolpad or M
    out = np.zeros((K // 128 + 1, 128, Mp), np.float32)
    for c in range(K // 128):
        out[c, :, 0:M] = w[c * 128:(c + 1) * 128, :]
    out[K // 128, 0, 0:M] = b
    return out


def _l2_perm():
    p = np.zeros(64, np.int64)
    for j in range(32):
        p[j] = 2 * j
        p[32 + j] = 2 * j + 1
    return p


def _rope_tabs_l2(pos):
    import math
    inv = np.exp(np.arange(0, 128, 2, dtype=np.float32) * (-math.log(10000.0) / 128.0))
    ang = pos[:, None].astype(np.float32) * inv[None, :]
    c = np.cos(ang)[:, 0:32].astype(np.float32)
    s = np.sin(ang)[:, 0:32].astype(np.float32)
    return c, s


def _rope_tabs_l3(pos):
    freqs = (1.0 / 10000.0 ** (np.arange(0, 64, 2, dtype=np.float32) / 64.0)).astype(np.float32)
    ang = np.outer(pos.astype(np.float32), freqs)
    return np.cos(ang).astype(np.float32), np.sin(ang).astype(np.float32)


def _cs_tiles(c, s):
    csq = np.concatenate([c, c], 1).T.copy()
    ssq = np.concatenate([-s, s], 1).T.copy()
    return csq, ssq


def _mask_tiles(q0):
    mt = np.zeros((32, 2, 128, 512), np.float32)
    ki = np.arange(128)[:, None]
    for kt in range(32):
        for qb in range(2):
            qj = q0 + qb * 512 + np.arange(512)[None, :]
            mt[kt, qb] = np.where(qj >= kt * 128 + ki, 0.0, -1e30)
    return mt


def _run(nc, maps):
    res = run_bass_kernel_spmd(nc, maps, core_ids=list(range(NCORES)))
    return res.results


_NCCACHE = {}


def _bld(key, fn, *a):
    if key not in _NCCACHE:
        _NCCACHE[key] = fn(*a)
    return _NCCACHE[key]


def kernel18(**inputs):
    I = {k: np.asarray(v, dtype=np.float32) for k, v in inputs.items()}
    ident = _ident()
    identb = ident.astype(bf)
    ones512 = np.ones((1, 512), np.float32).astype(bf)

    x_fm = to_fm(I["L0_attn_hidden_states"])

    nc = _bld("prep", build_prep)
    maps = []
    kT = np.ascontiguousarray(I["L2_ffn_kernel"].T)
    for c in range(NCORES):
        maps.append({
            "gsl": I["L0_ffn_gate_w"][c * 1024:(c + 1) * 1024],
            "usl": I["L0_ffn_up_w"][c * 1024:(c + 1) * 1024],
            "dsl": I["L0_ffn_down_w"][c * 256:(c + 1) * 256],
            "ksl": kT[c * 256:(c + 1) * 256],
            "bias2": I["L2_ffn_bias"][None, :],
            "relp": I["L1_attn_rel_pos"],
        })
    r = _run(nc, maps)
    gq = np.concatenate([r[c]["gq"] for c in range(8)], 0)
    uq = np.concatenate([r[c]["uq"] for c in range(8)], 0)
    dq = np.concatenate([r[c]["dq"] for c in range(8)], 0)
    kqi = np.concatenate([r[c]["kq"] for c in range(8)], 0)
    ksc = np.concatenate([r[c]["ksc"].reshape(-1) for c in range(8)], 0)
    bqv = r[0]["bqv"][0]
    bv8 = r[0]["bv8"][:, 0]

    nc = _bld("ln0", build_combine_ln, False)
    r = _run(nc, [{"x": s} for s in shard_tok(x_fm)])
    xn0 = np.concatenate([r[c]["xn"] for c in range(8)], 2)

    nc = _bld("l0a", build_l0_attn)
    maps = l0_attn_host_inputs(xn0)
    for c in range(NCORES):
        heads = list(range(c * L0_HPC, (c + 1) * L0_HPC))
        rows = []
        for part in range(3):
            for h in heads:
                rows.extend(range(h * 192 + part * 64, h * 192 + part * 64 + 64))
        maps[c]["wq"] = to_fm(I["L0_attn_qkv_w"][rows, :])
    r = _run(nc, maps)
    attn_fm = np.zeros((2048, NTOK), np.float32)
    for c in range(8):
        at = r[c]["attn"].astype(np.float32)
        for i in range(4):
            attn_fm[(c * 4 + i) * 64:(c * 4 + i) * 64 + 64, :] = at[i]
    attn_fm = attn_fm.reshape(DC, 128, NTOK)

    nc = _bld("l0p", build_l0_post)
    wg = to_fm(I["L0_attn_gate_w"])
    wo0 = to_fm(I["L0_attn_out_w"])
    nw = I["L0_attn_norm_w"].reshape(DC, 128).T.copy()
    xn0sh = shard_tok(xn0)
    maps = [{"attn": np.ascontiguousarray(attn_fm[:, :, c * TPC:(c + 1) * TPC]).astype(bf),
             "xn": xn0sh[c], "nw_s" if False else "nw": nw, "wg": wg, "wo": wo0}
            for c in range(8)]
    r = _run(nc, maps)
    s0 = [r[c]["s0"] for c in range(8)]

    nc = _bld("l0f", build_l0_ffn)
    wgu = np.zeros((16, DC, 128, 1024), bf)
    for j in range(16):
        for hc in range(DC):
            wgu[j, hc, :, 0:512] = gq[j * 512:(j + 1) * 512, hc * 128:(hc + 1) * 128].T
            wgu[j, hc, :, 512:1024] = uq[j * 512:(j + 1) * 512, hc * 128:(hc + 1) * 128].T
    wdq = np.zeros((64, 128, 2048), bf)
    for ic in range(64):
        wdq[ic] = dq[:, ic * 128:(ic + 1) * 128].T
    maps = [{"xn": xn0sh[c], "idf": ident, "idb": identb, "wgu": wgu, "wdq": wdq}
            for c in range(8)]
    r = _run(nc, maps)
    g0 = [r[c]["g0"] for c in range(8)]

    nc = _bld("cl", build_combine_ln, True)
    xsh = shard_tok(x_fm)
    r = _run(nc, [{"x": xsh[c], "s": s0[c], "g": g0[c]} for c in range(8)])
    x1 = [r[c]["xnew"] for c in range(8)]
    xn1 = [r[c]["xn"] for c in range(8)]
    xn1f = np.concatenate(xn1, 2)

    nc = _bld("l1p", build_l1_proj)
    wq1, wk1, wv1 = to_fm(I["L1_attn_wq"]), to_fm(I["L1_attn_wk"]), to_fm(I["L1_attn_wv"])
    r = _run(nc, [{"xn": xn1[c], "wq": wq1, "wk": wk1, "wv": wv1} for c in range(8)])
    qf1 = [r[c]["qf"] for c in range(8)]
    kf1 = np.concatenate([r[c]["kf"] for c in range(8)], 2)
    vt1 = np.concatenate([r[c]["vt"] for c in range(8)], 0)

    nc = _bld("l1a", build_l1_attn)
    wo1 = to_fm(I["L1_attn_wo"])
    clipidx = np.clip(np.arange(-4095, 4096), -(1023 - 512), 511)
    maps = []
    flip = np.zeros((128, 128), np.float32)
    flip[np.arange(128), 127 - np.arange(128)] = 1.0
    nidx = np.arange(128 * QF_F)
    for c in range(8):
        sq_, q0 = c // 4, (c % 4) * TPC
        kfs = np.ascontiguousarray(kf1[:, :, sq_ * S:(sq_ + 1) * S])
        vte = vt1[sq_ * S:(sq_ + 1) * S].reshape(S, L1_H, 64)
        qdv = bv8[np.clip(4606 - q0 - nidx, 0, 1022)][None, :]
        maps.append({"qf": qf1[c], "kf": kfs, "vte": vte.astype(bf), "qdv": qdv.astype(bf),
                     "idb": flip.astype(bf), "ones512": ones512, "wo": wo1})
    r = _run(nc, maps)
    s1 = [r[c]["s1"] for c in range(8)]

    nc = _bld("l1m", build_l1_moe)
    cls = to_fm(I["L1_ffn_classifier_w"])
    wgu_e = np.zeros((8, DC, 128, 2048), np.float32)
    wd_e = np.zeros((8, 8, 128, 2048), np.float32)
    for e in range(8):
        for hc in range(DC):
            wgu_e[e, hc, :, 0:1024] = I["L1_ffn_gate_w"][e][:, hc * 128:(hc + 1) * 128].T
            wgu_e[e, hc, :, 1024:2048] = I["L1_ffn_up_w"][e][:, hc * 128:(hc + 1) * 128].T
        for ic in range(8):
            wd_e[e, ic] = I["L1_ffn_down_w"][e][:, ic * 128:(ic + 1) * 128].T
    r = _run(nc, [{"xn": xn1[c], "cls": cls, "ebias": I["L1_ffn_e_score_correction_bias"][None, :],
                   "idf": ident, "wgu_e": wgu_e, "wd_e": wd_e} for c in range(8)])
    g1 = [r[c]["g1"] for c in range(8)]

    nc = _bld("cl", build_combine_ln, True)
    r = _run(nc, [{"x": x1[c], "s": s1[c], "g": g1[c]} for c in range(8)])
    x2 = [r[c]["xnew"] for c in range(8)]
    xn2 = [r[c]["xn"] for c in range(8)]

    nc = _bld("mla2", build_mla_proj, 512, 1536)
    perm = _l2_perm()
    w_rk2 = I["L2_attn_w_rk"][:, perm]
    b_rk2 = I["L2_attn_b_rk"][perm]
    w_rq2 = I["L2_attn_w_rq"].reshape(1536, 16, 64)[:, :, perm].reshape(1536, 1024)
    b_rq2 = I["L2_attn_b_rq"].reshape(16, 64)[:, perm].reshape(1024)
    m2 = {"w1": _padw(I["L2_attn_w_dkv"], I["L2_attn_b_dkv"]),
          "w2": _padw(I["L2_attn_w_dq"], I["L2_attn_b_dq"]),
          "w3": _padw(w_rk2, b_rk2, 128),
          "w4": _padw(I["L2_attn_w_uk"], I["L2_attn_b_uk"]),
          "w5": _padw(I["L2_attn_w_uv"], I["L2_attn_b_uv"]),
          "w6": _padw(I["L2_attn_w_uq"], I["L2_attn_b_uq"]),
          "w7": _padw(w_rq2, b_rq2)}
    r = _run(nc, [dict(m2, xn=xn2[c]) for c in range(8)])
    qf2 = [r[c]["qf"] for c in range(8)]
    rq2 = [r[c]["rqr"] for c in range(8)]
    kf2 = np.concatenate([r[c]["kf"] for c in range(8)], 2)
    rk2 = np.concatenate([r[c]["rkr"][0:64] for c in range(8)], 1)
    vt2 = np.concatenate([r[c]["vt"] for c in range(8)], 0)

    nc = _bld("ca", build_causal_attn)
    wo2 = to_fm(I["L2_attn_w_o"].T)
    bo2 = I["L2_attn_b_o"].reshape(DC, 128).T.copy()
    maps = []
    for c in range(8):
        sq_, q0 = c // 4, (c % 4) * TPC
        posq = np.arange(q0, q0 + TPC)
        posk = np.arange(S)
        cq, sq2 = _rope_tabs_l2(posq)
        ck, sk2 = _rope_tabs_l2(posk)
        csq, ssq = _cs_tiles(cq, sq2)
        csk, ssk = _cs_tiles(ck, sk2)
        maps.append({"qf": qf2[c], "rqr": rq2[c],
                     "kf": np.ascontiguousarray(kf2[:, :, sq_ * S:(sq_ + 1) * S]),
                     "rkr_f": np.ascontiguousarray(rk2[:, sq_ * S:(sq_ + 1) * S]).astype(bf),
                     "vt": vt2[sq_ * S:(sq_ + 1) * S].astype(bf),
                     "mt": _mask_tiles(q0).astype(bf),
                     "csq": csq.astype(bf), "ssq": ssq.astype(bf),
                     "csk": csk.astype(bf), "ssk": ssk.astype(bf),
                     "ones512": ones512, "idb": identb, "wo": wo2, "boc": bo2})
    r = _run(nc, maps)
    s2 = [r[c]["s2"] for c in range(8)]

    nc = _bld("l2f", build_l2_ffn)
    kqw = to_fm(kqi.astype(np.float32)).astype(bf)
    bqc = bqv.reshape(DC, 128).T.copy()
    kscc = ksc.reshape(DC, 128).T.copy()
    r = _run(nc, [{"xn": xn2[c], "kqw": kqw, "bqc": bqc, "kscc": kscc, "idf": ident}
                  for c in range(8)])
    g2 = [r[c]["g2"] for c in range(8)]

    nc = _bld("cl", build_combine_ln, True)
    r = _run(nc, [{"x": x2[c], "s": s2[c], "g": g2[c]} for c in range(8)])
    x3 = [r[c]["xnew"] for c in range(8)]
    xn3 = [r[c]["xn"] for c in range(8)]

    nc = _bld("mla3", build_mla_proj, 512, 512)
    z512 = np.zeros(512, np.float32)
    zD = np.zeros(D, np.float32)
    m3 = {"w1": _padw(I["L3_attn_Wkvd"], z512),
          "w2": _padw(I["L3_attn_Wqd"], z512),
          "w3": _padw(I["L3_attn_Wkr"], np.zeros(64, np.float32), 128),
          "w4": _padw(I["L3_attn_Wku"], zD),
          "w5": _padw(I["L3_attn_Wvu"], zD),
          "w6": _padw(I["L3_attn_Wqu"], zD),
          "w7": _padw(I["L3_attn_Wqr"], np.zeros(1024, np.float32))}
    r = _run(nc, [dict(m3, xn=xn3[c]) for c in range(8)])
    qf3 = [r[c]["qf"] for c in range(8)]
    rq3 = [r[c]["rqr"] for c in range(8)]
    kf3 = np.concatenate([r[c]["kf"] for c in range(8)], 2)
    rk3 = np.concatenate([r[c]["rkr"][0:64] for c in range(8)], 1)
    vt3 = np.concatenate([r[c]["vt"] for c in range(8)], 0)

    nc = _bld("ca", build_causal_attn)
    wo3 = to_fm(I["L3_attn_Wo"].T)
    bo3 = np.zeros((128, DC), np.float32)
    maps = []
    for c in range(8):
        sq_, q0 = c // 4, (c % 4) * TPC
        cq, sq2 = _rope_tabs_l3(np.arange(q0, q0 + TPC))
        ck, sk2 = _rope_tabs_l3(np.arange(S))
        csq, ssq = _cs_tiles(cq, sq2)
        csk, ssk = _cs_tiles(ck, sk2)
        maps.append({"qf": qf3[c], "rqr": rq3[c],
                     "kf": np.ascontiguousarray(kf3[:, :, sq_ * S:(sq_ + 1) * S]),
                     "rkr_f": np.ascontiguousarray(rk3[:, sq_ * S:(sq_ + 1) * S]).astype(bf),
                     "vt": vt3[sq_ * S:(sq_ + 1) * S].astype(bf),
                     "mt": _mask_tiles(q0).astype(bf),
                     "csq": csq.astype(bf), "ssq": ssq.astype(bf),
                     "csk": csk.astype(bf), "ssk": ssk.astype(bf),
                     "ones512": ones512, "idb": identb, "wo": wo3, "boc": bo3})
    r = _run(nc, maps)
    s3 = [r[c]["s2"] for c in range(8)]

    nc = _bld("l3f", build_l3_ffn)
    wrp = _padw(I["L3_ffn_Wr"], I["L3_ffn_br"])
    wdp = np.zeros((DC + 1, 128, 8, 16), np.float32)
    for e in range(8):
        for cc in range(DC):
            wdp[cc, :, e, :] = I["L3_ffn_Wd"][e][cc * 128:(cc + 1) * 128]
        wdp[DC, 0, e, :] = I["L3_ffn_bd"][e]
    wup = np.ascontiguousarray(np.transpose(I["L3_ffn_Wu"], (1, 0, 2)))
    r = _run(nc, [{"xn": xn3[c], "wr": wrp, "wdp": wdp, "wup": wup,
                   "bup": I["L3_ffn_bu"], "idf": ident} for c in range(8)])
    g3 = [r[c]["g3"] for c in range(8)]

    nc = _bld("cl", build_combine_ln, True)
    r = _run(nc, [{"x": x3[c], "s": s3[c], "g": g3[c]} for c in range(8)])
    out_fm = np.concatenate([r[c]["xnew"] for c in range(8)], 2)
    return from_fm(out_fm).reshape(B, S, D).astype(np.float32)


def kernel(**inputs):
    I = {k: np.asarray(v, dtype=np.float32) for k, v in inputs.items()}
    ident = _ident()
    identb = ident.astype(bf)
    ones512 = np.ones((1, 512), np.float32).astype(bf)
    flip = np.zeros((128, 128), np.float32)
    flip[np.arange(128), 127 - np.arange(128)] = 1.0
    flipb = flip.astype(bf)

    x_fm = to_fm(I["L0_attn_hidden_states"])
    xsh = shard_tok(x_fm)
    kT = np.ascontiguousarray(I["L2_ffn_kernel"].T)

    nc = _bld("LA", build_launchA)
    maps = []
    for c in range(NCORES):
        scl, scm = qscat_host(c)
        maps.append({
            "gsl": I["L0_ffn_gate_w"][c * 1024:(c + 1) * 1024],
            "usl": I["L0_ffn_up_w"][c * 1024:(c + 1) * 1024],
            "dsl": I["L0_ffn_down_w"][c * 256:(c + 1) * 256],
            "ksl": kT[c * 256:(c + 1) * 256],
            "bias2": I["L2_ffn_bias"][None, :],
            "relp": I["L1_attn_rel_pos"],
            "idp": identb, "scl": scl, "scm": scm, "x0": xsh[c],
        })
    rA = _run(nc, maps)
    wgu_full = np.concatenate([rA[c]["wguc"] for c in range(8)], 0)
    wdq_full = np.concatenate([rA[c]["wdqc"] for c in range(8)], 2)
    kqw_full = np.concatenate([rA[c]["kqwc"] for c in range(8)], 2)
    kscc = np.ascontiguousarray(np.concatenate([rA[c]["ksc"] for c in range(8)], 0).T)
    bqc = np.ascontiguousarray(rA[0]["bqv"][0].reshape(DC, 128).T)
    qdvs = [rA[c]["qdv"] for c in range(8)]
    xn0s = [rA[c]["xn0"] for c in range(8)]
    xn_full = np.concatenate(xn0s, 2)

    nc = _bld("LB", build_launchB)
    maps = l0_attn_host_inputs(xn_full)
    for c in range(NCORES):
        m = maps[c]
        m["xnf"] = m.pop("xn")
        heads = list(range(c * L0_HPC, (c + 1) * L0_HPC))
        rows = []
        for part in range(3):
            for h in heads:
                rows.extend(range(h * 192 + part * 64, h * 192 + part * 64 + 64))
        m["wq"] = to_fm(I["L0_attn_qkv_w"][rows, :])
    rB = _run(nc, maps)
    attn_fm = np.zeros((2048, NTOK), bf)
    for c in range(8):
        at = rB[c]["attn"]
        for i in range(4):
            attn_fm[(c * 4 + i) * 64:(c * 4 + i) * 64 + 64, :] = at[i]
    attn_fm = attn_fm.reshape(DC, 128, NTOK)

    nc = _bld("LC", build_launchC)
    wg = to_fm(I["L0_attn_gate_w"])
    wo0 = to_fm(I["L0_attn_out_w"])
    nw = np.ascontiguousarray(I["L0_attn_norm_w"].reshape(DC, 128).T)
    wq1, wk1, wv1 = to_fm(I["L1_attn_wq"]), to_fm(I["L1_attn_wk"]), to_fm(I["L1_attn_wv"])
    cls = to_fm(I["L1_ffn_classifier_w"])
    wgu_e = np.zeros((8, DC, 128, 2048), np.float32)
    wd_e = np.zeros((8, 8, 128, 2048), np.float32)
    for e in range(8):
        for hc in range(DC):
            wgu_e[e, hc, :, 0:1024] = I["L1_ffn_gate_w"][e][:, hc * 128:(hc + 1) * 128].T
            wgu_e[e, hc, :, 1024:2048] = I["L1_ffn_up_w"][e][:, hc * 128:(hc + 1) * 128].T
        for ic in range(8):
            wd_e[e, ic] = I["L1_ffn_down_w"][e][:, ic * 128:(ic + 1) * 128].T
    maps = [{"attn": np.ascontiguousarray(attn_fm[:, :, c * TPC:(c + 1) * TPC]),
             "xn0": xn0s[c], "x0": xsh[c], "nw": nw, "wg": wg, "wo0": wo0,
             "idf": ident, "idb": identb, "wgu": wgu_full, "wdq": wdq_full,
             "wq1": wq1, "wk1": wk1, "wv1": wv1,
             "cls": cls, "ebias": I["L1_ffn_e_score_correction_bias"][None, :],
             "idf2": ident, "wgu_e": wgu_e, "wd_e": wd_e} for c in range(8)]
    rC = _run(nc, maps)
    x1s = [rC[c]["x1"] for c in range(8)]
    qf1s = [rC[c]["qf1"] for c in range(8)]
    g1s = [rC[c]["g1"] for c in range(8)]
    kf1f = [np.concatenate([rC[c2]["kf1"] for c2 in range(sq * 4, sq * 4 + 4)], 2)
            for sq in range(2)]
    vte1f = [np.concatenate([rC[c2]["vte1"] for c2 in range(sq * 4, sq * 4 + 4)], 0)
             for sq in range(2)]

    nc = _bld("LD", build_launchD)
    wo1 = to_fm(I["L1_attn_wo"])
    perm = _l2_perm()
    w_rk2 = I["L2_attn_w_rk"][:, perm]
    b_rk2 = I["L2_attn_b_rk"][perm]
    w_rq2 = I["L2_attn_w_rq"].reshape(1536, 16, 64)[:, :, perm].reshape(1536, 1024)
    b_rq2 = I["L2_attn_b_rq"].reshape(16, 64)[:, perm].reshape(1024)
    m2 = {"m2w1": _padw(I["L2_attn_w_dkv"], I["L2_attn_b_dkv"]),
          "m2w2": _padw(I["L2_attn_w_dq"], I["L2_attn_b_dq"]),
          "m2w3": _padw(w_rk2, b_rk2, 128),
          "m2w4": _padw(I["L2_attn_w_uk"], I["L2_attn_b_uk"]),
          "m2w5": _padw(I["L2_attn_w_uv"], I["L2_attn_b_uv"]),
          "m2w6": _padw(I["L2_attn_w_uq"], I["L2_attn_b_uq"]),
          "m2w7": _padw(w_rq2, b_rq2)}
    maps = []
    for c in range(8):
        sq = c // 4
        maps.append(dict(m2, qf1=qf1s[c], kf1f=kf1f[sq], vte1f=vte1f[sq],
                         qdv=qdvs[c], flip=flipb, ones512=ones512, wo1=wo1,
                         x1=x1s[c], g1=g1s[c],
                         kqw=kqw_full, bqc=bqc, kscc=kscc, idf=ident))
    rD = _run(nc, maps)
    x2s = [rD[c]["x2"] for c in range(8)]
    qf2s = [rD[c]["qf2"] for c in range(8)]
    rq2s = [rD[c]["rq2"] for c in range(8)]
    g2s = [rD[c]["g2"] for c in range(8)]
    kf2f = [np.concatenate([rD[c2]["kf2"] for c2 in range(sq * 4, sq * 4 + 4)], 2)
            for sq in range(2)]
    rk2f = [np.ascontiguousarray(np.concatenate(
        [rD[c2]["rk2"] for c2 in range(sq * 4, sq * 4 + 4)], 1)[0:64]) for sq in range(2)]
    vt2f = [np.concatenate([rD[c2]["vt2"] for c2 in range(sq * 4, sq * 4 + 4)], 0)
            for sq in range(2)]

    nc = _bld("LE", build_launchE)
    wo2 = to_fm(I["L2_attn_w_o"].T)
    bo2 = np.ascontiguousarray(I["L2_attn_b_o"].reshape(DC, 128).T)
    z512 = np.zeros(512, np.float32)
    zD = np.zeros(D, np.float32)
    m3 = {"m3w1": _padw(I["L3_attn_Wkvd"], z512),
          "m3w2": _padw(I["L3_attn_Wqd"], z512),
          "m3w3": _padw(I["L3_attn_Wkr"], np.zeros(64, np.float32), 128),
          "m3w4": _padw(I["L3_attn_Wku"], zD),
          "m3w5": _padw(I["L3_attn_Wvu"], zD),
          "m3w6": _padw(I["L3_attn_Wqu"], zD),
          "m3w7": _padw(I["L3_attn_Wqr"], np.zeros(1024, np.float32))}
    wrp = _padw(I["L3_ffn_Wr"], I["L3_ffn_br"])
    wdp = np.zeros((DC + 1, 128, 8, 16), np.float32)
    for e in range(8):
        for cc in range(DC):
            wdp[cc, :, e, :] = I["L3_ffn_Wd"][e][cc * 128:(cc + 1) * 128]
        wdp[DC, 0, e, :] = I["L3_ffn_bd"][e]
    wup = np.ascontiguousarray(np.transpose(I["L3_ffn_Wu"], (1, 0, 2)))
    maps = []
    for c in range(8):
        sq, q0 = c // 4, (c % 4) * TPC
        cq, sq2 = _rope_tabs_l2(np.arange(q0, q0 + TPC))
        ck, sk2 = _rope_tabs_l2(np.arange(S))
        csq, ssq = _cs_tiles(cq, sq2)
        csk, ssk = _cs_tiles(ck, sk2)
        maps.append(dict(m3, qf=qf2s[c], rqr=rq2s[c], kff=kf2f[sq],
                         rkf=rk2f[sq], vtf=vt2f[sq],
                         mt=_mask_tiles(q0).astype(bf),
                         csq=csq.astype(bf), ssq=ssq.astype(bf),
                         csk=csk.astype(bf), ssk=ssk.astype(bf),
                         ones512=ones512, idb=identb, woc=wo2, boc=bo2,
                         xin=x2s[c], gin=g2s[c],
                         wr=wrp, wdp=wdp, wup=wup, bup=I["L3_ffn_bu"], idf=ident))
    rE = _run(nc, maps)
    x3s = [rE[c]["xout"] for c in range(8)]
    qf3s = [rE[c]["qf3"] for c in range(8)]
    rq3s = [rE[c]["rq3"] for c in range(8)]
    g3s = [rE[c]["g3"] for c in range(8)]
    kf3f = [np.concatenate([rE[c2]["kf3"] for c2 in range(sq * 4, sq * 4 + 4)], 2)
            for sq in range(2)]
    rk3f = [np.ascontiguousarray(np.concatenate(
        [rE[c2]["rk3"] for c2 in range(sq * 4, sq * 4 + 4)], 1)[0:64]) for sq in range(2)]
    vt3f = [np.concatenate([rE[c2]["vt3"] for c2 in range(sq * 4, sq * 4 + 4)], 0)
            for sq in range(2)]

    nc = _bld("LF", build_launchF)
    wo3 = to_fm(I["L3_attn_Wo"].T)
    bo3 = np.zeros((128, DC), np.float32)
    maps = []
    for c in range(8):
        sq, q0 = c // 4, (c % 4) * TPC
        cq, sq2 = _rope_tabs_l3(np.arange(q0, q0 + TPC))
        ck, sk2 = _rope_tabs_l3(np.arange(S))
        csq, ssq = _cs_tiles(cq, sq2)
        csk, ssk = _cs_tiles(ck, sk2)
        maps.append({"qf": qf3s[c], "rqr": rq3s[c], "kff": kf3f[sq],
                     "rkf": rk3f[sq], "vtf": vt3f[sq],
                     "mt": _mask_tiles(q0).astype(bf),
                     "csq": csq.astype(bf), "ssq": ssq.astype(bf),
                     "csk": csk.astype(bf), "ssk": ssk.astype(bf),
                     "ones512": ones512, "idb": identb, "woc": wo3, "boc": bo3,
                     "xin": x3s[c], "gin": g3s[c]})
    rF = _run(nc, maps)
    out_fm = np.concatenate([rF[c]["xout"] for c in range(8)], 2)
    return from_fm(out_fm).reshape(2, 4096, 2048)


# version 11
# speedup vs baseline: 1.2939x; 1.0125x over previous
from contextlib import ExitStack

import numpy as np
import ml_dtypes
import concourse.bass as bass
import concourse.mybir as mybir
from concourse.bass_utils import run_bass_kernel_spmd

F32 = mybir.dt.float32
BF16 = mybir.dt.bfloat16
I32 = mybir.dt.int32
U8 = mybir.dt.uint8
FP8 = mybir.dt.float8e4
AF = mybir.ActivationFunctionType
ALU = mybir.AluOpType
AX = mybir.AxisListType

NCORES = 8
B, S, D = 2, 4096, 2048
NTOK = B * S
TPC = NTOK // NCORES
DC = D // 128
MAGIC = float(np.float32(1.5 * 2 ** 23))

bf = ml_dtypes.bfloat16


class Ctx:

    def __init__(self, nc, pfx=None):
        self.nc = nc
        self.es = ExitStack()
        self.pfx = pfx
        self.handles = []

    def _n(self, name):
        return (self.pfx or "") + name

    def sb(self, name, shape, dtype=F32):
        return self.es.enter_context(self.nc.sbuf_tensor(self._n(name), shape, dtype))

    def ps(self, name, shape, dtype=F32):
        return self.es.enter_context(self.nc.psum_tensor(self._n(name), shape, dtype))

    def sem(self, name):
        if self.pfx is None:
            return self.es.enter_context(self.nc.semaphore(name))
        h = self.nc.alloc_semaphore(self._n(name))
        self.handles.append(h)
        return h

    def sems(self, *names):
        return [self.sem(n) for n in names]

    def block(self):
        return self.es.enter_context(self.nc.Block())

    def close(self):
        self.es.close()
        if self.pfx is not None:
            self.nc.all_engine_barrier()
            self.nc.clear_and_free_semaphores(self.handles)
            self.nc.all_engine_barrier()


def _io(mega, nc, name, shape, dtype, out=False):
    if mega is not None:
        return mega[name]
    return nc.dram_tensor(name, shape, dtype,
                          kind="ExternalOutput" if out else "ExternalInput").ap()


def to_fm(x):
    x = np.ascontiguousarray(x).reshape(-1, x.shape[-1])
    t, d = x.shape
    return np.ascontiguousarray(x.T.reshape(d // 128, 128, t))


def from_fm(xfm):
    dc, p, t = xfm.shape
    return np.ascontiguousarray(xfm.reshape(dc * p, t).T)


def shard_tok(x_fm):
    return [np.ascontiguousarray(x_fm[:, :, c * TPC:(c + 1) * TPC]) for c in range(NCORES)]


def build_combine_ln(has_combine, T=TPC, mega=None):
    nc = mega["nc"] if mega else bass.Bass("TRN2", target_bir_lowering=False)
    Xd = _io(mega, nc, "x", [DC, 128, T], F32)
    if has_combine:
        Sd = _io(mega, nc, "s", [DC, 128, T], F32)
        Gd = _io(mega, nc, "g", [DC, 128, T], F32)
        XNEWd = _io(mega, nc, "xnew", [DC, 128, T], F32, out=True)
    XNd = _io(mega, nc, "xn", [DC, 128, T], F32, out=True)
    NTB = T // 512

    cx = Ctx(nc, pfx=mega["pfx"] if mega else None)
    xt = cx.sb("xt", [128, DC, T], F32)
    st = cx.sb("st", [128, DC, T], F32)
    gt = cx.sb("gt", [128, DC, T], F32)
    gview = gt[:].bitcast(BF16)
    xb = gview[:, :, 0:T]
    sq = gview[:, :, T:2 * T]
    xnt = st[:]
    ones_b = cx.sb("ones_b", [128, 1], BF16)
    ones_f = cx.sb("ones_f", [1, 128], F32)
    mrow = cx.sb("mrow", [1, T], F32)
    vrow = cx.sb("vrow", [1, T], F32)
    rrow = cx.sb("rrow", [1, T], F32)
    psum_s = cx.ps("psum_s", [1, NTB, 512], F32)
    psum_q = cx.ps("psum_q", [1, NTB, 512], F32)
    pm = cx.ps("pm", [128, NTB, 512], F32)
    pr = cx.ps("pr", [128, NTB, 512], F32)
    (ld, cmb, prep, mm, rows, rst, bc, ap_, stm, vch, sch, lds, ldg,
     stm2) = cx.sems(
        "ld", "cmb", "prep", "mm", "rows", "rst", "bc", "ap", "stm", "vch", "sch",
        "lds", "ldg", "stm2")
    blk = cx.block()
    if True:
        nld = 3 if has_combine else 1

        @blk.sync
        def _(sync):
            sync.dma_start(out=xt[:], in_=Xd.rearrange("c p t -> p c t")).then_inc(ld, 16)

        @blk.scalar
        def _(sc):
            if has_combine:
                sc.dma_start(out=st[:], in_=Sd.rearrange("c p t -> p c t")).then_inc(lds, 16)
                sc.wait_ge(ldg, 16)
                sc.activation(out=gt[:], in_=gt[:], func=AF.Silu).then_inc(cmb, 1)
                sc.wait_ge(cmb, 2)
            else:
                sc.wait_ge(ld, 16)
            sc.activation(out=sq, in_=xt[:], func=AF.Square).then_inc(prep, 1)
            sc.wait_ge(rows, 1)
            sc.activation(out=rrow[:], in_=vrow[:], func=AF.Ln).then_inc(sch, 1)
            sc.wait_ge(sch, 1)
            sc.activation(out=rrow[:], in_=rrow[:], func=AF.Exp, scale=-0.5).then_inc(rst, 1)
            if has_combine:
                sc.wait_ge(ap_, 1)
                sc.dma_start(out=XNEWd.rearrange("c p t -> p c t"), in_=xt[:]).then_inc(stm2, 16)
                sc.wait_ge(stm2, 16)

        @blk.vector
        def _(v):
            nch = [0]

            def ch(instr):
                nch[0] += 1
                instr.then_inc(vch, 1)
                v.wait_ge(vch, nch[0])

            if has_combine:
                v.wait_ge(cmb, 1)
                v.wait_ge(ld, 16)
                v.wait_ge(lds, 16)
                ch(v.tensor_tensor(st[:], st[:], gt[:], op=ALU.mult))
                ch(v.tensor_tensor(xt[:], xt[:], st[:], op=ALU.add))
                v.sem_inc(cmb, 1)
            else:
                v.wait_ge(ld, 16)
            v.tensor_copy(xb, xt[:]).then_inc(prep, 1)
            v.wait_ge(mm, NTB)
            rs = 1.0 / D
            for tb in range(NTB):
                sl = slice(tb * 512, (tb + 1) * 512)
                v.tensor_scalar(mrow[:, sl], psum_s[0:1, tb, :], rs, None, op0=ALU.mult)
                ch(v.tensor_scalar(vrow[:, sl], psum_q[0:1, tb, :], rs, None, op0=ALU.mult))
            ch(v.tensor_tensor(rrow[:], mrow[:], mrow[:], op=ALU.mult))
            ch(v.tensor_tensor(vrow[:], vrow[:], rrow[:], op=ALU.subtract))
            v.tensor_scalar(vrow[:], vrow[:], 1e-5, None, op0=ALU.add).then_inc(rows, 1)
            for tb in range(NTB):
                v.wait_ge(bc, tb + 1)
                sl = slice(tb * 512, (tb + 1) * 512)
                ch(v.tensor_tensor(xnt[:, :, sl], xt[:, :, sl],
                                   pm[:, tb, :].unsqueeze(1).broadcast_to([128, DC, 512]), op=ALU.subtract))
                last = v.tensor_tensor(xnt[:, :, sl], xnt[:, :, sl],
                                       pr[:, tb, :].unsqueeze(1).broadcast_to([128, DC, 512]), op=ALU.mult)
            last.then_inc(ap_, 1)

        @blk.gpsimd
        def _(g):
            if has_combine:
                g.dma_start(out=gt[:], in_=Gd.rearrange("c p t -> p c t")).then_inc(ldg, 16)
            else:
                g.sem_inc(ldg, 16)
            g.memset(ones_b[:], 1.0)
            g.memset(ones_f[:], 1.0).then_inc(prep, 1)
            g.wait_ge(ap_, 1)
            g.dma_start(out=XNd.rearrange("c p t -> p c t"), in_=xnt).then_inc(stm, 16)
            g.wait_ge(stm, 16)

        @blk.tensor
        def _(t):
            t.wait_ge(prep, 3)
            for tb in range(NTB):
                sl = slice(tb * 512, (tb + 1) * 512)
                for c in range(DC):
                    t.matmul(psum_s[0:1, tb, :], ones_b[:], xb[:, c, sl], start=(c == 0), stop=(c == DC - 1))
                for c in range(DC):
                    mmi = t.matmul(psum_q[0:1, tb, :], ones_b[:], sq[:, c, sl], start=(c == 0), stop=(c == DC - 1))
                mmi.then_inc(mm, 1)
            t.wait_ge(rst, 1)
            for tb in range(NTB):
                sl = slice(tb * 512, (tb + 1) * 512)
                t.matmul(pm[:, tb, :], ones_f[:], mrow[:, sl], start=True, stop=True)
                t.matmul(pr[:, tb, :], ones_f[:], rrow[:, sl], start=True, stop=True).then_inc(bc, 1)

    cx.close()
    return nc



L0_H, L0_D, L0_BS = 32, 64, 256
L0_HPC = L0_H // NCORES
L0_NB = S // L0_BS


def l0_decay_consts(heads):
    H, BS = L0_H, L0_BS
    base = 1.0 / 2.0 ** (8.0 / H)
    rate = (base ** (np.arange(H, dtype=np.float32) + 1.0)).astype(np.float32)
    factor = np.float32(1.0 - 0.0 / (32 - 1 + 1e-05) + 1e-05)
    slope = (rate * np.float32(factor)).astype(np.float32)[heads]
    r = np.arange(BS, dtype=np.float32) + 1.0
    qd = np.exp(-slope[:, None] * r[None, :]).astype(np.float32)
    kd = np.exp(-slope[:, None] * (BS - r[None, :])).astype(np.float32)
    dd = slope[:, None, None].astype(np.float64) * (r[:, None] - r[None, :])[None]
    dd = np.where(dd >= 0, -dd, -np.inf)
    dd = np.exp(dd).astype(np.float32)
    bd = np.exp(-slope * np.float32(BS)).astype(np.float32)
    ddT = np.ascontiguousarray(np.transpose(dd, (0, 2, 1)))
    return qd, kd, ddT, bd


def build_l0_attn(bd4, mega=None):
    nc = mega["nc"] if mega else bass.Bass("TRN2", target_bir_lowering=False)
    XNd = _io(mega, nc, "xn", [DC, 128, NTOK], F32)
    Wd = _io(mega, nc, "wq", [DC, 128, 768], F32)
    DDd = _io(mega, nc, "ddt", [L0_HPC, 2, 128, L0_BS], F32)
    QDd = _io(mega, nc, "qdb", [64, L0_HPC, L0_BS], F32)
    KDd = _io(mega, nc, "kdc_d", [128, 2, L0_HPC], F32)
    IDd = _io(mega, nc, "ident", [128, 128], F32)
    BDd = _io(mega, nc, "bd4", [64, L0_HPC], F32)
    ATd = _io(mega, nc, "attn", [L0_HPC, 64, NTOK], BF16, out=True)

    NBLK = 2 * L0_NB if nblk is None else nblk
    HPC, BS = L0_HPC, L0_BS

    cx = Ctx(nc, pfx=mega["pfx"] if mega else None)
    wt = cx.sb("wt", [128, DC, 768], BF16)
    xnt = cx.sb("xnt", [128, 2, DC, BS], BF16)
    qkv = cx.sb("qkv", [128, 2, 6, BS], BF16)
    ddt = cx.sb("ddt", [128, HPC, 2, BS], BF16)
    qdb = cx.sb("qdb", [64, HPC, BS], F32)
    kdc = cx.sb("kdc", [128, 2, HPC], F32)
    iden = cx.sb("iden", [128, 64], BF16)
    bdt = cx.sb("bdt", [64, HPC], F32)
    qp = cx.sb("qp", [64, BS], BF16)
    pt = cx.sb("pt", [128, 2, BS], BF16)
    ktm = cx.sb("ktm", [128, 2, 64], BF16)
    vtm = cx.sb("vtm", [128, 2, 64], BF16)
    state = cx.sb("state", [64, HPC, 64], F32)
    stateb = cx.sb("stateb", [64, HPC, 64], BF16)
    atile = cx.sb("atile", [64, 2, HPC, BS], BF16)

    pproj = cx.ps("pproj", [128, 6, BS], F32)
    pst = cx.ps("pst", [128, 2, BS], F32)
    ptr = cx.ps("ptr", [128, 4, 64], BF16)
    po = cx.ps("po", [64, BS], F32)
    ptm = cx.ps("ptm", [64, 64], F32)

    (wld, wldg, xnld, projmm, silu, tpose, ev1, mmS, pT, mmO, mmT,
     evO, stup, outd, chn) = cx.sems("wld", "wldg", "xnld", "projmm", "silu", "tpose",
                                "ev1", "mmS", "pT", "mmO", "mmT", "evO", "stup", "outd", "chn")
    blk = cx.block()
    if True:
        def tok0(bi):
            sq, b = divmod(bi, L0_NB)
            return sq * S + b * BS

        def nresets(bi):
            return HPC * (1 + (1 if bi >= L0_NB else 0))

        @blk.sync
        def _(sync):
            sync.dma_start(out=wt[:], in_=Wd.rearrange("c p m -> p c m")).then_inc(wld, 16)
            sync.dma_start(out=ddt[:], in_=DDd.rearrange("h c p i -> p h c i")).then_inc(wld, 16)
            sync.dma_start(out=qdb[:], in_=QDd).then_inc(wld, 16)
            sync.dma_start(out=kdc[:], in_=KDd).then_inc(wld, 16)
            sync.dma_start(out=iden[:], in_=IDd).then_inc(wld, 16)
            sync.dma_start(out=bdt[:], in_=BDd).then_inc(wld, 16)

        @blk.gpsimd
        def _(g):
            for bi in range(NBLK):
                if bi >= 2:
                    g.wait_ge(projmm, bi - 1)
                t0 = tok0(bi)
                g.dma_start(out=xnt[:, bi % 2, :, :],
                            in_=XNd[:, :, t0:t0 + BS].rearrange("c p t -> p c t")).then_inc(xnld, 16)
            for bi in range(NBLK):
                g.wait_ge(evO, HPC * (bi + 1))
                t0 = tok0(bi)
                g.dma_start(out=ATd[:, :, t0:t0 + BS].rearrange("h d t -> d h t"),
                            in_=atile[:, bi % 2, :, :]).then_inc(outd, 16)
            g.wait_ge(outd, 16 * NBLK)

        @blk.tensor
        def _(t):
            t.wait_ge(wld, 16 * 5)
            t.wait_ge(wldg, 16)
            for bi in range(NBLK):
                t.wait_ge(xnld, 16 * (bi + 1))
                t.wait_ge(silu, bi)
                for o in range(6):
                    for c in range(DC):
                        mmi = t.matmul(pproj[:, o, :], wt[:, c, o * 128:(o + 1) * 128],
                                       xnt[:, bi % 2, c, :], start=(c == 0), stop=(c == DC - 1))
                mmi.then_inc(projmm, 1)
                t.wait_ge(silu, bi + 1)
                for h in range(HPC):
                    idx = bi * HPC + h + 1
                    pidx = (bi - 1) * HPC + h + 1
                    if bi > 0:
                        t.wait_ge(ev1, pidx)
                    ko, kp = 2 + h // 2, (h % 2) * 64
                    vo, vp = 4 + h // 2, (h % 2) * 64
                    for jc in range(2):
                        t.transpose(ptr[:, jc, :], qkv[ko * 0 + kp:kp + 64, bi % 2, ko, jc * 128:(jc + 1) * 128], iden[:])
                        mmi = t.transpose(ptr[:, 2 + jc, :], qkv[vp:vp + 64, bi % 2, vo, jc * 128:(jc + 1) * 128], iden[:])
                    mmi.then_inc(tpose, 1)
                for h in range(HPC):
                    idx = bi * HPC + h + 1
                    pidx = (bi - 1) * HPC + h + 1
                    if bi > 0:
                        t.wait_ge(pT, pidx)
                    qo, qp_ = h // 2, (h % 2) * 64
                    ko, kp = 2 + h // 2, (h % 2) * 64
                    for jc in range(2):
                        mmi = t.matmul(pst[:, jc, :], qkv[kp:kp + 64, bi % 2, ko, jc * 128:(jc + 1) * 128],
                                       qkv[qp_:qp_ + 64, bi % 2, qo, :], start=True, stop=True)
                    mmi.then_inc(mmS, 1)
                for h in range(HPC):
                    idx = bi * HPC + h + 1
                    pidx = (bi - 1) * HPC + h + 1
                    t.wait_ge(pT, idx)
                    if bi > 0:
                        t.wait_ge(evO, pidx)
                    t.wait_ge(stup, (bi - 1) * HPC + h + 1 + nresets(bi))
                    t.matmul(po[:], vtm2(cx, bi, h)[0][:, 0, :] if False else vtm[:, 0, :], pt[:, 0, :], start=True, stop=False) if False else None
                    t.matmul(po[:], vtmh(h)[:, 0, :] if False else vtm_h(h, 0), pt[:, 0, :], start=True, stop=False) if False else None
                    pass
    cx.close()
    return nc



L0_H, L0_D, L0_BS = 32, 64, 256
L0_HPC = L0_H // NCORES
L0_NB = S // L0_BS


def l0_decay_consts(heads):
    H, BS = L0_H, L0_BS
    base = 1.0 / 2.0 ** (8.0 / H)
    rate = (base ** (np.arange(H, dtype=np.float32) + 1.0)).astype(np.float32)
    factor = np.float32(1.0 - 0.0 / (32 - 1 + 1e-05) + 1e-05)
    slope = (rate * factor).astype(np.float32)[heads]
    r = np.arange(BS, dtype=np.float32) + 1.0
    qd = np.exp(-slope[:, None] * r[None, :]).astype(np.float32)
    kd = np.exp(-slope[:, None] * (BS - r[None, :])).astype(np.float32)
    dd = slope[:, None, None] * (r[:, None] - r[None, :])[None]
    dd = np.where(dd >= 0, -dd, -np.inf)
    dd = np.exp(dd).astype(np.float32)
    bd = np.exp(-slope * np.float32(BS)).astype(np.float32)
    ddT = np.ascontiguousarray(np.transpose(dd, (0, 2, 1)))
    return qd, kd, ddT, bd


def build_l0_attn(nblk=None, mega=None):
    nc = mega["nc"] if mega else bass.Bass("TRN2", target_bir_lowering=False)
    XNd = _io(mega, nc, "xn", [DC, 128, NTOK], F32)
    Wd = _io(mega, nc, "wq", [DC, 128, 768], F32)
    DDd = _io(mega, nc, "ddt_d", [L0_HPC, 2, 128, L0_BS], BF16)
    QDd = _io(mega, nc, "qdb_d", [128, 2, L0_BS], BF16)
    KDd = _io(mega, nc, "kdc_d", [128, 2, L0_HPC], F32)
    IDd = _io(mega, nc, "ident", [128, 64], BF16)
    BDd = _io(mega, nc, "bd4", [64, L0_HPC], F32)
    ATd = _io(mega, nc, "attn", [L0_HPC, 64, NTOK], BF16, out=True)

    NBLK = 2 * L0_NB if nblk is None else nblk
    HPC, BS = L0_HPC, L0_BS

    cx = Ctx(nc, pfx=mega["pfx"] if mega else None)
    wt = cx.sb("wt", [128, DC, 768], BF16)
    xnt = cx.sb("xnt", [128, 2, DC, BS], BF16)
    qkv = cx.sb("qkv", [128, 2, 6, BS], BF16)
    ddt = cx.sb("ddt", [128, HPC, 2, BS], BF16)
    qdb = cx.sb("qdb", [128, 2, BS], BF16)
    kdc = cx.sb("kdc", [128, 2, HPC], F32)
    iden = cx.sb("iden", [128, 64], BF16)
    bdt = cx.sb("bdt", [64, HPC], F32)
    qp = cx.sb("qp", [64, HPC, BS], BF16)
    pt = cx.sb("pt", [128, HPC, 2, BS], BF16)
    ktm = cx.sb("ktm", [128, HPC, 2, 64], BF16)
    vtm = cx.sb("vtm", [128, HPC, 2, 64], BF16)
    state = cx.sb("state", [64, HPC, 64], F32)
    stateb = cx.sb("stateb", [64, HPC, 64], BF16)
    atile = cx.sb("atile", [64, 2, HPC, BS], BF16)

    pproj = cx.ps("pproj", [128, 6, BS], F32)
    pst = cx.ps("pst", [128, 2, BS], F32)
    ptr = cx.ps("ptr", [128, 4, 64], BF16)
    po = cx.ps("po", [64, BS], F32)
    ptm = cx.ps("ptm", [64, 64], F32)

    (wld, wldg, xnld, projmm, silu, tpose, ev1, mmS, pT, mmO, mmT,
     evO, stup, outd, chn) = cx.sems("wld", "wldg", "xnld", "projmm", "silu", "tpose",
                                "ev1", "mmS", "pT", "mmO", "mmT", "evO", "stup", "outd", "chn")
    blk = cx.block()
    if True:
        def tok0(bi):
            sq, b = divmod(bi, L0_NB)
            return sq * S + b * BS

        def R(bi):
            return HPC * (1 + (1 if bi >= L0_NB else 0))

        @blk.sync
        def _(sync):
            sync.dma_start(out=ddt[:], in_=DDd.rearrange("h c p i -> p h c i")).then_inc(wld, 16)
            sync.dma_start(out=qdb[:], in_=QDd).then_inc(wld, 16)
            sync.dma_start(out=kdc[:], in_=KDd).then_inc(wld, 16)
            sync.dma_start(out=iden[:], in_=IDd).then_inc(wld, 16)
            sync.dma_start(out=bdt[:], in_=BDd).then_inc(wld, 16)

        @blk.gpsimd
        def _(g):
            g.dma_start(out=wt[:], in_=Wd.rearrange("c p m -> p c m")).then_inc(wldg, 16)
            for bi in range(NBLK):
                if bi >= 1:
                    g.wait_ge(xnld, 16 * bi)
                if bi >= 2:
                    g.wait_ge(projmm, bi - 1)
                t0 = tok0(bi)
                g.dma_start(out=xnt[:, bi % 2, :, :],
                            in_=XNd[:, :, t0:t0 + BS].rearrange("c p t -> p c t")).then_inc(xnld, 16)
                if bi >= 1:
                    g.wait_ge(evO, HPC * bi)
                    g.wait_ge(outd, 16 * (bi - 1))
                    t0p = tok0(bi - 1)
                    g.dma_start(out=ATd[:, :, t0p:t0p + BS].rearrange("h d t -> d h t"),
                                in_=atile[:, (bi - 1) % 2, :, :]).then_inc(outd, 16)
            g.wait_ge(evO, HPC * NBLK)
            g.wait_ge(outd, 16 * (NBLK - 1))
            t0p = tok0(NBLK - 1)
            g.dma_start(out=ATd[:, :, t0p:t0p + BS].rearrange("h d t -> d h t"),
                        in_=atile[:, (NBLK - 1) % 2, :, :]).then_inc(outd, 16)
            g.wait_ge(outd, 16 * NBLK)

        @blk.scalar
        def _(sc):
            for bi in range(NBLK):
                sc.wait_ge(projmm, bi + 1)
                if bi >= 2:
                    sc.wait_ge(mmS, HPC * (bi - 1))
                    sc.wait_ge(pT, HPC * (bi - 1))
                for bk in range(3):
                    mmi = sc.activation(out=qkv[:, bi % 2, 2 * bk:2 * bk + 2, :],
                                        in_=pproj[:, 2 * bk:2 * bk + 2, :], func=AF.Silu)
                mmi.then_inc(silu, 1)

        @blk.tensor
        def _(t):
            t.wait_ge(wld, 16 * 5)
            t.wait_ge(wldg, 16)
            for bi in range(NBLK):
                e0 = bi * HPC
                t.wait_ge(xnld, 16 * (bi + 1))
                t.wait_ge(silu, bi)
                for o in range(6):
                    for c in range(DC):
                        mmi = t.matmul(pproj[:, o, :], wt[:, c, o * 128:(o + 1) * 128],
                                       xnt[:, bi % 2, c, :], start=(c == 0), stop=(c == DC - 1))
                mmi.then_inc(projmm, 1)
                t.wait_ge(silu, bi + 1)
                for h in range(HPC):
                    e = e0 + h
                    ko, kp = 2 + h // 2, (h % 2) * 64
                    vo = 4 + h // 2
                    t.wait_ge(ev1, e)
                    for jc in range(2):
                        t.transpose(ptr[:, jc, :], qkv[kp:kp + 64, bi % 2, ko, jc * 128:(jc + 1) * 128], iden[kp:kp + 64, :])
                        mmi = t.transpose(ptr[:, 2 + jc, :], qkv[kp:kp + 64, bi % 2, vo, jc * 128:(jc + 1) * 128], iden[kp:kp + 64, :])
                    mmi.then_inc(tpose, 1)
                for h in range(HPC):
                    e = e0 + h
                    qo, qpp = h // 2, (h % 2) * 64
                    ko, kp = 2 + h // 2, (h % 2) * 64
                    t.wait_ge(pT, e)
                    for jc in range(2):
                        mmi = t.matmul(pst[:, jc, :], qkv[kp:kp + 64, bi % 2, ko, jc * 128:(jc + 1) * 128],
                                       qkv[qpp:qpp + 64, bi % 2, qo, :], start=True, stop=True)
                    mmi.then_inc(mmS, 1)
                for h in range(HPC):
                    e = e0 + h
                    t.wait_ge(pT, e + 1)
                    t.wait_ge(evO, e)
                    t.wait_ge(stup, (bi - 1) * HPC + h + 1 + R(bi))
                    t.matmul(po[:], vtm[:, h, 0, :], pt[:, h, 0, :], start=True, stop=False)
                    t.matmul(po[:], vtm[:, h, 1, :], pt[:, h, 1, :], start=False, stop=False)
                    t.matmul(po[:], stateb[:, h, :], qp[:, h, :], start=False, stop=True).then_inc(mmO, 1)
                for h in range(HPC):
                    e = e0 + h
                    t.wait_ge(ev1, e + 1)
                    t.wait_ge(stup, e + R(bi))
                    t.matmul(ptm[:], ktm[:, h, 0, :], vtm[:, h, 0, :], start=True, stop=False)
                    t.matmul(ptm[:], ktm[:, h, 1, :], vtm[:, h, 1, :], start=False, stop=True).then_inc(mmT, 1)

        @blk.vector
        def _(v):
            v.wait_ge(wld, 16 * 5)
            v.wait_ge(wldg, 16)
            v.memset(state[:], 0.0)
            v.memset(stateb[:], 0.0).then_inc(stup, HPC)
            for bi in range(NBLK):
                e0 = bi * HPC
                if bi % L0_NB == 0 and bi > 0:
                    v.memset(state[:], 0.0)
                    v.memset(stateb[:], 0.0).then_inc(stup, HPC)
                for h in range(HPC):
                    e = e0 + h
                    v.wait_ge(tpose, e + 1)
                    if bi > 0:
                        v.wait_ge(mmT, (bi - 1) * HPC + h + 1)
                    for jc in range(2):
                        v.tensor_scalar(ktm[:, h, jc, :], ptr[:, jc, :], kdc[:, jc, h:h + 1], None, op0=ALU.mult)
                    mmi = v.tensor_copy(vtm[:, h, :, :], ptr[:, 2:4, :])
                    mmi.then_inc(ev1, 1)
                for h in range(HPC):
                    e = e0 + h
                    v.wait_ge(mmS, e + 1)
                    if bi > 0:
                        v.wait_ge(mmO, (bi - 1) * HPC + h + 1)
                    qo, qpp = h // 2, (h % 2) * 64
                    v.tensor_tensor(qp[:, h, :], qkv[qpp:qpp + 64, bi % 2, qo, :], qdb[qpp:qpp + 64, qo, :], op=ALU.mult)
                    v.tensor_tensor(pt[:, h, :, :], pst[:], ddt[:, h, :, :], op=ALU.mult).then_inc(pT, 1)
                for h in range(HPC):
                    e = e0 + h
                    v.wait_ge(mmO, e + 1)
                    if bi >= 1:
                        v.wait_ge(outd, 16 * (bi - 1))
                    v.tensor_copy(atile[:, bi % 2, h, :], po[:]).then_inc(evO, 1)
                for h in range(HPC):
                    e = e0 + h
                    v.wait_ge(mmT, e + 1)
                    v.scalar_tensor_tensor(state[:, h, :], state[:, h, :], bdt[:, h:h + 1],
                                           ptm[:], op0=ALU.mult, op1=ALU.add).then_inc(chn, 1)
                    v.wait_ge(chn, bi * HPC + h + 1)
                    v.tensor_copy(stateb[:, h, :], state[:, h, :]).then_inc(stup, 1)

    cx.close()
    return nc


def l0_attn_host_inputs(xn_fm_full):
    maps = []
    ident = np.concatenate([np.eye(64, dtype=np.float32)] * 2, 0).astype(bf)
    for c in range(NCORES):
        heads = list(range(c * L0_HPC, (c + 1) * L0_HPC))
        qd, kd, ddT, bd = l0_decay_consts(heads)
        qdb = np.zeros((128, 2, L0_BS), np.float32)
        for h in range(L0_HPC):
            qdb[(h % 2) * 64:(h % 2) * 64 + 64, h // 2, :] = qd[h][None, :]
        qdb = qdb.astype(bf)
        kdc = np.zeros((128, 2, L0_HPC), np.float32)
        for jc in range(2):
            kdc[:, jc, :] = kd[:, jc * 128:(jc + 1) * 128].T
        ddt = ddT.reshape(L0_HPC, 2, 128, L0_BS).astype(bf)
        bd4 = np.ascontiguousarray(np.broadcast_to(bd[None, :], (64, L0_HPC)))
        maps.append({"xn": xn_fm_full, "ddt_d": ddt, "qdb_d": qdb, "kdc_d": kdc,
                     "ident": ident, "bd4": bd4})
    return maps



def build_l0_post(T=TPC, mega=None):
    nc = mega["nc"] if mega else bass.Bass("TRN2", target_bir_lowering=False)
    ATd = _io(mega, nc, "attn", [DC, 128, T], BF16)
    XNd = _io(mega, nc, "xn", [DC, 128, T], F32)
    NWd = _io(mega, nc, "nw", [128, DC], F32)
    WGd = _io(mega, nc, "wg", [DC, 128, D], F32)
    WOd = _io(mega, nc, "wo", [DC, 128, D], F32)
    S0d = _io(mega, nc, "s0", [DC, 128, T], F32, out=True)
    NTB = T // 512

    cx = Ctx(nc, pfx=mega["pfx"] if mega else None)
    at = cx.sb("at", [128, DC, T], BF16)
    xnb = cx.sb("xnb", [128, DC, T], BF16)
    m = cx.sb("m", [128, DC, T], BF16)
    nw = cx.sb("nw_s", [128, DC], F32)
    wt = cx.sb("wtl", [128, DC, D], BF16)
    ones_b = cx.sb("ones_b", [128, 1], BF16)
    ones_f = cx.sb("ones_f", [1, 128], F32)
    rrow = cx.sb("rrow", [1, T], F32)
    so = cx.sb("so", [128, T], F32)
    pq = cx.ps("pq", [1, NTB, 512], F32)
    pr = cx.ps("pr", [128, NTB, 512], F32)
    pg = cx.ps("pg", [128, 2, 512], F32)
    (ld, ldw, sq_s, mm, rows, rst, bc, ms, gmm, gev, w2, omm, oev, stm,
     vch, sch) = cx.sems("ld", "ldw", "sq", "mm", "rows", "rst", "bc", "ms",
                         "gmm", "gev", "w2", "omm", "oev", "stm", "vch", "sch")
    blk = cx.block()
    if True:
        @blk.sync
        def _(sync):
            sync.dma_start(out=at[:], in_=ATd.rearrange("c p t -> p c t")).then_inc(ld, 16)
            sync.dma_start(out=nw[:], in_=NWd).then_inc(ld, 16)

        @blk.gpsimd
        def _(g):
            g.memset(ones_b[:], 1.0)
            g.memset(ones_f[:], 1.0).then_inc(ms, 1)
            g.dma_start(out=xnb[:], in_=XNd.rearrange("c p t -> p c t")).then_inc(ldw, 16)
            g.wait_ge(ldw, 16)
            g.dma_start(out=wt[:], in_=WGd.rearrange("c p m -> p c m")).then_inc(ldw, 16)
            g.wait_ge(gmm, 16 * NTB)
            g.dma_start(out=wt[:], in_=WOd.rearrange("c p m -> p c m")).then_inc(w2, 16)
            for o in range(DC):
                g.wait_ge(oev, NTB * (o + 1))
                g.wait_ge(stm, 16 * o)
                g.dma_start(out=S0d[o], in_=so[:]).then_inc(stm, 16)
            g.wait_ge(stm, 16 * DC)

        @blk.scalar
        def _(sc):
            sc.wait_ge(ld, 32)
            sc.activation(out=m[:], in_=at[:], func=AF.Square).then_inc(sq_s, 1)
            sc.wait_ge(rows, 1)
            sc.activation(out=rrow[:], in_=rrow[:], func=AF.Ln).then_inc(sch, 1)
            sc.wait_ge(sch, 1)
            sc.activation(out=rrow[:], in_=rrow[:], func=AF.Exp, scale=-0.5).then_inc(rst, 1)
            for i in range(DC * NTB):
                sc.wait_ge(gmm, i + 1)
                o, tb = divmod(i, NTB)
                sl = slice(tb * 512, (tb + 1) * 512)
                sc.activation(out=at[:, o, sl], in_=pg[:, i % 2, :], func=AF.Sigmoid).then_inc(gev, 1)

        @blk.vector
        def _(v):
            nch = [0]

            def ch(instr):
                nch[0] += 1
                instr.then_inc(vch, 1)
                v.wait_ge(vch, nch[0])

            v.wait_ge(mm, NTB)
            for tb in range(NTB):
                sl = slice(tb * 512, (tb + 1) * 512)
                ch(v.tensor_scalar(rrow[:, sl], pq[0:1, tb, :], 1.0 / D, 1e-5, op0=ALU.mult, op1=ALU.add))
            v.sem_inc(rows, 1)
            for tb in range(NTB):
                v.wait_ge(bc, tb + 1)
                sl = slice(tb * 512, (tb + 1) * 512)
                for c in range(DC):
                    mmi = v.scalar_tensor_tensor(m[:, c, sl], at[:, c, sl], nw[:, c:c + 1],
                                                 pr[:, tb, :], op0=ALU.mult, op1=ALU.mult)
            mmi.then_inc(ms, 1)
            for i in range(DC * NTB):
                v.wait_ge(gev, i + 1)
                o, tb = divmod(i, NTB)
                sl = slice(tb * 512, (tb + 1) * 512)
                v.tensor_tensor(at[:, o, sl], at[:, o, sl], m[:, o, sl], op=ALU.mult).then_inc(ms, 1)
            for i in range(DC * NTB):
                v.wait_ge(omm, i + 1)
                o, tb = divmod(i, NTB)
                sl = slice(tb * 512, (tb + 1) * 512)
                if tb == 0:
                    v.wait_ge(stm, 16 * o)
                v.tensor_copy(so[:, sl], pg[:, i % 2, :]).then_inc(oev, 1)

        @blk.tensor
        def _(t):
            t.wait_ge(sq_s, 1)
            t.wait_ge(ms, 1)
            for tb in range(NTB):
                sl = slice(tb * 512, (tb + 1) * 512)
                for c in range(DC):
                    mmi = t.matmul(pq[0:1, tb, :], ones_b[:], m[:, c, sl], start=(c == 0), stop=(c == DC - 1))
                mmi.then_inc(mm, 1)
            t.wait_ge(rst, 1)
            for tb in range(NTB):
                sl = slice(tb * 512, (tb + 1) * 512)
                t.matmul(pr[:, tb, :], ones_f[:], rrow[:, sl], start=True, stop=True).then_inc(bc, 1)
            t.wait_ge(ldw, 32)
            t.wait_ge(ms, 2)
            for i in range(DC * NTB):
                o, tb = divmod(i, NTB)
                sl = slice(tb * 512, (tb + 1) * 512)
                t.wait_ge(gev, max(0, i - 1))
                for c in range(DC):
                    mmi = t.matmul(pg[:, i % 2, :], wt[:, c, o * 128:(o + 1) * 128],
                                   xnb[:, c, sl], start=(c == 0), stop=(c == DC - 1))
                mmi.then_inc(gmm, 1)
            t.wait_ge(w2, 16)
            t.wait_ge(ms, 2 + DC * NTB)
            for i in range(DC * NTB):
                o, tb = divmod(i, NTB)
                sl = slice(tb * 512, (tb + 1) * 512)
                t.wait_ge(oev, max(0, i - 1))
                for c in range(DC):
                    mmi = t.matmul(pg[:, i % 2, :], wt[:, c, o * 128:(o + 1) * 128],
                                   at[:, c, sl], start=(c == 0), stop=(c == DC - 1))
                mmi.then_inc(omm, 1)

    cx.close()
    return nc



def emit_mx_quant_row(v, ch, out_bf, t_f32, scr, rs, ts2, nb):
    P, Fd = t_f32.shape[0], t_f32.shape[-1]
    tv = t_f32.rearrange("p (b w) -> p b w", w=32)
    ch(v.tensor_reduce(rs, tv, axis=AX.X, op=ALU.max, apply_absolute_value=True))
    ch(v.tensor_scalar(rs, rs, 1.0 / 448.0, 1e-30, op0=ALU.mult, op1=ALU.max))
    ch(v.tensor_scalar(ts2, rs, 2.0, None, op0=ALU.mult))
    ch(v.reciprocal(rs, rs))
    ch(v.tensor_scalar(rs, rs, 0.5, None, op0=ALU.mult))
    rsb = rs.unsqueeze(2).broadcast_to([P, nb, 32])
    tsb = ts2.unsqueeze(2).broadcast_to([P, nb, 32])
    scrv = scr.rearrange("p (b w) -> p b w", w=32)
    outv = out_bf.rearrange("p (b w) -> p b w", w=32)
    ch(v.tensor_tensor(scrv, tv, rsb, op=ALU.mult))
    ch(v.tensor_tensor(outv, scrv, tsb, op=ALU.mult))


def emit_i8_scale(v, ch, sc_out, rc_out, amax_col, s0, s1):
    ch(v.tensor_scalar(s0, amax_col, 0.0, None, op0=ALU.is_gt))
    ch(v.tensor_scalar(sc_out, amax_col, 1.0 / 127.0, None, op0=ALU.mult))
    ch(v.tensor_tensor(sc_out, sc_out, s0, op=ALU.mult))
    ch(v.tensor_scalar(s1, s0, -1.0, 1.0, op0=ALU.mult, op1=ALU.add))
    ch(v.tensor_tensor(sc_out, sc_out, s1, op=ALU.add))
    ch(v.reciprocal(rc_out, sc_out))


def build_prep(trans=False, mega=None):
    nc = mega["nc"] if mega else bass.Bass("TRN2", target_bir_lowering=False)
    GS = _io(mega, nc, "gsl", [1024, 2048], F32)
    US = _io(mega, nc, "usl", [1024, 2048], F32)
    DS = _io(mega, nc, "dsl", [256, 8192], F32)
    KS = _io(mega, nc, "ksl", [256, 2048], F32)
    B2 = _io(mega, nc, "bias2", [1, 2048], F32)
    RP = _io(mega, nc, "relp", [1023, 64], F32)
    if not trans:
        GQ = _io(mega, nc, "gq", [1024, 2048], BF16, out=True)
        UQ = _io(mega, nc, "uq", [1024, 2048], BF16, out=True)
        DQ = _io(mega, nc, "dq", [256, 8192], BF16, out=True)
        KQ = _io(mega, nc, "kq", [256, 2048], BF16, out=True)
    else:
        GQ = UQ = DQ = KQ = None
        WGUc = _io(mega, nc, "wguc", [2, DC, 128, 1024], BF16, out=True)
        WDQc = _io(mega, nc, "wdqc", [64, 128, 256], BF16, out=True)
        KQWc = _io(mega, nc, "kqwc", [DC, 128, 256], BF16, out=True)
        IDP = _io(mega, nc, "idp", [128, 128], BF16)
    BQ = _io(mega, nc, "bqv", [1, 2048], F32, out=True)
    BV = _io(mega, nc, "bv8", [1023, 1], F32, out=True)
    KSC = _io(mega, nc, "ksc", [2, 128], F32, out=True)

    cx = Ctx(nc, pfx=mega["pfx"] if mega else None)
    tin = cx.sb("tin", [128, 2, 8192], F32)
    tout = cx.sb("tout", [128, 2, 8192], BF16)
    scr8 = cx.sb("scr8", [128, 8192], FP8)
    rs = cx.sb("rs", [128, 256], F32)
    ts2 = cx.sb("ts2", [128, 256], F32)
    rp = cx.sb("rp", [128, 8, 64], F32)
    bcol = cx.sb("bcol", [128, 8], F32)
    brow = cx.sb("brow", [1, 2048], F32)
    bqrow = cx.sb("bqrow", [1, 2048], F32)
    bs = cx.sb("bs", [1, 8], F32)
    ld, q8, st, vch, mz, tp, te, st2 = cx.sems(
        "ld", "q8", "st", "vch", "mz", "tp", "te", "st2")
    if trans:
        idp = cx.sb("idp_s", [128, 128], BF16)
        ttr = cx.sb("ttr_s", [128, 2, 512], BF16)
        pt0 = cx.ps("pt0_s", [128, 512], BF16)
        pt1 = cx.ps("pt1_s", [128, 512], BF16)
        PT = [pt0, pt1]
    blk = cx.block()

    jobs = []
    for i in range(8):
        jobs.append((GS[i * 128:(i + 1) * 128, :],
                     None if trans else GQ[i * 128:(i + 1) * 128, :], 2048, "mx"))
    for i in range(8):
        jobs.append((US[i * 128:(i + 1) * 128, :],
                     None if trans else UQ[i * 128:(i + 1) * 128, :], 2048, "mx"))
    for i in range(2):
        jobs.append((DS[i * 128:(i + 1) * 128, :],
                     None if trans else DQ[i * 128:(i + 1) * 128, :], 8192, "mx"))
    for i in range(2):
        jobs.append((KS[i * 128:(i + 1) * 128, :],
                     None if trans else KQ[i * 128:(i + 1) * 128, :], 2048, "i8"))
    ntile = [c // 128 for (_, _, c, _) in jobs]
    cgend = []
    _t = 0
    for n in ntile:
        _t += n // 4
        cgend.append(_t)

    def ttarget(j, t):
        if j < 8:
            return WGUc[j // 4, t, :, (j % 4) * 128:(j % 4) * 128 + 128]
        if j < 16:
            jj = j - 8
            return WGUc[jj // 4, t, :, 512 + (jj % 4) * 128:512 + (jj % 4) * 128 + 128]
        if j < 18:
            return WDQc[t, :, (j - 16) * 128:(j - 16) * 128 + 128]
        return KQWc[t, :, (j - 18) * 128:(j - 18) * 128 + 128]

    if True:
        @blk.sync
        def _(sync):
            if trans:
                sync.dma_start(out=idp[:], in_=IDP).then_inc(mz, 16)
            sync.dma_start(out=rp[:, 0:7, :], in_=RP[0:896].rearrange("(a p) b -> p a b", p=128)).then_inc(ld, 16)
            sync.wait_ge(mz, 17 if trans else 1)
            sync.dma_start(out=rp[0:127, 7, :], in_=RP[896:1023]).then_inc(ld, 16)
            sync.dma_start(out=brow[:], in_=B2).then_inc(ld, 16)
            for j, (srcap, dstap, cols, kind) in enumerate(jobs):
                sync.wait_ge(ld, 16 * (3 + j))
                if j >= 2:
                    sync.wait_ge(q8, j)
                sync.dma_start(out=tin[:, j % 2, 0:cols], in_=srcap).then_inc(ld, 16)

        @blk.vector
        def _(v):
            nch = [0]

            def ch(instr):
                nch[0] += 1
                instr.then_inc(vch, 1)
                v.wait_ge(vch, nch[0])

            v.wait_ge(ld, 48)
            v.wait_ge(mz, 17 if trans else 1)
            ch(v.tensor_reduce(bcol[:], rp[:], axis=AX.X, op=ALU.add))
            ch(v.tensor_scalar(bcol[:], bcol[:], 0.125, None, op0=ALU.mult))
            ch(v.tensor_reduce(bs[:, 0:1], brow[:], axis=AX.X, op=ALU.max, apply_absolute_value=True))
            emit_i8_scale(v, ch, bs[:, 1:2], bs[:, 2:3], bs[:, 0:1], bs[:, 3:4], bs[:, 4:5])
            ch(v.tensor_scalar(bqrow[:], brow[:], bs[:, 2:3], MAGIC, op0=ALU.mult, op1=ALU.add))
            ch(v.tensor_scalar(bqrow[:], bqrow[:], -MAGIC, 127.0, op0=ALU.add, op1=ALU.min))
            ch(v.tensor_scalar(bqrow[:], bqrow[:], -127.0, None, op0=ALU.max))
            ch(v.tensor_scalar(bqrow[:], bqrow[:], bs[:, 1:2], None, op0=ALU.mult))
            v.sem_inc(q8, 1)
            stcount = {}
            _n = 3
            for _j, (_s, _d, _c, _k) in enumerate(jobs):
                _n += 1
                stcount[_j] = _n
                if _k == "i8":
                    _n += 1
                    stcount[_j] = _n
            for j, (srcap, dstap, cols, kind) in enumerate(jobs):
                v.wait_ge(ld, 16 * (4 + j))
                if j >= 2:
                    if trans:
                        v.wait_ge(tp, cgend[j - 2])
                    else:
                        v.wait_ge(st, 16 * stcount[j - 2])
                nb = cols // 32
                if kind == "mx":
                    emit_mx_quant_row(v, ch, tout[:, j % 2, 0:cols], tin[:, j % 2, 0:cols],
                                      scr8[:, 0:cols], rs[:, 0:nb], ts2[:, 0:nb], nb)
                else:
                    ch(v.tensor_reduce(rs[:, 0:1], tin[:, j % 2, 0:cols], axis=AX.X, op=ALU.max, apply_absolute_value=True))
                    emit_i8_scale(v, ch, rs[:, 1:2], rs[:, 2:3], rs[:, 0:1], rs[:, 3:4], rs[:, 4:5])
                    ch(v.tensor_scalar(tin[:, j % 2, 0:cols], tin[:, j % 2, 0:cols], rs[:, 2:3], MAGIC, op0=ALU.mult, op1=ALU.add))
                    ch(v.tensor_scalar(tin[:, j % 2, 0:cols], tin[:, j % 2, 0:cols], -MAGIC, 127.0, op0=ALU.add, op1=ALU.min))
                    ch(v.tensor_scalar(tout[:, j % 2, 0:cols], tin[:, j % 2, 0:cols], -127.0, None, op0=ALU.max))
                    ch(v.tensor_copy(rs[:, 6 + (j % 2):7 + (j % 2)], rs[:, 1:2]))
                v.sem_inc(q8, 1)
                if trans and j >= 1:
                    for gj in range(ntile[j - 1] // 4):
                        gg = cgend[j - 2] + gj if j >= 2 else gj
                        v.wait_ge(tp, gg + 1)
                        if gg >= 2:
                            v.wait_ge(st2, 16 * 4 * (gg - 1))
                        ch(v.tensor_copy(ttr[:, gg % 2, :], PT[gg % 2][:]))
                        v.sem_inc(te, 1)
            if trans:
                j = len(jobs) - 1
                for gj in range(ntile[j] // 4):
                    gg = cgend[j - 1] + gj
                    v.wait_ge(tp, gg + 1)
                    if gg >= 2:
                        v.wait_ge(st2, 16 * 4 * (gg - 1))
                    ch(v.tensor_copy(ttr[:, gg % 2, :], PT[gg % 2][:]))
                    v.sem_inc(te, 1)

        @blk.gpsimd
        def _(g):
            g.memset(rp[:, 7, :], 0.0).then_inc(mz, 1)
            g.wait_ge(q8, 1)
            g.dma_start(out=BV[0:896].rearrange("(a p) o -> p (a o)", p=128), in_=bcol[:, 0:7], allow_slow_non_contiguous=True).then_inc(st, 16)
            g.wait_ge(st, 16)
            g.dma_start(out=BV[896:1023], in_=bcol[0:127, 7:8]).then_inc(st, 16)
            g.wait_ge(st, 32)
            g.dma_start(out=BQ, in_=bqrow[:]).then_inc(st, 16)
            g.wait_ge(st, 48)
            nst = 3
            if not trans:
                for j, (srcap, dstap, cols, kind) in enumerate(jobs):
                    g.wait_ge(q8, j + 2)
                    g.dma_start(out=dstap, in_=tout[:, j % 2, 0:cols]).then_inc(st, 16)
                    nst += 1
                    g.wait_ge(st, 16 * nst)
                    if kind == "i8":
                        ki = j - 18
                        g.dma_start(out=KSC[ki:ki + 1, :],
                                    in_=rs[:, 6 + (j % 2):7 + (j % 2)], allow_slow_non_contiguous=True).then_inc(st, 16)
                        nst += 1
                        g.wait_ge(st, 16 * nst)
            else:
                gg = 0
                for j, (srcap, dstap, cols, kind) in enumerate(jobs):
                    for gj in range(ntile[j] // 4):
                        g.wait_ge(te, gg + 1)
                        for s_ in range(4):
                            t = gj * 4 + s_
                            g.dma_start(out=ttarget(j, t),
                                        in_=ttr[:, gg % 2, s_ * 128:(s_ + 1) * 128]).then_inc(st2, 16)
                        gg += 1
                        g.wait_ge(st2, 16 * 4 * gg)
                    if kind == "i8":
                        ki = j - 18
                        g.wait_ge(q8, j + 2)
                        g.dma_start(out=KSC[ki:ki + 1, :],
                                    in_=rs[:, 6 + (j % 2):7 + (j % 2)], allow_slow_non_contiguous=True).then_inc(st, 16)
                        nst += 1
                        g.wait_ge(st, 16 * nst)

        if trans:
            @blk.tensor
            def _(t_):
                t_.wait_ge(mz, 17)
                gg = 0
                for j in range(len(jobs)):
                    t_.wait_ge(q8, j + 2)
                    for gj in range(ntile[j] // 4):
                        if gg >= 2:
                            t_.wait_ge(te, gg - 1)
                        for s_ in range(4):
                            t = gj * 4 + s_
                            mmi = t_.transpose(PT[gg % 2][:, s_ * 128:(s_ + 1) * 128],
                                               tout[:, j % 2, t * 128:(t + 1) * 128], idp[:])
                        mmi.then_inc(tp, 1)
                        gg += 1

    cx.close()
    return nc



QF_F = 41
QW = 4992


def build_qscat(mega=None):
    nc = mega["nc"] if mega else bass.Bass("TRN2", target_bir_lowering=False)
    BV = _io(mega, nc, "bv8", [1023, 1], F32)
    SCL = _io(mega, nc, "scl", [10, 128, 128], BF16)
    SCM = _io(mega, nc, "scm", [10, 128, QF_F], BF16)
    QDo = _io(mega, nc, "qdv", [1, 128 * QF_F], BF16, out=True)

    cx = Ctx(nc, pfx=mega["pfx"] if mega else None)
    bvc = cx.sb("bvc_q", [128, 10], F32)
    scl = cx.sb("scl_q", [128, 10, 128], BF16)
    scm = cx.sb("scm_q", [128, 10, QF_F], BF16)
    rhs = cx.sb("rhs_q", [128, 10, QF_F], BF16)
    qsb = cx.sb("qsb_q", [128, QF_F], BF16)
    onr = cx.sb("onr_q", [1, 128], F32)
    bvx = cx.sb("bvx_q", [1, 1], F32)
    psq = cx.ps("psq_q", [128, QF_F], F32)
    psb = cx.ps("psb_q", [128, 2], F32)
    ld, rh, mmd, bcd, ev, vch, msq = cx.sems("ld", "rh", "mmd", "bcd", "ev", "vch", "msq")
    blk = cx.block()

    if True:
        @blk.gpsimd
        def _(g):
            g.memset(bvc[:, 7:8], 0.0).then_inc(msq, 1)
            g.memset(onr[:], 1.0).then_inc(msq, 1)
            g.dma_start(out=bvc[:, 0:7],
                        in_=BV[0:896].rearrange("(a p) o -> p (a o)", p=128),
                        allow_slow_non_contiguous=True).then_inc(ld, 16)
            g.wait_ge(msq, 1)
            g.dma_start(out=bvc[0:127, 7:8], in_=BV[896:1023],
                        allow_slow_non_contiguous=True).then_inc(ld, 16)
            g.dma_start(out=bvx[:], in_=BV[1022:1023, :]).then_inc(ld, 16)
            g.wait_ge(ld, 48)
            g.dma_start(out=scl[:], in_=SCL.rearrange("a p b -> p a b")).then_inc(ld, 16)
            g.wait_ge(ld, 64)
            g.dma_start(out=scm[:], in_=SCM.rearrange("a p b -> p a b")).then_inc(ld, 16)
            g.wait_ge(ev, 1)
            g.dma_start(out=QDo.rearrange("o (p f) -> (o p) f", p=128), in_=qsb[:]).then_inc(ld, 16)
            g.wait_ge(ld, 96)

        @blk.tensor
        def _(t):
            t.wait_ge(ld, 48)
            t.wait_ge(msq, 2)
            t.matmul(psb[:, 0:1], onr[:], bvc[0:1, 0:1], start=True, stop=True)
            t.matmul(psb[:, 1:2], onr[:], bvx[0:1, 0:1],
                     start=True, stop=True).then_inc(bcd, 1)
            t.wait_ge(rh, 10)
            for mc in range(10):
                mmi = t.matmul(psq[:], scl[:, mc, :], rhs[:, mc, :],
                               start=(mc == 0), stop=(mc == 9))
            mmi.then_inc(mmd, 1)

        @blk.vector
        def _(v):
            nch = [0]

            def ch(instr):
                nch[0] += 1
                instr.then_inc(vch, 1)
                v.wait_ge(vch, nch[0])

            v.wait_ge(ld, 80)
            v.wait_ge(bcd, 1)
            ch(v.tensor_copy(bvc[:, 8:9], psb[:, 0:1]))
            ch(v.tensor_copy(bvc[:, 9:10], psb[:, 1:2]))
            for mc in range(10):
                ch(v.tensor_scalar(rhs[:, mc, :], scm[:, mc, :], bvc[:, mc:mc + 1],
                                   None, op0=ALU.mult))
                v.sem_inc(rh, 1)
            v.wait_ge(mmd, 1)
            ch(v.tensor_copy(qsb[:], psq[:]))
            v.sem_inc(ev, 1)

    cx.close()
    return nc


def qscat_host(core):
    F = QF_F
    A = 4606 - (core % 4) * 1024
    scl = np.zeros((10, 128, 128), np.float32)
    scm = np.zeros((10, 128, F), np.float32)
    for m in range(1023):
        n = A - m
        if 0 <= n < 128 * F:
            p, f = divmod(n, F)
            scl[m // 128][m % 128, p] = 1.0
            scm[m // 128][m % 128, f] = 1.0
    for p in range(128):
        f0s = [f for f in range(F) if p * F + f > A]
        if f0s:
            scl[8][p, p] = 1.0
            for f in f0s:
                scm[8][p, f] = 1.0
        f9s = [f for f in range(F) if p * F + f < A - 1022]
        if f9s:
            scl[9][p, p] = 1.0
            for f in f9s:
                scm[9][p, f] = 1.0
    return scl.astype(bf), scm.astype(bf)


def build_l0_ffn(T=TPC, mega=None):
    nc = mega["nc"] if mega else bass.Bass("TRN2", target_bir_lowering=False)
    XNd = _io(mega, nc, "xn", [DC, 128, T], F32)
    IDF = _io(mega, nc, "idf", [128, 128], F32)
    IDB = _io(mega, nc, "idb", [128, 128], BF16)
    WGU = _io(mega, nc, "wgu", [16, DC, 128, 1024], BF16)
    WDQ = _io(mega, nc, "wdq", [64, 128, 2048], BF16)
    G0d = _io(mega, nc, "g0", [DC, 128, T], F32, out=True)
    NTS = T // 512
    NTT = T // 128
    NU = 64

    cx = Ctx(nc, pfx=mega["pfx"] if mega else None)
    xnf = cx.sb("xnf", [128, DC, T], F32)
    hq_fm = xnf[:].bitcast(BF16).rearrange("p c t -> p (c t)").rearrange("p (ic t) -> p ic t", t=512)
    xq = cx.sb("xq", [128, DC, T], BF16)
    wgu = cx.sb("wgu_s", [128, 2, DC, 1024], BF16)
    wd = cx.sb("wd_s", [128, 1, 64, 128], BF16)
    idf = cx.sb("idf_s", [128, 128], F32)
    idb = cx.sb("idb_s", [128, 128], BF16)
    xtm = cx.sb("xtm", [128, DC, 128], F32)
    xtq = cx.sb("xtq", [128, DC, 128], BF16)
    hst = cx.sb("hst", [128, 2, 512], F32)
    htm = cx.sb("htm", [128, 512], F32)
    hqt = cx.sb("hqt", [128, 2, 512], BF16)
    scr8 = cx.sb("scr8", [128, 2048], FP8)
    rs = cx.sb("rs", [128, 64], F32)
    ts2 = cx.sb("ts2", [128, 64], F32)
    so = cx.sb("so", [128, 512], F32)

    ptf0 = cx.ps("ptf0", [128, 4, 128], F32)
    ptb0 = cx.ps("ptb0", [128, 4, 128], BF16)
    ptb1 = cx.ps("ptb1", [128, 4, 128], BF16)
    pgu = cx.ps("pgu", [128, 2, 2, 512], F32)
    pdn = cx.ps("pdn", [128, 512], F32)
    PTF = [ptf0, ptf0]
    PTB = [ptb0, ptb1]

    (xl, il, wl, wdl, stm, fpt, fev, xqs, bpt, bev, gumm, hsil, huev, hqs,
     t2p, t2e, dnmm, dnev, vch) = cx.sems(
        "xl", "il", "wl", "wdl", "stm", "fpt", "fev", "xqs", "bpt", "bev",
        "gumm", "hsil", "huev", "hqs", "t2p", "t2e", "dnmm", "dnev", "vch")
    blk = cx.block()

    if True:
        @blk.sync
        def _(sync):
            sync.dma_start(out=idf[:], in_=IDF).then_inc(il, 16)
            sync.dma_start(out=idb[:], in_=IDB).then_inc(il, 16)
            sync.dma_start(out=xnf[:], in_=XNd.rearrange("c p t -> p c t")).then_inc(xl, 16)
            for ts in range(NTS):
                for j in range(16):
                    li = ts * 16 + j
                    sync.wait_ge(wl, 16 * li)
                    if li >= 2:
                        sync.wait_ge(gumm, 4 * (li - 1))
                    sync.dma_start(out=wgu[:, li % 2, :, :], in_=WGU[j].rearrange("c p m -> p c m")).then_inc(wl, 16)

        @blk.gpsimd
        def _(g):
            for li in range(NTS * DC):
                ts, ot = divmod(li, DC)
                g.wait_ge(wdl, 16 * li)
                if li >= 1:
                    g.wait_ge(dnmm, li)
                g.dma_start(out=wd[:, 0, :, :],
                            in_=WDQ[:, :, ot * 128:(ot + 1) * 128].rearrange("i p o -> p i o")).then_inc(wdl, 16)
                if li >= 1:
                    g.wait_ge(dnev, li)
                    g.wait_ge(stm, 16 * (li - 1))
                    pts, pot = divmod(li - 1, DC)
                    g.dma_start(out=G0d[pot, :, pts * 512:(pts + 1) * 512], in_=so[:]).then_inc(stm, 16)
            li = NTS * DC
            g.wait_ge(dnev, li)
            g.wait_ge(stm, 16 * (li - 1))
            g.dma_start(out=G0d[DC - 1, :, (NTS - 1) * 512:NTS * 512], in_=so[:]).then_inc(stm, 16)
            g.wait_ge(stm, 16 * li)

        @blk.tensor
        def _(t):
            t.wait_ge(xl, 16)
            t.wait_ge(il, 32)
            for tt in range(NTT):
                for gq in range(4):
                    gi = tt * 4 + gq
                    if gi >= 1:
                        t.wait_ge(fev, gi)
                    for jj in range(4):
                        c = gq * 4 + jj
                        mmi = t.transpose(PTF[gi % 2][:, jj, :], xnf[:, c, tt * 128:(tt + 1) * 128], idf[:])
                    mmi.then_inc(fpt, 1)
                for gq in range(4):
                    gi = tt * 4 + gq
                    t.wait_ge(xqs, tt + 1)
                    if gi >= 2:
                        t.wait_ge(bev, gi - 1)
                    for jj in range(4):
                        c = gq * 4 + jj
                        mmi = t.transpose(PTB[gi % 2][:, jj, :], xtq[:, c, :], idb[:])
                    mmi.then_inc(bpt, 1)
            for ts in range(NTS):
                for u in range(NU):
                    Gg = ts * NU + u
                    j, tsub = divmod(u, 4)
                    li = ts * 16 + j
                    t.wait_ge(wl, 16 * (li + 1))
                    if ts == 0 and u == 0:
                        t.wait_ge(bev, NTT * 4)
                    if Gg >= 2:
                        t.wait_ge(huev, Gg - 1)
                    tsl = slice(ts * 512 + tsub * 128, ts * 512 + tsub * 128 + 128)
                    for gu in range(2):
                        for hc in range(DC):
                            mmi = t.matmul(pgu[:, Gg % 2, gu, :], xq[:, hc, tsl],
                                           wgu[:, li % 2, hc, gu * 512:(gu + 1) * 512],
                                           start=(hc == 0), stop=(hc == DC - 1))
                    mmi.then_inc(gumm, 1)
                    w = Gg - 2
                    if w >= ts * NU:
                        t.wait_ge(hqs, w + 1)
                        if w >= 2:
                            t.wait_ge(t2e, w - 1)
                        for jj in range(4):
                            mmi = t.transpose(PTB[w % 2][:, jj, :], hqt[:, w % 2, jj * 128:(jj + 1) * 128], idb[:])
                        mmi.then_inc(t2p, 1)
                for w in range(ts * NU + NU - 2, ts * NU + NU):
                    t.wait_ge(hqs, w + 1)
                    t.wait_ge(t2e, w - 1)
                    for jj in range(4):
                        mmi = t.transpose(PTB[w % 2][:, jj, :], hqt[:, w % 2, jj * 128:(jj + 1) * 128], idb[:])
                    mmi.then_inc(t2p, 1)
                t.wait_ge(t2e, (ts + 1) * NU)
                for ot in range(DC):
                    li = ts * DC + ot
                    t.wait_ge(wdl, 16 * (li + 1))
                    t.wait_ge(dnev, li)
                    for ic in range(64):
                        mmi = t.matmul(pdn[:], wd[:, 0, ic, :], hq_fm[:, ic, :],
                                       start=(ic == 0), stop=(ic == 63))
                    mmi.then_inc(dnmm, 1)

        @blk.scalar
        def _(sc):
            for ts in range(NTS):
                for u in range(NU):
                    Gg = ts * NU + u
                    sc.wait_ge(gumm, Gg + 1)
                    if Gg >= 2:
                        sc.wait_ge(huev, Gg - 1)
                    sc.activation(out=hst[:, Gg % 2, :], in_=pgu[:, Gg % 2, 0, :], func=AF.Silu).then_inc(hsil, 1)

        @blk.vector
        def _(v):
            nch = [0]

            def ch(instr):
                nch[0] += 1
                instr.then_inc(vch, 1)
                v.wait_ge(vch, nch[0])

            for tt in range(NTT):
                for gq in range(4):
                    gi = tt * 4 + gq
                    v.wait_ge(fpt, gi + 1)
                    ch(v.tensor_copy(xtm[:, gq * 4:(gq + 1) * 4, :], PTF[gi % 2][:]))
                    v.sem_inc(fev, 1)
                emit_mx_quant_row(v, ch, xtq[:].rearrange("p c t -> p (c t)"),
                                  xtm[:].rearrange("p c t -> p (c t)"),
                                  scr8[:], rs[:], ts2[:], 64)
                v.sem_inc(xqs, 1)
                for gq in range(4):
                    gi = tt * 4 + gq
                    v.wait_ge(bpt, gi + 1)
                    ch(v.tensor_copy(xq[:, gq * 4:(gq + 1) * 4, tt * 128:(tt + 1) * 128], PTB[gi % 2][:]))
                    v.sem_inc(bev, 1)
            for ts in range(NTS):
                for u in range(NU):
                    Gg = ts * NU + u
                    j, tsub = divmod(u, 4)
                    v.wait_ge(hsil, Gg + 1)
                    ch(v.tensor_tensor(htm[:], hst[:, Gg % 2, :], pgu[:, Gg % 2, 1, :], op=ALU.mult))
                    v.sem_inc(huev, 1)
                    if Gg >= 2:
                        v.wait_ge(t2p, Gg - 1)
                    emit_mx_quant_row(v, ch, hqt[:, Gg % 2, :], htm[:], scr8[:, 0:512],
                                      rs[:, 0:16], ts2[:, 0:16], 16)
                    v.sem_inc(hqs, 1)
                    w = Gg - 2
                    if w >= ts * NU:
                        v.wait_ge(t2p, w + 1)
                        wts, wu = divmod(w, NU)
                        wj, wtsub = divmod(wu, 4)
                        ch(v.tensor_copy(hq_fm[:, wj * 4:(wj + 1) * 4, wtsub * 128:(wtsub + 1) * 128], PTB[w % 2][:]))
                        v.sem_inc(t2e, 1)
                for w in range(ts * NU + NU - 2, ts * NU + NU):
                    v.wait_ge(t2p, w + 1)
                    wts, wu = divmod(w, NU)
                    wj, wtsub = divmod(wu, 4)
                    ch(v.tensor_copy(hq_fm[:, wj * 4:(wj + 1) * 4, wtsub * 128:(wtsub + 1) * 128], PTB[w % 2][:]))
                    v.sem_inc(t2e, 1)
                for ot in range(DC):
                    li = ts * DC + ot
                    v.wait_ge(dnmm, li + 1)
                    v.wait_ge(stm, 16 * li)
                    ch(v.tensor_copy(so[:], pdn[:]))
                    v.sem_inc(dnev, 1)

    cx.close()
    return nc



def build_l1_proj(T=TPC, mega=None):
    nc = mega["nc"] if mega else bass.Bass("TRN2", target_bir_lowering=False)
    XNd = _io(mega, nc, "xn", [DC, 128, T], F32)
    WQ = _io(mega, nc, "wq", [DC, 128, D], F32)
    WK = _io(mega, nc, "wk", [DC, 128, D], F32)
    WV = _io(mega, nc, "wv", [DC, 128, D], F32)
    QF = _io(mega, nc, "qf", [DC, 128, T], BF16, out=True)
    KF = _io(mega, nc, "kf", [DC, 128, T], BF16, out=True)
    VT = _io(mega, nc, "vt", [T, D], BF16, out=True)
    NTB = T // 512

    cx = Ctx(nc, pfx=mega["pfx"] if mega else None)
    xnb = cx.sb("xnb", [128, DC, T], BF16)
    wt = cx.sb("wt", [128, 2, DC, D], BF16)
    ost = cx.sb("ost", [128, 4, 512], BF16)
    pp = cx.ps("pp", [128, 2, 512], F32)
    wl, xld, mm, ev, stm = cx.sems("wl", "xld", "mm", "ev", "stm")
    blk = cx.block()

    if True:
        @blk.gpsimd
        def _(g):
            g.dma_start(out=xnb[:], in_=XNd.rearrange("c p t -> p c t")).then_inc(xld, 16)
            for wi, W in enumerate([WQ, WK, WV]):
                g.wait_ge(wl, 16 * wi)
                if wi >= 2:
                    g.wait_ge(mm, 32 * (wi - 1))
                g.dma_start(out=wt[:, wi % 2, :, :], in_=W.rearrange("c p m -> p c m")).then_inc(wl, 16)

        @blk.tensor
        def _(t):
            t.wait_ge(xld, 16)
            k = 0
            for wi in range(3):
                t.wait_ge(wl, 16 * (wi + 1))
                if wi < 2:
                    for ot in range(DC):
                        for tb in range(NTB):
                            t.wait_ge(ev, max(0, k - 1))
                            for c in range(DC):
                                mmi = t.matmul(pp[:, k % 2, :], wt[:, wi % 2, c, ot * 128:(ot + 1) * 128],
                                               xnb[:, c, tb * 512:(tb + 1) * 512], start=(c == 0), stop=(c == DC - 1))
                            mmi.then_inc(mm, 1)
                            k += 1
                else:
                    for tt in range(T // 128):
                        for vb in range(4):
                            t.wait_ge(ev, max(0, k - 1))
                            for c in range(DC):
                                mmi = t.matmul(pp[:, k % 2, :], xnb[:, c, tt * 128:(tt + 1) * 128],
                                               wt[:, wi % 2, c, vb * 512:(vb + 1) * 512], start=(c == 0), stop=(c == DC - 1))
                            mmi.then_inc(mm, 1)
                            k += 1

        @blk.vector
        def _(v):
            for k in range(32 * 3):
                v.wait_ge(mm, k + 1)
                v.wait_ge(stm, 16 * max(0, k - 3))
                v.tensor_copy(ost[:, k % 4, :], pp[:, k % 2, :]).then_inc(ev, 1)

        @blk.scalar
        def _(sc):
            for k in range(32 * 3):
                wi, r = divmod(k, 32)
                sc.wait_ge(ev, k + 1)
                sc.wait_ge(stm, 16 * k)
                if wi < 2:
                    ot, tb = divmod(r, NTB)
                    dst = (QF if wi == 0 else KF)[ot, :, tb * 512:(tb + 1) * 512]
                else:
                    tt, vb = divmod(r, 4)
                    dst = VT[tt * 128:(tt + 1) * 128, vb * 512:(vb + 1) * 512]
                sc.dma_start(out=dst, in_=ost[:, k % 4, :]).then_inc(stm, 16)
            sc.wait_ge(stm, 16 * 96)

    cx.close()
    return nc


L1_H = 32


def build_l1_attn(T=TPC, mega=None):
    nc = mega["nc"] if mega else bass.Bass("TRN2", target_bir_lowering=False)
    QFd = _io(mega, nc, "qf", [L1_H, 64, T], BF16)
    KFd = _io(mega, nc, "kf", [DC, 128, S], BF16)
    VTe = _io(mega, nc, "vte", [S, L1_H, 64], BF16)
    QDv = _io(mega, nc, "qdv", [1, 128 * QF_F], BF16)
    IDB = _io(mega, nc, "idb", [128, 128], BF16)
    ON5 = _io(mega, nc, "ones512", [1, 512], BF16)
    WO = _io(mega, nc, "wo", [DC, 128, D], F32)
    S1d = _io(mega, nc, "s1", [DC, 128, T], F32, out=True)
    NTB = T // 512
    NKT = S // 128
    NPAIR = NKT // 2
    H = L1_H

    cx = Ctx(nc, pfx=mega["pfx"] if mega else None)
    qbuf = cx.sb("qbuf_s", [72, 2, T], BF16)
    khe = cx.sb("khe_s", [72, 2, NKT, 128], BF16)
    vth = cx.sb("vth_s", [128, 2, NKT, 65], BF16)
    rwin = cx.sb("rwin_s", [128, QW], BF16)
    idb = cx.sb("idb_s", [128, 128], BF16)
    on5 = cx.sb("on5_s", [1, 512], BF16)
    ones_b = cx.sb("ones_b", [64, 1], BF16)
    ones_f = cx.sb("ones_f", [1, 64], F32)
    sqq = cx.sb("sqq_s", [64, S], BF16)
    nrm = cx.sb("nrm_s", [1, 2, 512], F32)
    q2row = cx.sb("q2row_s", [1, 2, 512], F32)
    csr = cx.sb("csr_s", [1, 2, 512], F32)
    macc = cx.sb("macc_s", [1, 16], F32)
    nmrow = cx.sb("nmrow_s", [1, 2, 512], BF16)
    rrow = cx.sb("rrow_s", [1, 512], F32)
    exps = cx.sb("exps_s", [128, 4, 2, 512], BF16)
    ctx = cx.sb("ctx_s", [128, DC, T], BF16)
    wt = cx.sb("wt_s", [128, 2, DC, 128], BF16)
    so = cx.sb("so_s", [128, 512], F32)

    psA = cx.ps("psA", [128, 2, 2, 512], F32)
    pav = cx.ps("pav", [65, 512], F32)
    pnr = cx.ps("pnr", [1, 2, 512], F32)
    pbc = cx.ps("pbc", [64, 512], F32)
    pso = psA[:, 0, 0, :]

    (bl, il, qld, khl, vhl, wol, sqs, nmm, nev, mrd, smm, sex, avp, avdn,
     nrz, pbd, ctxe, omm, oev, stm, vch, nl1, nl2, mrw) = cx.sems(
        "bl", "il", "qld", "khl", "vhl", "wol", "sqs", "nmm", "nev", "mrd",
        "smm", "sex", "avp", "avdn", "nrz", "pbd", "ctxe", "omm", "oev", "stm", "vch",
        "nl1", "nl2", "mrw")
    blk = cx.block()

    if True:
        @blk.sync
        def _(sync):
            sync.dma_start(out=rwin[:], in_=bass.AP(QDv.tensor, 0, [[1, 128], [1, QW]])).then_inc(bl, 16)
            sync.dma_start(out=idb[:], in_=IDB).then_inc(il, 16)
            sync.dma_start(out=on5[:], in_=ON5).then_inc(il, 16)
            for h in range(H):
                hc, hp = h // 2, (h % 2) * 64
                sync.wait_ge(khl, 16 * h)
                if h >= 2:
                    sync.wait_ge(smm, NTB * NPAIR * (h - 1))
                sync.dma_start(out=khe[0:64, h % 2, :, :],
                               in_=KFd[hc, hp:hp + 64, :].rearrange("p (kt j) -> p kt j", j=128)).then_inc(khl, 16)
                sync.wait_ge(qld, 16 * h)
                sync.dma_start(out=qbuf[0:64, h % 2, :], in_=QFd[h]).then_inc(qld, 16)

        @blk.gpsimd
        def _(g):
            g.memset(ones_b[:], 1.0)
            g.memset(ones_f[:], 1.0).then_inc(il, 1)
            g.memset(vth[:, :, :, 64:65], 1.0)
            g.memset(khe[64:65, :, :, :], 1.0)
            for h in range(H):
                g.wait_ge(vhl, 16 * h)
                if h >= 2:
                    g.wait_ge(avp, NPAIR * NTB * (h - 1))
                g.dma_start(out=vth[:, h % 2, :, 0:64],
                            in_=VTe[:, h, :].rearrange("(kt p) e -> p kt e", p=128)).then_inc(vhl, 16)
                g.wait_ge(mrd, h + 1)
                if h >= 2:
                    g.wait_ge(smm, NTB * NPAIR * (h - 1))
                for qb in range(NTB):
                    g.dma_start(out=qbuf[64:65, h % 2, qb * 512:(qb + 1) * 512],
                                in_=nmrow[0:1, qb, :]).then_inc(mrw, 16)
            for ot in range(DC):
                g.wait_ge(wol, 16 * ot)
                if ot >= 1:
                    g.wait_ge(omm, NTB * ot)
                g.dma_start(out=wt[:, 0, :, :],
                            in_=WO[:, :, ot * 128:(ot + 1) * 128].rearrange("c p m -> p c m")).then_inc(wol, 16)
                if ot >= 1:
                    for tb in range(NTB):
                        k = (ot - 1) * NTB + tb
                        g.wait_ge(oev, k + 1)
                        g.wait_ge(stm, 16 * k)
                        g.dma_start(out=S1d[ot - 1, :, tb * 512:(tb + 1) * 512], in_=so[:]).then_inc(stm, 16)
            for tb in range(NTB):
                k = (DC - 1) * NTB + tb
                g.wait_ge(oev, k + 1)
                g.wait_ge(stm, 16 * k)
                g.dma_start(out=S1d[DC - 1, :, tb * 512:(tb + 1) * 512], in_=so[:]).then_inc(stm, 16)
            g.wait_ge(stm, 16 * DC * NTB)

        @blk.scalar
        def _(sc):
            for h in range(H):
                hc, hp = h // 2, (h % 2) * 64
                sc.wait_ge(qld, 16 * (h + 1))
                sc.wait_ge(nmm, 5 * h)
                sc.activation(out=sqq[0:64, 0:T], in_=qbuf[0:64, h % 2, :], func=AF.Square).then_inc(sqs, 1)
                sc.wait_ge(khl, 16 * (h + 1))
                sc.wait_ge(nmm, 5 * h + 1)
                sc.activation(out=sqq[0:64, 0:S], in_=khe[0:64, h % 2, :, :].rearrange("p a b -> p (a b)"),
                              func=AF.Square).then_inc(sqs, 1)
                sc.wait_ge(nl1, h + 1)
                if h >= 1:
                    sc.wait_ge(mrd, h)
                sc.activation(out=macc[:, 3:4], in_=macc[:, 2:3], func=AF.Ln).then_inc(nl2, 1)
                sc.wait_ge(nl2, 6 * h + 1)
                sc.activation(out=macc[:, 3:4], in_=macc[:, 3:4], func=AF.Exp, scale=0.5).then_inc(nl2, 1)
                for qb in range(NTB):
                    sc.activation(out=csr[:, qb, :], in_=q2row[:, qb, :], func=AF.Ln).then_inc(nl2, 1)
                    sc.wait_ge(nl2, 6 * h + 3 + 2 * qb)
                    sc.activation(out=csr[:, qb, :], in_=csr[:, qb, :], func=AF.Exp, scale=0.5).then_inc(nl2, 1)
                for qb in range(NTB):
                    for kp in range(NPAIR):
                        pi = (h * NTB + qb) * NPAIR + kp
                        sc.wait_ge(smm, pi + 1)
                        if pi >= 4:
                            sc.wait_ge(avp, pi - 3)
                        sc.activation(out=exps[:, pi % 4, :, :].rearrange("p a b -> p (a b)"),
                                      in_=psA[:, pi % 2, :, :].rearrange("p a b -> p (a b)"),
                                      func=AF.Exp, scale=0.125).then_inc(sex, 1)

        @blk.tensor
        def _(t):
            t.wait_ge(bl, 16)
            t.wait_ge(il, 33)
            for h in range(H):
                hc, hp = h // 2, (h % 2) * 64
                t.wait_ge(khl, 16 * (h + 1))
                t.wait_ge(qld, 16 * (h + 1))
                t.wait_ge(vhl, 16 * (h + 1))
                t.wait_ge(sqs, 2 * h + 1)
                t.wait_ge(nev, 5 * h)
                for tb in range(2):
                    mmi = t.matmul(pnr[0:1, tb, :], ones_b[:], sqq[0:64, tb * 512:(tb + 1) * 512],
                                   start=True, stop=True)
                mmi.then_inc(nmm, 1)
                t.wait_ge(sqs, 2 * h + 2)
                for rd in range(4):
                    t.wait_ge(nev, 5 * h + rd + 1)
                    for s_ in range(2):
                        mmi = t.matmul(pnr[0:1, s_, :], ones_b[:],
                                       sqq[0:64, (rd * 2 + s_) * 512:(rd * 2 + s_ + 1) * 512],
                                       start=True, stop=True)
                    mmi.then_inc(nmm, 1)
                t.wait_ge(mrw, 32 * (h + 1))
                for qb in range(NTB):
                    base = (h * NTB + qb) * NPAIR

                    def av_pair(kp2):
                        pi2 = base + kp2
                        t.wait_ge(sex, pi2 + 1)
                        for s2 in range(2):
                            kt2 = kp2 * 2 + s2
                            mmi2 = t.matmul(pav[:], vth[:, h % 2, kt2, :], exps[:, pi2 % 4, s2, :],
                                            start=(kt2 == 0), stop=(kt2 == NKT - 1))
                        mmi2.then_inc(avp, 1)

                    t.wait_ge(ctxe, h * NTB + qb)
                    for kp in range(NPAIR):
                        pi = base + kp
                        if pi >= 2:
                            t.wait_ge(sex, pi - 1)
                        for s_ in range(2):
                            kt = kp * 2 + s_
                            t.matmul(psA[:, pi % 2, s_, :], khe[0:65, h % 2, kt, :],
                                     qbuf[0:65, h % 2, qb * 512:(qb + 1) * 512], start=True, stop=False)
                            cw = 3968 - kt * 128 + qb * 512
                            mmi = t.matmul(psA[:, pi % 2, s_, :], idb[:], rwin[:, cw:cw + 512], start=False, stop=True)
                        mmi.then_inc(smm, 1)
                        if kp >= 2:
                            av_pair(kp - 2)
                    av_pair(NPAIR - 2)
                    av_pair(NPAIR - 1)
                    t.wait_ge(nrz, h * NTB + qb + 1)
                    t.matmul(pbc[:], ones_f[:], rrow[:], start=True, stop=True).then_inc(pbd, 1)
            t.wait_ge(wol, 16)
            t.wait_ge(ctxe, H * NTB)
            for k in range(DC * NTB):
                ot, tb = divmod(k, NTB)
                t.wait_ge(wol, 16 * (ot + 1))
                t.wait_ge(oev, k)
                for c in range(DC):
                    mmi = t.matmul(pso, wt[:, 0, c, :], ctx[:, c, tb * 512:(tb + 1) * 512],
                                   start=(c == 0), stop=(c == DC - 1))
                mmi.then_inc(omm, 1)

        @blk.vector
        def _(v):
            nch = [0]

            def ch(instr):
                nch[0] += 1
                instr.then_inc(vch, 1)
                v.wait_ge(vch, nch[0])

            v.wait_ge(bl, 16)
            ch(v.tensor_reduce(macc[:, 0:1], rwin[0:1, :], axis=AX.X, op=ALU.max))
            for h in range(H):
                v.wait_ge(nmm, 5 * h + 1)
                ch(v.tensor_copy(q2row[:], pnr[:]))
                v.sem_inc(nev, 1)
                ch(v.tensor_scalar(macc[:, 2:3], macc[:, 0:1], 0.0, None, op0=ALU.mult))
                for rd in range(4):
                    v.wait_ge(nmm, 5 * h + 2 + rd)
                    ch(v.tensor_copy(nrm[:], pnr[:]))
                    ch(v.tensor_reduce(macc[:, 4:5], nrm[:].rearrange("a b c -> a (b c)"),
                                       axis=AX.X, op=ALU.max))
                    ch(v.tensor_tensor(macc[:, 2:3], macc[:, 2:3], macc[:, 4:5], op=ALU.max))
                    v.sem_inc(nev, 1)
                v.sem_inc(nl1, 1)
                if h >= 1:
                    v.wait_ge(mrw, 32 * h)
                v.wait_ge(nl2, 6 * h + 2)
                ch(v.tensor_scalar(macc[:, 6:7], macc[:, 3:4], 0.35, None, op0=ALU.mult))
                v.wait_ge(nl2, 6 * h + 6)
                for qb in range(NTB):
                    ch(v.tensor_scalar(csr[:, qb, :], csr[:, qb, :], macc[:, 6:7], None, op0=ALU.mult))
                    ch(v.tensor_scalar(csr[:, qb, :], csr[:, qb, :], macc[:, 0:1], None, op0=ALU.add))
                    ch(v.tensor_scalar(nmrow[:, qb, :], csr[:, qb, :], -1.0, None, op0=ALU.mult))
                v.sem_inc(mrd, 1)
                for qb in range(NTB):
                    v.wait_ge(avp, (h * NTB + qb + 1) * NPAIR)
                    ch(v.tensor_copy(rrow[:], pav[64:65, :]))
                    ch(v.reciprocal(rrow[:], rrow[:]))
                    v.sem_inc(nrz, 1)
                    v.wait_ge(pbd, h * NTB + qb + 1)
                    hp = (h % 2) * 64
                    ch(v.tensor_copy(ctx[hp:hp + 64, h // 2, qb * 512:(qb + 1) * 512], pav[0:64, :]))
                    ch(v.tensor_tensor(ctx[hp:hp + 64, h // 2, qb * 512:(qb + 1) * 512],
                                       ctx[hp:hp + 64, h // 2, qb * 512:(qb + 1) * 512], pbc[:], op=ALU.mult))
                    v.sem_inc(ctxe, 1)
            for k in range(DC * NTB):
                v.wait_ge(omm, k + 1)
                v.wait_ge(stm, 16 * k)
                ch(v.tensor_copy(so[:], pso))
                v.sem_inc(oev, 1)

    cx.close()
    return nc



def build_l1_moe(T=TPC, mega=None):
    nc = mega["nc"] if mega else bass.Bass("TRN2", target_bir_lowering=False)
    XNd = _io(mega, nc, "xn", [DC, 128, T], F32)
    CLS = _io(mega, nc, "cls", [DC, 128, 12], F32)
    EBI = _io(mega, nc, "ebias", [1, 12], F32)
    IDF = _io(mega, nc, "idf", [128, 128], F32)
    WGU = _io(mega, nc, "wgu_e", [8, DC, 128, 2048], F32)
    WDE = _io(mega, nc, "wd_e", [8, 8, 128, 2048], F32)
    G1d = _io(mega, nc, "g1", [DC, 128, T], F32, out=True)
    WSC = nc.dram_tensor(("" if mega is None else mega["pfx"]) + "wscr_m", [16, T], F32, kind="Internal").ap()
    NTB = T // 512
    NTT = T // 128

    cx = Ctx(nc, pfx=mega["pfx"] if mega else None)
    wrow = cx.sb("wrow_m", [1, 2, T], F32)
    xnb = cx.sb("xnb_m", [128, DC, T], BF16)
    wgu = cx.sb("wgu_m", [128, DC, 2048], BF16)
    xnf = wgu[:].bitcast(F32)[:, :, 0:1024]
    wd = cx.sb("wd_m", [128, 8, 2048], BF16)
    cls = cx.sb("cls_m", [128, DC, 12], F32)
    idf = cx.sb("idf_m", [128, 128], F32)
    ones_f = cx.sb("ones_fm", [1, 128], F32)
    ebt = cx.sb("ebt_m", [1, 12], F32)
    lt = cx.sb("lt_m", [12, T], F32)
    rt = cx.sb("rt_m", [128, NTT, 12], F32)
    r1 = cx.sb("r1_m", [128, NTT, 12], F32)
    mx1 = cx.sb("mx1_m", [128, NTT, 1], F32)
    wx = cx.sb("wx_m", [128, NTT, 16], F32)
    wfm = cx.sb("wfm_m", [16, NTT, 128], F32)
    sst = cx.sb("sst_m", [128, 512], F32)
    act = cx.sb("act_m", [128, 8, T], BF16)
    sacc = cx.sb("sacc_m", [128, DC, T], BF16)
    so = cx.sb("so_m", [128, 512], F32)

    pgA = cx.ps("pgA", [128, 512], F32)
    pgB = cx.ps("pgB", [128, 512], F32)
    pdn = cx.ps("pdn_m", [128, 2, 512], F32)
    plog = cx.ps("plog", [12, 512], F32)
    ptm_ = cx.ps("ptm_m", [128, 128], F32)
    pwb = cx.ps("pwb", [128, 2, 512], F32)
    prt = ptm_[:, 0:12]
    pwf = ptm_[0:16, 0:128]

    (xld, cld, ebl, ifl, wld, wdld, lmm, ltd, rte, wtp, wte, sfs, sfe,
     wfme, wbd, wbu, gmm, sil, hue, dmm, dacc, dev_, stm, vch, wsc) = cx.sems(
        "xld", "cld", "ebl", "ifl", "wld", "wdld", "lmm", "ltd", "rte", "wtp",
        "wte", "sfs", "sfe", "wfme", "wbd", "wbu", "gmm", "sil", "hue",
        "dmm", "dacc", "dev", "stm", "vch", "wsc")
    blk = cx.block()

    if True:
        @blk.sync
        def _(sync):
            sync.dma_start(out=idf[:], in_=IDF).then_inc(ifl, 16)
            sync.dma_start(out=ebt[:], in_=EBI).then_inc(ebl, 16)
            sync.dma_start(out=cls[:], in_=CLS.rearrange("c p m -> p c m")).then_inc(cld, 16)
            sync.wait_ge(wfme, NTT)
            sync.dma_start(out=WSC, in_=wfm[:, :, :]).then_inc(wsc, 16)
            for e in range(9):
                ei = 8 if e == 8 else e
                row = 12 if e == 8 else e
                sync.wait_ge(wsc, 16 * (e + 1))
                if e >= 2:
                    sync.wait_ge(wbd, 1 + 2 * (e - 1))
                sync.dma_start(out=wrow[:, e % 2, :], in_=WSC[row:row + 1, :]).then_inc(wsc, 16)

        @blk.gpsimd
        def _(g):
            g.memset(ones_f[:], 1.0).then_inc(ifl, 1)
            g.dma_start(out=xnf, in_=XNd.rearrange("c p t -> p c t")).then_inc(xld, 16)
            g.wait_ge(xld, 16)
            g.dma_start(out=xnb[:], in_=XNd.rearrange("c p t -> p c t")).then_inc(xld, 16)
            for e in range(8):
                g.wait_ge(wld, 16 * e)
                if e == 0:
                    g.wait_ge(lmm, NTB)
                if e >= 1:
                    g.wait_ge(gmm, 16 * e)
                g.dma_start(out=wgu[:], in_=WGU[e].rearrange("c p m -> p c m")).then_inc(wld, 16)
                g.wait_ge(wdld, 16 * e)
                if e >= 1:
                    g.wait_ge(dmm, 32 * e)
                g.dma_start(out=wd[:], in_=WDE[e].rearrange("i p o -> p i o")).then_inc(wdld, 16)
            for k in range(DC * NTB):
                ot, tb = divmod(k, NTB)
                g.wait_ge(dev_, k + 1)
                g.wait_ge(stm, 16 * k)
                g.dma_start(out=G1d[ot, :, tb * 512:(tb + 1) * 512], in_=so[:]).then_inc(stm, 16)
            g.wait_ge(stm, 16 * DC * NTB)

        @blk.tensor
        def _(t):
            t.wait_ge(xld, 16)
            t.wait_ge(cld, 16)
            t.wait_ge(ifl, 17)
            for tb in range(NTB):
                t.wait_ge(ltd, tb)
                for c in range(DC):
                    mmi = t.matmul(plog[:], cls[:, c, :], xnf[:, c, tb * 512:(tb + 1) * 512],
                                   start=(c == 0), stop=(c == DC - 1))
                mmi.then_inc(lmm, 1)
            for tt in range(NTT):
                t.wait_ge(ltd, NTB)
                t.wait_ge(rte, tt)
                t.transpose(prt, lt[:, tt * 128:(tt + 1) * 128], idf[0:12, 0:12]).then_inc(wtp, 1)
            t.wait_ge(ebl, 16)
            t.matmul(pwb[:, 0, 0:12], ones_f[:], ebt[:], start=True, stop=True).then_inc(wbd, 1)
            for tt in range(NTT):
                t.wait_ge(wte, tt + 1)
                t.wait_ge(wfme, tt)
                t.transpose(pwf, wx[:, tt, :], idf[:]).then_inc(wtp, 1)
            for e in range(8):
                t.wait_ge(wsc, 16 * (e + 2))
                t.wait_ge(wbu, 16 * e)
                for tb in range(NTB):
                    t.matmul(pwb[:, tb, :], ones_f[:],
                             wrow[:, e % 2, tb * 512:(tb + 1) * 512],
                             start=True, stop=True).then_inc(wbd, 1)
                t.wait_ge(wld, 16 * (e + 1))
                t.wait_ge(xld, 32)
                for it in range(8):
                    for tb in range(NTB):
                        u = (e * 8 + it) * NTB + tb
                        if u >= 1:
                            t.wait_ge(hue, u)
                        for c in range(DC):
                            t.matmul(pgA[:], wgu[:, c, it * 128:(it + 1) * 128],
                                     xnb[:, c, tb * 512:(tb + 1) * 512], start=(c == 0), stop=(c == DC - 1))
                        for c in range(DC):
                            mmi = t.matmul(pgB[:], wgu[:, c, 1024 + it * 128:1024 + (it + 1) * 128],
                                           xnb[:, c, tb * 512:(tb + 1) * 512], start=(c == 0), stop=(c == DC - 1))
                        mmi.then_inc(gmm, 1)
                t.wait_ge(wdld, 16 * (e + 1))
                t.wait_ge(hue, 16 * (e + 1))
                for k in range(DC * NTB):
                    gk = e * 32 + k
                    ot, tb = divmod(k, NTB)
                    if gk >= 2:
                        t.wait_ge(dacc, gk - 1)
                    for ic in range(8):
                        mmi = t.matmul(pdn[:, gk % 2, :], wd[:, ic, ot * 128:(ot + 1) * 128],
                                       act[:, ic, tb * 512:(tb + 1) * 512], start=(ic == 0), stop=(ic == 7))
                    mmi.then_inc(dmm, 1)
            t.wait_ge(wbu, 16 * 8)
            t.wait_ge(wsc, 16 * 10)
            for tb in range(NTB):
                t.matmul(pwb[:, tb, :], ones_f[:],
                         wrow[:, 0, tb * 512:(tb + 1) * 512],
                         start=True, stop=True).then_inc(wbd, 1)

        @blk.scalar
        def _(sc):
            for tb in range(NTB):
                sc.wait_ge(lmm, tb + 1)
                sc.activation(out=lt[:, tb * 512:(tb + 1) * 512], in_=plog[:], func=AF.Identity).then_inc(ltd, 1)
            sc.wait_ge(sfs, 1)
            sc.activation(out=rt[:].rearrange("p a b -> p (a b)"), in_=rt[:].rearrange("p a b -> p (a b)"),
                          func=AF.Exp).then_inc(sfe, 1)
            for u in range(8 * 8 * NTB):
                sc.wait_ge(gmm, u + 1)
                if u >= 1:
                    sc.wait_ge(hue, u)
                sc.activation(out=sst[:], in_=pgA[:], func=AF.Silu).then_inc(sil, 1)

        @blk.vector
        def _(v):
            nch = [0]

            def ch(instr):
                nch[0] += 1
                instr.then_inc(vch, 1)
                v.wait_ge(vch, nch[0])

            for tt in range(NTT):
                v.wait_ge(wtp, tt + 1)
                ch(v.tensor_copy(rt[:, tt, :], prt))
                v.sem_inc(rte, 1)
            ch(v.tensor_reduce(mx1[:], rt[:], axis=AX.X, op=ALU.max))
            ch(v.tensor_tensor(rt[:], rt[:], mx1[:].broadcast_to([128, NTT, 12]), op=ALU.subtract))
            v.sem_inc(sfs, 1)
            v.wait_ge(sfe, 1)
            ch(v.tensor_reduce(mx1[:], rt[:], axis=AX.X, op=ALU.add))
            ch(v.reciprocal(mx1[:], mx1[:]))
            ch(v.tensor_tensor(rt[:], rt[:], mx1[:].broadcast_to([128, NTT, 12]), op=ALU.mult))
            v.wait_ge(wbd, 1)
            ch(v.tensor_tensor(r1[:], rt[:], pwb[:, 0, 0:12].unsqueeze(1).broadcast_to([128, NTT, 12]), op=ALU.add))
            ch(v.tensor_reduce(mx1[:], r1[:], axis=AX.X, op=ALU.max))
            ch(v.tensor_tensor(wx[:, :, 0:12], r1[:], mx1[:].broadcast_to([128, NTT, 12]), op=ALU.is_ge))
            ch(v.scalar_tensor_tensor(r1[:], wx[:, :, 0:12], -1e30, r1[:], op0=ALU.mult, op1=ALU.add))
            ch(v.tensor_reduce(mx1[:], r1[:], axis=AX.X, op=ALU.max))
            ch(v.tensor_tensor(r1[:], r1[:], mx1[:].broadcast_to([128, NTT, 12]), op=ALU.is_ge))
            ch(v.tensor_tensor(wx[:, :, 0:12], wx[:, :, 0:12], r1[:], op=ALU.add))
            ch(v.tensor_tensor(wx[:, :, 0:12], wx[:, :, 0:12], rt[:], op=ALU.mult))
            ch(v.tensor_scalar(wx[:, :, 0:12], wx[:, :, 0:12], 1.5, None, op0=ALU.mult))
            ch(v.tensor_reduce(wx[:, :, 12:13], wx[:, :, 8:12], axis=AX.X, op=ALU.add))
            ch(v.memset(wx[:, :, 13:16], 0.0))
            for tt in range(NTT):
                v.sem_inc(wte, 1)
            for tt in range(NTT):
                v.wait_ge(wtp, NTT + tt + 1)
                ch(v.tensor_copy(wfm[:, tt, :], pwf))
                v.sem_inc(wfme, 1)
            v.memset(sacc[:], 0.0)
            for e in range(8):
                for it in range(8):
                    for tb in range(NTB):
                        u = (e * 8 + it) * NTB + tb
                        v.wait_ge(sil, u + 1)
                        ch(v.tensor_tensor(sst[:], sst[:], pgB[:], op=ALU.mult))
                        v.wait_ge(wbd, 2 + e * NTB + tb)
                        ch(v.tensor_tensor(act[:, it, tb * 512:(tb + 1) * 512], sst[:], pwb[:, tb, :], op=ALU.mult))
                        v.sem_inc(hue, 1)
                        v.sem_inc(wbu, 1)
                for k in range(DC * NTB):
                    gk = e * 32 + k
                    ot, tb = divmod(k, NTB)
                    v.wait_ge(dmm, gk + 1)
                    ch(v.tensor_tensor(sacc[:, ot, tb * 512:(tb + 1) * 512],
                                       sacc[:, ot, tb * 512:(tb + 1) * 512], pdn[:, gk % 2, :], op=ALU.add))
                    v.sem_inc(dacc, 1)
            for k in range(DC * NTB):
                ot, tb = divmod(k, NTB)
                v.wait_ge(wbd, 17 + tb + 1)
                v.wait_ge(stm, 16 * k)
                ch(v.tensor_tensor(so[:], xnb[:, ot, tb * 512:(tb + 1) * 512], pwb[:, tb, :], op=ALU.mult))
                ch(v.tensor_tensor(so[:], so[:], sacc[:, ot, tb * 512:(tb + 1) * 512], op=ALU.add))
                v.sem_inc(dev_, 1)

    cx.close()
    return nc



def build_mla_proj(dA, dB, T=TPC, mega=None):
    nc = mega["nc"] if mega else bass.Bass("TRN2", target_bir_lowering=False)
    cA, cB = dA // 128, dB // 128
    XNd = _io(mega, nc, "xn", [DC, 128, T], F32)
    Wd = {}
    for nm, shp in [("w1", [DC + 1, 128, dA]), ("w2", [DC + 1, 128, dB]),
                    ("w3", [DC + 1, 128, 128]), ("w4", [cA + 1, 128, D]),
                    ("w5", [cA + 1, 128, D]), ("w6", [cB + 1, 128, D]),
                    ("w7", [cB + 1, 128, 1024])]:
        Wd[nm] = _io(mega, nc, nm, shp, F32)
    QF = _io(mega, nc, "qf", [DC, 128, T], BF16, out=True)
    RQR = _io(mega, nc, "rqr", [64, 16, T], BF16, out=True)
    KF = _io(mega, nc, "kf", [DC, 128, T], BF16, out=True)
    RKR = _io(mega, nc, "rkr", [128, T], BF16, out=True)
    VT = _io(mega, nc, "vt", [T, D], BF16, out=True)
    NTB = T // 512

    cx = Ctx(nc, pfx=mega["pfx"] if mega else None)
    xnb = cx.sb("xnb_p", [128, DC + 1, T], BF16)
    wb = cx.sb("wb_p", [128, 2, 26624], BF16)
    Ab = cx.sb("Ab_p", [128, cA + 1, T], BF16)
    Bb = cx.sb("Bb_p", [128, cB + 1, T], BF16)
    ost = cx.sb("ost_p", [128, 4, 512], BF16)
    pp = cx.ps("pp_p", [128, 2, 512], F32)

    xld, wl, ms_, mm, ev, stm = cx.sems("xld", "wl", "ms", "mm", "ev", "stm")
    blk = cx.block()

    jobs = [("w1", dA, "x", "A"), ("w2", dB, "x", "B"), ("w3", 128, "x", "rk"),
            ("w4", D, "A", "kf"), ("w5", D, "A", "vt"), ("w6", D, "B", "qf"),
            ("w7", 1024, "B", "rq")]

    def units(j):
        return (T // 128) * 4 if jobs[j][3] == "vt" else (jobs[j][1] // 128) * NTB

    def mm_before(j):
        return sum(units(i) for i in range(j))

    def store_dsts(j, u):
        nm, ncol, srck, dst = jobs[j]
        if dst == "vt":
            tt, vb = divmod(u, 4)
            return [(VT[tt * 128:(tt + 1) * 128, vb * 512:(vb + 1) * 512], (0, 128))]
        ot, tb = divmod(u, NTB)
        sl = slice(tb * 512, (tb + 1) * 512)
        if dst == "rk":
            return [(RKR[:, sl], (0, 128))]
        if dst == "kf":
            return [(KF[ot, :, sl], (0, 128))]
        if dst == "qf":
            return [(QF[ot, :, sl], (0, 128))]
        if dst == "rq":
            return [(RQR[:, 2 * ot, sl], (0, 64)), (RQR[:, 2 * ot + 1, sl], (64, 128))]
        return None

    stores = []
    done = {}
    k = 0
    for j in range(7):
        for u in range(units(j)):
            if jobs[j][3] not in ("A", "B"):
                for dstap, prng in store_dsts(j, u):
                    stores.append((k, dstap, prng))
                done[k] = len(stores)
            k += 1
    skeys = sorted(done.keys())
    sidx = {kk: (done[skeys[i - 4]] if i >= 4 else 0) for i, kk in enumerate(skeys)}

    if True:
        @blk.gpsimd
        def _(g):
            g.memset(xnb[:, DC, :], 0.0).then_inc(ms_, 1)
            g.memset(Ab[:, cA, :], 0.0).then_inc(ms_, 1)
            g.memset(Bb[:, cB, :], 0.0).then_inc(ms_, 1)
            g.wait_ge(ms_, 3)
            g.memset(xnb[0:1, DC, :], 1.0).then_inc(ms_, 1)
            g.memset(Ab[0:1, cA, :], 1.0).then_inc(ms_, 1)
            g.memset(Bb[0:1, cB, :], 1.0).then_inc(ms_, 1)
            g.dma_start(out=xnb[:, 0:DC, :], in_=XNd.rearrange("c p t -> p c t")).then_inc(xld, 16)
            for j, (nm, ncol, srck, dst) in enumerate(jobs):
                g.wait_ge(wl, 16 * j)
                if j >= 2:
                    g.wait_ge(mm, mm_before(j - 1))
                nsc1 = {"x": DC, "A": cA, "B": cB}[srck] + 1
                g.dma_start(out=wb[:, j % 2, 0:nsc1 * ncol].rearrange("p (c m) -> p c m", m=ncol),
                            in_=Wd[nm].rearrange("c p m -> p c m")).then_inc(wl, 16)

        @blk.sync
        def _(sync):
            for i, (kk, dstap, prng) in enumerate(stores):
                sync.wait_ge(ev, kk + 1)
                sync.wait_ge(stm, 16 * i)
                sync.dma_start(out=dstap, in_=ost[prng[0]:prng[1], kk % 4, :]).then_inc(stm, 16)
            sync.wait_ge(stm, 16 * len(stores))

        @blk.tensor
        def _(t):
            t.wait_ge(xld, 16)
            t.wait_ge(ms_, 6)
            k = 0
            for j, (nm, ncol, srck, dst) in enumerate(jobs):
                t.wait_ge(wl, 16 * (j + 1))
                t.wait_ge(ev, mm_before(j))
                srcb = {"x": xnb, "A": Ab, "B": Bb}[srck]
                nsc = {"x": DC, "A": cA, "B": cB}[srck] + 1
                if dst == "vt":
                    for tt in range(T // 128):
                        for vb in range(4):
                            t.wait_ge(ev, max(0, k - 1))
                            for c in range(nsc):
                                mmi = t.matmul(pp[:, k % 2, :], srcb[:, c, tt * 128:(tt + 1) * 128],
                                               wb[:, j % 2, c * ncol + vb * 512:c * ncol + (vb + 1) * 512],
                                               start=(c == 0), stop=(c == nsc - 1))
                            mmi.then_inc(mm, 1)
                            k += 1
                else:
                    for ot in range(ncol // 128):
                        for tb in range(NTB):
                            t.wait_ge(ev, max(0, k - 1))
                            for c in range(nsc):
                                mmi = t.matmul(pp[:, k % 2, :],
                                               wb[:, j % 2, c * ncol + ot * 128:c * ncol + (ot + 1) * 128],
                                               srcb[:, c, tb * 512:(tb + 1) * 512],
                                               start=(c == 0), stop=(c == nsc - 1))
                            mmi.then_inc(mm, 1)
                            k += 1

        @blk.scalar
        def _(sc):
            k = 0
            for j, (nm, ncol, srck, dst) in enumerate(jobs):
                for u in range(units(j)):
                    sc.wait_ge(mm, k + 1)
                    if dst in ("A", "B"):
                        ot, tb = divmod(u, NTB)
                        dstap = (Ab if dst == "A" else Bb)[:, ot, tb * 512:(tb + 1) * 512]
                    else:
                        sc.wait_ge(stm, 16 * sidx[k])
                        dstap = ost[:, k % 4, :]
                    sc.activation(out=dstap, in_=pp[:, k % 2, :], func=AF.Identity).then_inc(ev, 1)
                    k += 1

    cx.close()
    return nc



def build_causal_attn(T=TPC, mega=None):
    SC192 = 1.0 / float(np.sqrt(np.float32(192.0)))
    nc = mega["nc"] if mega else bass.Bass("TRN2", target_bir_lowering=False)
    QFd = _io(mega, nc, "qf", [DC, 128, T], BF16)
    RQd = _io(mega, nc, "rqr", [64, 16, T], BF16)
    KFd = _io(mega, nc, "kf", [DC, 128, S], BF16)
    RKd = _io(mega, nc, "rkr_f", [64, S], BF16)
    VTd = _io(mega, nc, "vt", [S, D], BF16)
    MTd = _io(mega, nc, "mt", [32, 2, 128, 512], BF16)
    CSQ = _io(mega, nc, "csq", [64, T], BF16)
    SSQ = _io(mega, nc, "ssq", [64, T], BF16)
    CSK = _io(mega, nc, "csk", [64, S], BF16)
    SSK = _io(mega, nc, "ssk", [64, S], BF16)
    ON5 = _io(mega, nc, "ones512", [1, 512], BF16)
    IDB = _io(mega, nc, "idb", [128, 128], BF16)
    WO = _io(mega, nc, "wo", [DC, 128, D], F32)
    BOC = _io(mega, nc, "boc", [128, DC], F32)
    S2d = _io(mega, nc, "s2", [DC, 128, T], F32, out=True)
    NTB = T // 512
    NKT = S // 128
    NPAIR = NKT // 2
    H = 16

    cx = Ctx(nc, pfx=mega["pfx"] if mega else None)
    kh = cx.sb("kh_c", [128, 2, NKT, 128], BF16)
    vth = cx.sb("vth_c", [128, 2, NKT, 128], BF16)
    qh = cx.sb("qh_c", [128, 2, T], BF16)
    rqh = cx.sb("rqh_c", [72, 2, 3, T], BF16)
    rkf = cx.sb("rkf_c", [72, 2, S], BF16)
    mt = cx.sb("mt_c", [128, NKT, 2, 512], BF16)
    csq = cx.sb("csq_c", [64, T], BF16)
    ssq = cx.sb("ssq_c", [64, T], BF16)
    on5 = cx.sb("on5_c", [1, 512], BF16)
    idb = cx.sb("idb_c", [128, 128], BF16)
    ones_b = cx.sb("ones_bc", [128, 1], BF16)
    ones_f = cx.sb("ones_fc", [1, 128], F32)
    sqq = cx.sb("sqq_c", [128, S], BF16)
    nrm = cx.sb("nrm_c", [1, 2, 512], F32)
    q2row = cx.sb("q2row_c", [1, 2, 512], F32)
    csr = nrm
    macc = cx.sb("macc_c", [1, 16], F32)
    nmrow = cx.sb("nmrow_c", [1, 2, 512], BF16)
    rrow = cx.sb("rrow_c", [1, 512], F32)
    exps = cx.sb("exps_c", [128, 2, 2, 512], BF16)
    ctx = cx.sb("ctx_c", [128, DC, T], BF16)
    wt = cx.sb("wt_c", [128, 1, DC, 128], BF16)
    boc = cx.sb("boc_c", [128, DC], F32)
    _ctxflat = None
    so = cx.sb("so_c", [128, 512], F32)
    rsc = cx.sb("rsc_c", [64, S], BF16)
    rksq = rsc
    _cf = ctx[:].rearrange("p c t -> p (c t)")
    csk = _cf[0:64, 0:S]
    ssk = _cf[0:64, S:2 * S]

    psA = cx.ps("psA_c", [128, 2, 2, 512], F32)
    pav = cx.ps("pav_c", [128, 512], F32)
    pnr = cx.ps("pnr_c", [1, 2, 512], F32)
    psm = pnr[0:1, 0:1, :]
    pbc = cx.ps("pbc_c", [128, 512], F32)
    pso = psA[:, 0, 0, :]

    (cl, il, qld, khl, vhl, wol, rkl, rkp, sqs, nmm, nev, mrd, smm, sex,
     avp, nrz, pbd, ctxe, omm, oev, stm, vch, sch, msc, nl1, nl2, mrw) = cx.sems(
        "cl", "il", "qld", "khl", "vhl", "wol", "rkl", "rkp", "sqs", "nmm",
        "nev", "mrd", "smm", "sex", "avp", "nrz", "pbd", "ctxe", "omm",
        "oev", "stm", "vch", "sch", "msc", "nl1", "nl2", "mrw")
    blk = cx.block()

    if True:
        @blk.sync
        def _(sync):
            sync.dma_start(out=mt[:], in_=MTd.rearrange("k q p j -> p k q j")).then_inc(cl, 16)
            sync.dma_start(out=idb[:], in_=IDB).then_inc(il, 16)
            sync.dma_start(out=on5[:], in_=ON5).then_inc(il, 16)
            sync.dma_start(out=csq[:], in_=CSQ).then_inc(il, 16)
            sync.dma_start(out=ssq[:], in_=SSQ).then_inc(il, 16)
            sync.dma_start(out=csk, in_=CSK).then_inc(il, 16)
            sync.dma_start(out=ssk, in_=SSK).then_inc(il, 16)
            sync.dma_start(out=boc[:], in_=BOC).then_inc(il, 16)
            sync.dma_start(out=rkf[0:64, 0, :], in_=RKd).then_inc(rkl, 16)
            sync.wait_ge(rkl, 16)
            sync.dma_start(out=rsc[0:32, :], in_=RKd[32:64, :]).then_inc(rkl, 16)
            sync.wait_ge(rkl, 32)
            sync.dma_start(out=rsc[32:64, :], in_=RKd[0:32, :]).then_inc(rkl, 16)
            for h in range(H):
                sync.wait_ge(khl, 48 * h)
                if h >= 2:
                    sync.wait_ge(smm, NTB * NPAIR * (h - 1))
                sync.dma_start(out=kh[:, h % 2, :, :],
                               in_=KFd[h, :, :].rearrange("p (kt j) -> p kt j", j=128)).then_inc(khl, 16)
                sync.wait_ge(khl, 48 * h + 16)
                sync.dma_start(out=qh[:, h % 2, :], in_=QFd[h, :, :]).then_inc(khl, 16)
                sync.wait_ge(khl, 48 * h + 32)
                sync.dma_start(out=rqh[0:64, h % 2, 0, :], in_=RQd[:, h, :]).then_inc(khl, 16)

        @blk.gpsimd
        def _(g):
            g.memset(ones_b[:], 1.0)
            g.memset(rkf[64:65, 1:2, :], 1.0)
            g.memset(ones_f[:], 1.0).then_inc(msc, 2)
            for h in range(H):
                g.wait_ge(vhl, 48 * h)
                if h >= 2:
                    g.wait_ge(avp, NPAIR * NTB * (h - 1))
                g.dma_start(out=vth[:, h % 2, :, :],
                            in_=VTd[:, h * 128:(h + 1) * 128].rearrange("(kt p) d -> p kt d", p=128)).then_inc(vhl, 16)
                g.wait_ge(vhl, 48 * h + 16)
                g.wait_ge(khl, 48 * (h + 1))
                g.dma_start(out=rqh[0:32, h % 2, 1, :], in_=RQd[32:64, h, :]).then_inc(vhl, 16)
                g.wait_ge(vhl, 48 * h + 32)
                g.dma_start(out=rqh[32:64, h % 2, 1, :], in_=RQd[0:32, h, :]).then_inc(vhl, 16)
                g.wait_ge(mrd, h + 1)
                if h >= 2:
                    g.wait_ge(smm, NTB * NPAIR * (h - 1))
                for qb in range(NTB):
                    g.dma_start(out=rqh[64:65, h % 2, 2, qb * 512:(qb + 1) * 512],
                                in_=nmrow[0:1, qb, :]).then_inc(mrw, 16)
            for ot in range(DC):
                g.wait_ge(wol, 16 * ot)
                if ot >= 1:
                    g.wait_ge(omm, NTB * ot)
                g.dma_start(out=wt[:, 0, :, :],
                            in_=WO[:, :, ot * 128:(ot + 1) * 128].rearrange("c p m -> p c m")).then_inc(wol, 16)
                if ot >= 1:
                    for tb in range(NTB):
                        k = (ot - 1) * NTB + tb
                        g.wait_ge(oev, k + 1)
                        g.wait_ge(stm, 16 * k)
                        g.dma_start(out=S2d[ot - 1, :, tb * 512:(tb + 1) * 512], in_=so[:]).then_inc(stm, 16)
            for tb in range(NTB):
                k = (DC - 1) * NTB + tb
                g.wait_ge(oev, k + 1)
                g.wait_ge(stm, 16 * k)
                g.dma_start(out=S2d[DC - 1, :, tb * 512:(tb + 1) * 512], in_=so[:]).then_inc(stm, 16)
            g.wait_ge(stm, 16 * DC * NTB)

        @blk.scalar
        def _(sc):
            sc.wait_ge(rkp, 1)
            sc.activation(out=rksq[:], in_=rkf[0:64, 1, :], func=AF.Square).then_inc(sqs, 1)

            def sq(hh):
                sc.wait_ge(khl, 48 * (hh + 1))
                sc.wait_ge(nmm, 6 * hh)
                sc.activation(out=sqq[:, 0:T], in_=qh[:, hh % 2, :], func=AF.Square).then_inc(sqs, 1)
                sc.wait_ge(rkp, hh + 2)
                sc.activation(out=sqq[0:64, T:2 * T], in_=rqh[0:64, hh % 2, 2, :], func=AF.Square).then_inc(sqs, 1)
                sc.wait_ge(nmm, 6 * hh + 2)
                sc.activation(out=sqq[:, 0:S],
                              in_=kh[:, hh % 2, :, :].rearrange("p a b -> p (a b)"), func=AF.Square).then_inc(sqs, 1)

            def mchain(hh):
                sc.wait_ge(nl1, hh + 1)
                if hh >= 1:
                    sc.wait_ge(mrd, hh)
                sc.activation(out=macc[:, 3:4], in_=macc[:, 2:3], func=AF.Ln).then_inc(nl2, 1)
                sc.wait_ge(nl2, 6 * hh + 1)
                sc.activation(out=macc[:, 3:4], in_=macc[:, 3:4], func=AF.Exp, scale=0.5).then_inc(nl2, 1)
                for qb in range(NTB):
                    sc.activation(out=csr[:, qb, :], in_=q2row[:, qb, :], func=AF.Ln).then_inc(nl2, 1)
                    sc.wait_ge(nl2, 6 * hh + 3 + 2 * qb)
                    sc.activation(out=csr[:, qb, :], in_=csr[:, qb, :], func=AF.Exp, scale=0.5).then_inc(nl2, 1)

            sq(0)
            mchain(0)
            for h in range(H):
                for qb in range(NTB):
                    if qb == 1 and h + 1 < H:
                        sq(h + 1)
                        mchain(h + 1)
                    for kp_ in range(NPAIR):
                        pi = (h * NTB + qb) * NPAIR + kp_
                        sc.wait_ge(smm, pi + 1)
                        if pi >= 2:
                            sc.wait_ge(avp, pi - 1)
                        sc.activation(out=exps[:, pi % 2, :, :].rearrange("p a b -> p (a b)"),
                                      in_=psA[:, pi % 2, :, :].rearrange("p a b -> p (a b)"),
                                      func=AF.Exp, scale=SC192).then_inc(sex, 1)
            for k in range(DC * NTB):
                ot, tb = divmod(k, NTB)
                sc.wait_ge(omm, k + 1)
                sc.wait_ge(stm, 16 * k)
                sc.activation(out=so[:], in_=pso, func=AF.Identity, bias=boc[:, ot:ot + 1]).then_inc(oev, 1)

        @blk.vector
        def _(v):
            nch = [0]

            def ch(instr):
                nch[0] += 1
                instr.then_inc(vch, 1)
                v.wait_ge(vch, nch[0])

            v.wait_ge(rkl, 48)
            v.wait_ge(il, 112)
            ch(v.tensor_tensor(rkf[0:64, 1, :], rkf[0:64, 0, :], csk, op=ALU.mult))
            ch(v.tensor_tensor(rsc[:], rsc[:], ssk, op=ALU.mult))
            ch(v.tensor_tensor(rkf[0:64, 1, :], rkf[0:64, 1, :], rsc[:], op=ALU.add))
            v.sem_inc(rkp, 1)

            def ropeq(hh):
                v.wait_ge(vhl, 48 * (hh + 1))
                v.wait_ge(khl, 48 * (hh + 1))
                ch(v.tensor_tensor(rqh[0:64, hh % 2, 2, :], rqh[0:64, hh % 2, 0, :], csq[:], op=ALU.mult))
                ch(v.tensor_tensor(rqh[0:64, hh % 2, 1, :], rqh[0:64, hh % 2, 1, :], ssq[:], op=ALU.mult))
                ch(v.tensor_tensor(rqh[0:64, hh % 2, 2, :], rqh[0:64, hh % 2, 2, :],
                                   rqh[0:64, hh % 2, 1, :], op=ALU.add))
                v.sem_inc(rkp, 1)

            def evacs(hh):
                for tb in range(2):
                    v.wait_ge(nmm, 6 * hh + tb + 1)
                    ch(v.tensor_copy(q2row[:, tb, :], pnr[0:1, 1, :]))
                    v.sem_inc(nev, 1)
                for rd in range(4):
                    v.wait_ge(nmm, 6 * hh + 3 + rd)
                    ch(v.tensor_copy(nrm[:, 0, :], pnr[0:1, 1, :]))
                    if rd == 0:
                        ch(v.tensor_reduce(macc[:, 2:3], nrm[0:1, 0, :], axis=AX.X, op=ALU.max))
                    else:
                        ch(v.tensor_reduce(macc[:, 3:4], nrm[0:1, 0, :], axis=AX.X, op=ALU.max))
                        ch(v.tensor_tensor(macc[:, 2:3], macc[:, 2:3], macc[:, 3:4], op=ALU.max))
                    v.sem_inc(nev, 1)
                v.sem_inc(nl1, 1)

            def mrows(hh):
                if hh >= 1:
                    v.wait_ge(mrw, 32 * hh)
                v.wait_ge(nl2, 6 * hh + 2)
                ch(v.tensor_scalar(macc[:, 6:7], macc[:, 3:4], 0.25, None, op0=ALU.mult))
                v.wait_ge(nl2, 6 * hh + 6)
                for qb in range(NTB):
                    ch(v.tensor_scalar(csr[:, qb, :], csr[:, qb, :], macc[:, 6:7], None, op0=ALU.mult))
                    ch(v.tensor_scalar(nmrow[:, qb, :], csr[:, qb, :], -1.0, None, op0=ALU.mult))
                v.sem_inc(mrd, 1)

            ropeq(0)
            evacs(0)
            mrows(0)
            for h in range(H):
                if h + 1 < H:
                    ropeq(h + 1)
                for qb in range(NTB):
                    if qb == 1 and h + 1 < H:
                        evacs(h + 1)
                        mrows(h + 1)
                    v.wait_ge(avp, (h * NTB + qb + 1) * NPAIR)
                    ch(v.tensor_copy(rrow[:], pnr[0:1, 0, :]))
                    ch(v.reciprocal(rrow[:], rrow[:]))
                    v.sem_inc(nrz, 1)
                    v.wait_ge(pbd, h * NTB + qb + 1)
                    ch(v.tensor_copy(ctx[:, h, qb * 512:(qb + 1) * 512], pav[:]))
                    ch(v.tensor_tensor(ctx[:, h, qb * 512:(qb + 1) * 512],
                                       ctx[:, h, qb * 512:(qb + 1) * 512], pbc[:], op=ALU.mult))
                    v.sem_inc(ctxe, 1)
        @blk.tensor
        def _(t):
            t.wait_ge(cl, 16)
            t.wait_ge(il, 112)
            t.wait_ge(msc, 2)
            t.wait_ge(rkl, 48)

            def norms(hh):
                t.wait_ge(khl, 48 * (hh + 1))
                t.wait_ge(vhl, 48 * (hh + 1))
                t.wait_ge(rkp, hh + 2)
                t.wait_ge(sqs, 3 * hh + 3)
                for tb in range(2):
                    t.wait_ge(nev, 6 * hh + tb)
                    t.matmul(pnr[0:1, 1, :], ones_b[:], sqq[:, tb * 512:(tb + 1) * 512],
                             start=True, stop=False)
                    mmi = t.matmul(pnr[0:1, 1, :], ones_b[0:64, :],
                                   sqq[0:64, T + tb * 512:T + (tb + 1) * 512],
                                   start=False, stop=True)
                    mmi.then_inc(nmm, 1)
                t.wait_ge(sqs, 3 * hh + 4)
                for rd in range(4):
                    t.wait_ge(nev, 6 * hh + 2 + rd)
                    for cc in range(2):
                        ci = rd * 2 + cc
                        t.matmul(pnr[0:1, 1, :], ones_b[:], sqq[:, ci * 512:(ci + 1) * 512],
                                 start=(cc == 0), stop=False)
                        mmi = t.matmul(pnr[0:1, 1, :], ones_b[0:64, :],
                                       rksq[:, ci * 512:(ci + 1) * 512],
                                       start=False, stop=(cc == 1))
                    mmi.then_inc(nmm, 1)

            norms(0)
            for h in range(H):
                t.wait_ge(mrw, 32 * (h + 1))
                for qb in range(NTB):
                    if qb == 1 and h + 1 < H:
                        norms(h + 1)
                    base = (h * NTB + qb) * NPAIR

                    def av_pair(kp2):
                        pi2 = base + kp2
                        t.wait_ge(sex, pi2 + 1)
                        for s2 in range(2):
                            kt2 = kp2 * 2 + s2
                            t.matmul(pav[:], vth[:, h % 2, kt2, :], exps[:, pi2 % 2, s2, :],
                                     start=(kt2 == 0), stop=(kt2 == NKT - 1))
                            mmi2 = t.matmul(pnr[0:1, 0, :], ones_b[:], exps[:, pi2 % 2, s2, :],
                                            start=(kt2 == 0), stop=(kt2 == NKT - 1))
                        mmi2.then_inc(avp, 1)

                    t.wait_ge(ctxe, h * NTB + qb)
                    for kp_ in range(NPAIR):
                        pi = base + kp_
                        if pi >= 2:
                            t.wait_ge(sex, pi - 1)
                        for s_ in range(2):
                            kt = kp_ * 2 + s_
                            t.matmul(psA[:, pi % 2, s_, :], kh[:, h % 2, kt, :],
                                     qh[:, h % 2, qb * 512:(qb + 1) * 512], start=True, stop=False)
                            t.matmul(psA[:, pi % 2, s_, :], rkf[0:65, 1, kt * 128:(kt + 1) * 128],
                                     rqh[0:65, h % 2, 2, qb * 512:(qb + 1) * 512], start=False, stop=False)
                            mmi = t.matmul(psA[:, pi % 2, s_, :], idb[:], mt[:, kt, qb, :], start=False, stop=True)
                        mmi.then_inc(smm, 1)
                        if kp_ >= 2:
                            av_pair(kp_ - 2)
                    av_pair(NPAIR - 2)
                    av_pair(NPAIR - 1)
                    t.wait_ge(nrz, h * NTB + qb + 1)
                    t.matmul(pbc[:], ones_f[:], rrow[:], start=True, stop=True).then_inc(pbd, 1)
            t.wait_ge(ctxe, H * NTB)
            for k in range(DC * NTB):
                ot, tb = divmod(k, NTB)
                t.wait_ge(wol, 16 * (ot + 1))
                t.wait_ge(oev, k)
                for c in range(DC):
                    mmi = t.matmul(pso, wt[:, 0, c, :], ctx[:, c, tb * 512:(tb + 1) * 512],
                                   start=(c == 0), stop=(c == DC - 1))
                mmi.then_inc(omm, 1)

    cx.close()
    return nc



def build_l2_ffn(T=TPC, mega=None):
    nc = mega["nc"] if mega else bass.Bass("TRN2", target_bir_lowering=False)
    XNd = _io(mega, nc, "xn", [DC, 128, T], F32)
    KQW = _io(mega, nc, "kqw", [DC, 128, D], BF16)
    BQC = _io(mega, nc, "bqc", [128, DC], F32)
    KSC2 = _io(mega, nc, "kscc", [128, DC], F32)
    IDF = _io(mega, nc, "idf", [128, 128], F32)
    G2d = _io(mega, nc, "g2", [DC, 128, T], F32, out=True)
    WSC = nc.dram_tensor(("" if mega is None else mega["pfx"]) + "wsc2", [4, T], F32, kind="Internal").ap()
    NTB = T // 512
    NTT = T // 128

    cx = Ctx(nc, pfx=mega["pfx"] if mega else None)
    xnf = cx.sb("xnf_2", [128, DC, T], F32)
    xq = cx.sb("xq_2", [128, DC, T], BF16)
    kqw = cx.sb("kqw_2", [128, DC, D], BF16)
    tre = cx.sb("tre_2", [128, 8, T], F32)
    bqc = cx.sb("bqc_2", [128, DC], F32)
    kscc = cx.sb("kscc_2", [128, DC], F32)
    idf = cx.sb("idf_2", [128, 128], F32)
    ones_f = cx.sb("ones_f2", [1, 128], F32)
    mxc = cx.sb("mxc_2", [128, 8], F32)
    scc = cx.sb("scc_2", [128, 2, 8], F32)
    sscr = cx.sb("sscr_2", [128, 2, 8], F32)
    sct = cx.sb("sct_2", [16, 128], F32)
    srow = cx.sb("srow_2", [1, 2, T], F32)
    st = cx.sb("st_2", [128, 2, 512], F32)
    pp = cx.ps("pp_2", [128, 2, 512], F32)
    ptf = cx.ps("ptf_2", [128, 128], F32)
    pbc = cx.ps("pbc_2", [128, 2, 512], F32)
    pbs = cx.ps("pbs_2", [128, 2, 512], F32)

    (xld, kld, bld, ifl, msx, tmax, tpz, tev, sdone, srl, bcd, qdone, mmy,
     yev, ymax, ytpz, ytev, ysdone, ysrl, ybcd, oev, stm, vch, sch, ya, yb) = cx.sems(
        "xld", "kld", "bld", "ifl", "msx", "tmax", "tpz", "tev", "sdone",
        "srl", "bcd", "qdone", "mmy", "yev", "ymax", "ytpz", "ytev", "ysdone",
        "ysrl", "ybcd", "oev", "stm", "vch", "sch", "ya", "yb")
    blk = cx.block()

    def emit_tree(v, ch, srcap, use_abs):
        if use_abs:
            ch(v.tensor_tensor(tre[:, 0:8, :], srcap[:, 0:8, :], srcap[:, 8:16, :], op=ALU.max))
            ch(v.tensor_tensor(tre[:, 0:4, :], tre[:, 0:4, :], tre[:, 4:8, :], op=ALU.max))
            ch(v.tensor_tensor(tre[:, 0:2, :], tre[:, 0:2, :], tre[:, 2:4, :], op=ALU.max))
            ch(v.tensor_tensor(tre[:, 0:1, :], tre[:, 0:1, :], tre[:, 1:2, :], op=ALU.max))
            ch(v.tensor_tensor(tre[:, 1:2, :].rearrange("p o t -> p (o t)").rearrange("p (o t) -> p o t", o=1), srcap[:, 0:8, :].rearrange("p c t -> p c t")[:, 0:1, :], srcap[:, 0:1, :], op=ALU.bypass)) if False else None
            ch(v.tensor_tensor(tre[:, 4:8, :].rearrange("p c t -> p c t")[:, 0:4, :], srcap[:, 0:8, :][:, 0:4, :], srcap[:, 8:16, :][:, 0:4, :], op=ALU.min)) if False else None
        return

    if True:
        @blk.sync
        def _(sync):
            sync.dma_start(out=xnf[:], in_=XNd.rearrange("c p t -> p c t")).then_inc(xld, 16)
            sync.dma_start(out=bqc[:], in_=BQC).then_inc(bld, 16)
            sync.dma_start(out=kscc[:], in_=KSC2).then_inc(bld, 16)
            sync.dma_start(out=idf[:], in_=IDF).then_inc(ifl, 16)
            sync.dma_start(out=kqw[:], in_=KQW.rearrange("c p m -> p c m")).then_inc(kld, 16)
            sync.wait_ge(sdone, 1)
            sync.dma_start(out=WSC[0:2, :].rearrange("a (b t) -> (a b) t", t=128), in_=sct[:]).then_inc(srl, 16)
            sync.wait_ge(srl, 16)
            sync.dma_start(out=srow[:, :, :], in_=WSC[0:2, :].unsqueeze(0)).then_inc(srl, 16)
            sync.wait_ge(ysdone, 1)
            sync.dma_start(out=WSC[2:4, :].rearrange("a (b t) -> (a b) t", t=128), in_=sct[:]).then_inc(ysrl, 16)
            sync.wait_ge(ysrl, 16)
            sync.dma_start(out=srow[:, :, :], in_=WSC[2:4, :].unsqueeze(0)).then_inc(ysrl, 16)

        @blk.gpsimd
        def _(g):
            g.memset(ones_f[:], 1.0).then_inc(ifl, 1)
            for k in range(DC * NTB):
                ot, tb = divmod(k, NTB)
                g.wait_ge(oev, k + 1)
                g.wait_ge(stm, 16 * k)
                g.dma_start(out=G2d[ot, :, tb * 512:(tb + 1) * 512], in_=st[:, k % 2, :]).then_inc(stm, 16)
            g.wait_ge(stm, 16 * DC * NTB)

        @blk.vector
        def _(v):
            nch = [0]

            def ch(instr):
                nch[0] += 1
                instr.then_inc(vch, 1)
                v.wait_ge(vch, nch[0])

            v.wait_ge(xld, 16)
            ch(v.tensor_tensor(tre[:, 0:8, :], xnf[:, 0:8, :], xnf[:, 8:16, :], op=ALU.max))
            ch(v.tensor_tensor(tre[:, 0:4, :], tre[:, 0:4, :], tre[:, 4:8, :], op=ALU.max))
            ch(v.tensor_tensor(tre[:, 0:2, :], tre[:, 0:2, :], tre[:, 2:4, :], op=ALU.max))
            ch(v.tensor_tensor(tre[:, 0:1, :], tre[:, 0:1, :], tre[:, 1:2, :], op=ALU.max))
            ch(v.tensor_tensor(tre[:, 4:8, :], xnf[:, 0:4, :], xnf[:, 4:8, :], op=ALU.min))
            ch(v.tensor_tensor(tre[:, 4:8, :], tre[:, 4:8, :], xnf[:, 8:12, :], op=ALU.min))
            ch(v.tensor_tensor(tre[:, 4:8, :], tre[:, 4:8, :], xnf[:, 12:16, :], op=ALU.min))
            ch(v.tensor_tensor(tre[:, 4:6, :], tre[:, 4:6, :], tre[:, 6:8, :], op=ALU.min))
            ch(v.tensor_tensor(tre[:, 4:5, :], tre[:, 4:5, :], tre[:, 5:6, :], op=ALU.min))
            ch(v.tensor_scalar(tre[:, 4:5, :], tre[:, 4:5, :], -1.0, None, op0=ALU.mult))
            ch(v.tensor_tensor(tre[:, 0:1, :], tre[:, 0:1, :], tre[:, 4:5, :], op=ALU.max))
            v.sem_inc(tmax, 1)
            for tt in range(NTT):
                v.wait_ge(tpz, tt + 1)
                ch(v.tensor_reduce(mxc[:, tt:tt + 1], ptf[:], axis=AX.X, op=ALU.max))
                v.sem_inc(tev, 1)
            emit_i8_scale(v, ch, scc[:, 0, :], scc[:, 1, :], mxc[:], sscr[:, 0, :], sscr[:, 1, :])
            v.sem_inc(tmax, 1)
            v.wait_ge(tpz, NTT + 1)
            ch(v.tensor_copy(sct[:], ptf[0:16, :]))
            v.sem_inc(sdone, 1)
            for tb in range(NTB):
                v.wait_ge(bcd, tb + 1)
                sl = slice(tb * 512, (tb + 1) * 512)
                for c in range(DC):
                    w8 = tre[:, c % 8, 0:512]
                    ch(v.tensor_tensor(w8, xnf[:, c, sl], pbc[:, tb, :], op=ALU.mult))
                    ch(v.tensor_scalar(w8, w8, MAGIC, -MAGIC, op0=ALU.add, op1=ALU.add))
                    ch(v.tensor_scalar(xq[:, c, sl], w8, 127.0, -127.0, op0=ALU.min, op1=ALU.max))
            v.sem_inc(qdone, 1)
            for k in range(DC * NTB):
                ot, tb = divmod(k, NTB)
                v.wait_ge(ya, k + 1)
                sl = slice(tb * 512, (tb + 1) * 512)
                ch(v.tensor_tensor(xnf[:, ot, sl], xnf[:, ot, sl], pbs[:, tb, :], op=ALU.mult))
                ch(v.tensor_scalar(xnf[:, ot, sl], xnf[:, ot, sl], kscc[:, ot:ot + 1], None, op0=ALU.mult))
                v.sem_inc(yb, 1)
            v.wait_ge(yev, DC * NTB)
            ch(v.tensor_tensor(tre[:, 0:8, :], xnf[:, 0:8, :], xnf[:, 8:16, :], op=ALU.max))
            ch(v.tensor_tensor(tre[:, 0:4, :], tre[:, 0:4, :], tre[:, 4:8, :], op=ALU.max))
            ch(v.tensor_tensor(tre[:, 0:2, :], tre[:, 0:2, :], tre[:, 2:4, :], op=ALU.max))
            ch(v.tensor_tensor(tre[:, 0:1, :], tre[:, 0:1, :], tre[:, 1:2, :], op=ALU.max))
            v.sem_inc(ymax, 1)
            for tt in range(NTT):
                v.wait_ge(ytpz, tt + 1)
                ch(v.tensor_reduce(mxc[:, tt:tt + 1], ptf[:], axis=AX.X, op=ALU.max))
                v.sem_inc(ytev, 1)
            emit_i8_scale(v, ch, scc[:, 0, :], scc[:, 1, :], mxc[:], sscr[:, 0, :], sscr[:, 1, :])
            v.sem_inc(ymax, 1)
            v.wait_ge(ytpz, NTT + 1)
            ch(v.tensor_copy(sct[:], ptf[0:16, :]))
            v.sem_inc(ysdone, 1)
            for k in range(DC * NTB):
                ot, tb = divmod(k, NTB)
                v.wait_ge(ybcd, tb + 1)
                v.wait_ge(stm, 16 * max(0, k - 1))
                sl = slice(tb * 512, (tb + 1) * 512)
                ch(v.tensor_tensor(st[:, k % 2, :], xnf[:, ot, sl], pbc[:, tb, :], op=ALU.mult))
                ch(v.tensor_scalar(st[:, k % 2, :], st[:, k % 2, :], MAGIC, -MAGIC, op0=ALU.add, op1=ALU.add))
                ch(v.tensor_scalar(st[:, k % 2, :], st[:, k % 2, :], 127.0, -127.0, op0=ALU.min, op1=ALU.max))
                ch(v.tensor_tensor(st[:, k % 2, :], st[:, k % 2, :], pbs[:, tb, :], op=ALU.mult))
                v.sem_inc(oev, 1)

        @blk.tensor
        def _(t):
            t.wait_ge(xld, 16)
            t.wait_ge(ifl, 17)
            t.wait_ge(tmax, 1)
            for tt in range(NTT):
                if tt >= 1:
                    t.wait_ge(tev, tt)
                t.transpose(ptf[:], tre[:, 0, tt * 128:(tt + 1) * 128], idf[:]).then_inc(tpz, 1)
            t.wait_ge(tmax, 2)
            t.wait_ge(tev, NTT)
            t.transpose(ptf[0:16, 0:128], scc[:].rearrange("p a b -> p (a b)"), idf[:]).then_inc(tpz, 1)
            t.wait_ge(srl, 32)
            for tb in range(NTB):
                t.matmul(pbc[:, tb, :], ones_f[:], srow[:, 1, tb * 512:(tb + 1) * 512], start=True, stop=True)
                t.matmul(pbs[:, tb, :], ones_f[:], srow[:, 0, tb * 512:(tb + 1) * 512], start=True, stop=True).then_inc(bcd, 1)
            t.wait_ge(kld, 16)
            t.wait_ge(qdone, 1)
            for k in range(DC * NTB):
                ot, tb = divmod(k, NTB)
                if k >= 2:
                    t.wait_ge(ya, k - 1)
                for c in range(DC):
                    mmi = t.matmul(pp[:, k % 2, :], kqw[:, c, ot * 128:(ot + 1) * 128],
                                   xq[:, c, tb * 512:(tb + 1) * 512], start=(c == 0), stop=(c == DC - 1))
                mmi.then_inc(mmy, 1)
            t.wait_ge(ymax, 1)
            for tt in range(NTT):
                if tt >= 1:
                    t.wait_ge(ytev, tt)
                t.transpose(ptf[:], tre[:, 0, tt * 128:(tt + 1) * 128], idf[:]).then_inc(ytpz, 1)
            t.wait_ge(ymax, 2)
            t.wait_ge(ytev, NTT)
            t.transpose(ptf[0:16, 0:128], scc[:].rearrange("p a b -> p (a b)"), idf[:]).then_inc(ytpz, 1)
            t.wait_ge(ysrl, 32)
            for tb in range(NTB):
                t.matmul(pbc[:, tb, :], ones_f[:], srow[:, 1, tb * 512:(tb + 1) * 512], start=True, stop=True)
                t.matmul(pbs[:, tb, :], ones_f[:], srow[:, 0, tb * 512:(tb + 1) * 512], start=True, stop=True).then_inc(ybcd, 1)

        @blk.scalar
        def _(sc):
            sc.wait_ge(bld, 32)
            for k in range(DC * NTB):
                ot, tb = divmod(k, NTB)
                sc.wait_ge(mmy, k + 1)
                sc.activation(out=xnf[:, ot, tb * 512:(tb + 1) * 512], in_=pp[:, k % 2, :],
                              func=AF.Identity).then_inc(ya, 1)
                if k >= 1:
                    sc.wait_ge(yb, k)
                    ktb = k - 1
                    pot, ptb = divmod(ktb, NTB)
                    sc.activation(out=xnf[:, pot, ptb * 512:(ptb + 1) * 512],
                                  in_=xnf[:, pot, ptb * 512:(ptb + 1) * 512],
                                  func=AF.Relu, bias=bqc[:, pot:pot + 1]).then_inc(yev, 1)
            sc.wait_ge(yb, DC * NTB)
            ktb = DC * NTB - 1
            pot, ptb = divmod(ktb, NTB)
            sc.activation(out=xnf[:, pot, ptb * 512:(ptb + 1) * 512],
                          in_=xnf[:, pot, ptb * 512:(ptb + 1) * 512],
                          func=AF.Relu, bias=bqc[:, pot:pot + 1]).then_inc(yev, 1)

    cx.close()
    return nc



def build_l3_ffn(T=TPC, mega=None):
    nc = mega["nc"] if mega else bass.Bass("TRN2", target_bir_lowering=False)
    XNd = _io(mega, nc, "xn", [DC, 128, T], F32)
    WR = _io(mega, nc, "wr", [DC + 1, 128, 8], F32)
    WDP = _io(mega, nc, "wdp", [DC + 1, 128, 8, 16], F32)
    WUP = _io(mega, nc, "wup", [16, 8, D], F32)
    BUP = _io(mega, nc, "bup", [8, D], F32)
    IDF = _io(mega, nc, "idf", [128, 128], F32)
    G3d = _io(mega, nc, "g3", [DC, 128, T], F32, out=True)
    WS3 = nc.dram_tensor(("" if mega is None else mega["pfx"]) + "wsc3", [8, T], F32, kind="Internal").ap()
    NTB = T // 512
    NTT = T // 128
    BONUS = float(np.float32(3.0 * (1.0 - 1.0 / 5000.0)))

    cx = Ctx(nc, pfx=mega["pfx"] if mega else None)
    xnf = cx.sb("xnf_3", [128, DC + 1, T], F32)
    xnb = cx.sb("xnb_3", [128, DC + 1, T], BF16)
    wr = cx.sb("wr_3", [128, DC + 1, 8], F32)
    wd = cx.sb("wd_3", [128, DC + 1, 8, 16], BF16)
    wu = cx.sb("wu_3", [16, 8, D], BF16)
    bu = cx.sb("bu_3", [8, D], F32)
    idf = cx.sb("idf_3", [128, 128], F32)
    ones_f = cx.sb("ones_f3", [1, 128], F32)
    onesb16 = cx.sb("onesb16_3", [1, 16], BF16)
    lt = cx.sb("lt_3", [8, T], F32)
    rt = cx.sb("rt_3", [128, NTT, 8], F32)
    r1 = cx.sb("r1_3", [128, NTT, 8], F32)
    mx1 = cx.sb("mx1_3", [128, NTT, 1], F32)
    mx2 = cx.sb("mx2_3", [128, NTT, 1], F32)
    wno = cx.sb("wno_3", [128, NTT, 1], F32)
    wx = cx.sb("wx_3", [128, NTT, 8], F32)
    wfm = cx.sb("wfm_3", [8, NTT, 128], F32)
    wrow = cx.sb("wrow_3", [1, 8, T], BF16)
    hst = cx.sb("hst_3", [16, 512], F32)
    hp = cx.sb("hp_3", [16, 8, T], BF16)
    so = cx.sb("so_3", [128, 2, 512], F32)

    plog = cx.ps("plog_3", [8, 2, 512], F32)
    ptr = cx.ps("ptr_3", [128, 128], F32)
    pex = cx.ps("pex_3", [16, 512], F32)
    pwb = cx.ps("pwb_3", [16, 512], F32)
    pout = cx.ps("pout_3", [128, 512], F32)
    pbu = cx.ps("pbu_3", [128, 512], F32)

    (xld, xbl, wrl, wdl, wul, ifl, lmm, lev, rtp, rte, wtb, wte, wsd, wrl2,
     hmm, hev, hsc, omm, oev, stm, vch, sch, msc, wrl3, ya2) = cx.sems(
        "xld", "xbl", "wrl", "wdl", "wul", "ifl", "lmm", "lev", "rtp", "rte",
        "wtb", "wte", "wsd", "wrl2", "hmm", "hev", "hsc", "omm", "oev", "stm",
        "vch", "sch", "msc", "wrl3", "ya2")
    blk = cx.block()

    if True:
        @blk.sync
        def _(sync):
            sync.dma_start(out=idf[:], in_=IDF).then_inc(ifl, 16)
            sync.dma_start(out=wr[:], in_=WR.rearrange("c p m -> p c m")).then_inc(wrl, 16)
            sync.dma_start(out=xnf[:, 0:DC, :], in_=XNd.rearrange("c p t -> p c t")).then_inc(xld, 16)
            sync.wait_ge(wsd, NTT)
            sync.dma_start(out=WS3, in_=wfm[:, :, :]).then_inc(wrl2, 16)

        @blk.gpsimd
        def _(g):
            g.memset(ones_f[:], 1.0)
            g.memset(onesb16[:], 1.0)
            g.memset(xnf[:, DC, :], 0.0).then_inc(msc, 1)
            g.wait_ge(msc, 1)
            g.memset(xnf[0:1, DC, :], 1.0).then_inc(msc, 1)
            g.wait_ge(xld, 16)
            g.wait_ge(msc, 2)
            g.dma_start(out=xnb[:, 0:DC, :], in_=XNd.rearrange("c p t -> p c t")).then_inc(xbl, 16)
            g.wait_ge(xbl, 16)
            g.memset(xnb[:, DC, :], 0.0).then_inc(msc, 1)
            g.wait_ge(msc, 3)
            g.memset(xnb[0:1, DC, :], 1.0).then_inc(msc, 1)
            g.dma_start(out=wd[:], in_=WDP.rearrange("c p e r -> p c (e r)").rearrange("p c (e r) -> p c e r", r=16)).then_inc(wdl, 16)
            g.wait_ge(wrl2, 16)
            g.dma_start(out=wrow[0:1, :, :], in_=WS3.unsqueeze(0)).then_inc(wrl3, 16)
            g.dma_start(out=wu[:], in_=WUP).then_inc(wul, 16)
            g.dma_start(out=bu[:], in_=BUP).then_inc(wul, 16)
            for k in range(DC * NTB):
                ot, tb = divmod(k, NTB)
                g.wait_ge(oev, k + 1)
                g.wait_ge(stm, 16 * k)
                g.dma_start(out=G3d[ot, :, tb * 512:(tb + 1) * 512], in_=so[:, k % 2, :]).then_inc(stm, 16)
            g.wait_ge(stm, 16 * DC * NTB)

        @blk.tensor
        def _(t):
            t.wait_ge(xld, 16)
            t.wait_ge(wrl, 16)
            t.wait_ge(ifl, 16)
            t.wait_ge(msc, 2)
            for tb in range(NTB):
                for c in range(DC + 1):
                    mmi = t.matmul(plog[:, tb, :], wr[:, c, :], xnf[:, c, tb * 512:(tb + 1) * 512],
                                   start=(c == 0), stop=(c == DC))
            mmi.then_inc(lmm, 1)
            for tt in range(NTT):
                t.wait_ge(lev, 1)
                t.wait_ge(rte, tt)
                t.transpose(ptr[:, 0:8], lt[:, tt * 128:(tt + 1) * 128], idf[0:8, 0:8]).then_inc(rtp, 1)
            for tt in range(NTT):
                t.wait_ge(wte, tt + 1)
                t.wait_ge(wsd, tt)
                t.transpose(ptr[0:8, :], wx[:, tt, :], idf[:]).then_inc(rtp, 1)
            t.wait_ge(xbl, 16)
            t.wait_ge(wdl, 16)
            t.wait_ge(msc, 4)
            for e in range(8):
                for tb in range(NTB):
                    u = e * NTB + tb
                    if u >= 1:
                        t.wait_ge(hev, u)
                    for c in range(DC + 1):
                        mmi = t.matmul(pex[:], xnb[:, c, tb * 512:(tb + 1) * 512] if False else wd[:, c, e, :],
                                       xnb[:, c, tb * 512:(tb + 1) * 512],
                                       start=(c == 0), stop=(c == DC))
                    mmi.then_inc(hmm, 1)
                    t.wait_ge(wrl3, 16)
                    t.wait_ge(hsc, u)
                    t.matmul(pwb[:], onesb16[0:1, 0:16], wrow[0:1, e, tb * 512:(tb + 1) * 512],
                             start=True, stop=True).then_inc(hmm, 1)
            t.wait_ge(wul, 32)
            t.wait_ge(hsc, 16)
            for k in range(DC * NTB):
                ot, tb = divmod(k, NTB)
                t.wait_ge(ya2, k)
                for e in range(8):
                    mmi = t.matmul(pout[:], wu[:, e, ot * 128:(ot + 1) * 128],
                                   hp[:, e, tb * 512:(tb + 1) * 512], start=(e == 0), stop=(e == 7))
                mmi.then_inc(omm, 1)
                t.wait_ge(oev, k)
                t.matmul(pbu[:], bu[:, ot * 128:(ot + 1) * 128],
                         wfm[:].rearrange("p a b -> p (a b)")[:, tb * 512:(tb + 1) * 512],
                         start=True, stop=True).then_inc(omm, 1)

        @blk.scalar
        def _(sc):
            sc.wait_ge(lmm, 1)
            sc.activation(out=lt[:], in_=plog[:].rearrange("p a b -> p (a b)"), func=AF.Identity).then_inc(sch, 1)
            sc.wait_ge(sch, 2)
            sc.activation(out=rt[:].rearrange("p a b -> p (a b)"), in_=rt[:].rearrange("p a b -> p (a b)"),
                          func=AF.Exp).then_inc(sch, 1)
            for e in range(8):
                for tb in range(NTB):
                    u = e * NTB + tb
                    sc.wait_ge(hmm, 2 * u + 1)
                    if u >= 1:
                        sc.wait_ge(hsc, u)
                    sc.activation(out=hst[:, :], in_=pex[:], func=AF.Relu).then_inc(hev, 1)
            for k in range(DC * NTB):
                sc.wait_ge(omm, 2 * k + 1)
                sc.wait_ge(stm, 16 * max(0, k - 1))
                sc.activation(out=so[:, k % 2, :], in_=pout[:], func=AF.Identity).then_inc(ya2, 1)

        @blk.vector
        def _(v):
            nch = [0]

            def ch(instr):
                nch[0] += 1
                instr.then_inc(vch, 1)
                v.wait_ge(vch, nch[0])

            v.wait_ge(sch, 1)
            ch(v.tensor_scalar(lt[:], lt[:], BONUS, None, op0=ALU.add))
            v.sem_inc(lev, 1)
            for tt in range(NTT):
                v.wait_ge(rtp, tt + 1)
                ch(v.tensor_copy(rt[:, tt, :], ptr[:, 0:8]))
                v.sem_inc(rte, 1)
            ch(v.tensor_reduce(mx1[:], rt[:], axis=AX.X, op=ALU.max))
            ch(v.tensor_tensor(rt[:], rt[:], mx1[:].broadcast_to([128, NTT, 8]), op=ALU.subtract))
            v.sem_inc(sch, 1)
            v.wait_ge(sch, 3)
            ch(v.tensor_reduce(mx1[:], rt[:], axis=AX.X, op=ALU.add))
            ch(v.reciprocal(mx1[:], mx1[:]))
            ch(v.tensor_tensor(rt[:], rt[:], mx1[:].broadcast_to([128, NTT, 8]), op=ALU.mult))
            ch(v.tensor_reduce(mx1[:], rt[:], axis=AX.X, op=ALU.max))
            ch(v.tensor_tensor(wx[:], rt[:], mx1[:].broadcast_to([128, NTT, 8]), op=ALU.is_ge))
            ch(v.scalar_tensor_tensor(r1[:], wx[:], -1e30, rt[:], op0=ALU.mult, op1=ALU.add))
            ch(v.tensor_reduce(mx2[:], r1[:], axis=AX.X, op=ALU.max))
            ch(v.tensor_tensor(r1[:], r1[:], mx2[:].broadcast_to([128, NTT, 8]), op=ALU.is_ge))
            ch(v.tensor_tensor(wno[:], mx1[:], mx2[:], op=ALU.add))
            ch(v.reciprocal(wno[:], wno[:]))
            ch(v.tensor_tensor(wx[:], wx[:], mx1[:].broadcast_to([128, NTT, 8]), op=ALU.mult))
            ch(v.tensor_tensor(r1[:], r1[:], mx2[:].broadcast_to([128, NTT, 8]), op=ALU.mult))
            ch(v.tensor_tensor(wx[:], wx[:], r1[:], op=ALU.add))
            ch(v.tensor_tensor(wx[:], wx[:], wno[:].broadcast_to([128, NTT, 8]), op=ALU.mult))
            for tt in range(NTT):
                v.sem_inc(wte, 1)
            for tt in range(NTT):
                v.wait_ge(rtp, NTT + tt + 1)
                ch(v.tensor_copy(wfm[:, tt, :], ptr[0:8, :]))
                v.sem_inc(wsd, 1)
            for e in range(8):
                for tb in range(NTB):
                    u = e * NTB + tb
                    v.wait_ge(hev, u + 1)
                    v.wait_ge(hmm, 2 * u + 2)
                    ch(v.tensor_tensor(hp[:, e, tb * 512:(tb + 1) * 512], hst[:, :], pwb[:], op=ALU.mult))
                    v.sem_inc(hsc, 1)
            for k in range(DC * NTB):
                v.wait_ge(ya2, k + 1)
                v.wait_ge(omm, 2 * k + 2)
                ch(v.tensor_tensor(so[:, k % 2, :], so[:, k % 2, :], pbu[:], op=ALU.add))
                v.sem_inc(oev, 1)

    cx.close()
    return nc



def _EI(nc, name, shape, dtype):
    return nc.dram_tensor(name, shape, dtype, kind="ExternalInput").ap()


def _EO(nc, name, shape, dtype):
    return nc.dram_tensor(name, shape, dtype, kind="ExternalOutput").ap()


def _IN(nc, name, shape, dtype):
    return nc.dram_tensor(name, shape, dtype, kind="Internal").ap()


def build_launchA():
    nc = bass.Bass("TRN2", target_bir_lowering=False)
    BV8 = _IN(nc, "BV8", [1023, 1], F32)
    io_p = {"gsl": _EI(nc, "gsl", [1024, 2048], F32),
            "usl": _EI(nc, "usl", [1024, 2048], F32),
            "dsl": _EI(nc, "dsl", [256, 8192], F32),
            "ksl": _EI(nc, "ksl", [256, 2048], F32),
            "bias2": _EI(nc, "bias2", [1, 2048], F32),
            "relp": _EI(nc, "relp", [1023, 64], F32),
            "idp": _EI(nc, "idp", [128, 128], BF16),
            "wguc": _EO(nc, "wguc", [2, DC, 128, 1024], BF16),
            "wdqc": _EO(nc, "wdqc", [64, 128, 256], BF16),
            "kqwc": _EO(nc, "kqwc", [DC, 128, 256], BF16),
            "bqv": _EO(nc, "bqv", [1, 2048], F32),
            "bv8": BV8, "ksc": _EO(nc, "ksc", [2, 128], F32),
            "nc": nc, "pfx": "pA_"}
    build_prep(trans=True, mega=io_p)
    io_q = {"bv8": BV8,
            "scl": _EI(nc, "scl", [10, 128, 128], BF16),
            "scm": _EI(nc, "scm", [10, 128, QF_F], BF16),
            "qdv": _EO(nc, "qdv", [1, 128 * QF_F], BF16),
            "nc": nc, "pfx": "qA_"}
    build_qscat(mega=io_q)
    io_n = {"x": _EI(nc, "x0", [DC, 128, TPC], F32),
            "xn": _EO(nc, "xn0", [DC, 128, TPC], F32),
            "nc": nc, "pfx": "nA_"}
    build_combine_ln(False, mega=io_n)
    return nc


def build_launchB():
    nc = bass.Bass("TRN2", target_bir_lowering=False)
    io = {"xn": _EI(nc, "xnf", [DC, 128, NTOK], F32),
          "wq": _EI(nc, "wq", [DC, 128, 768], F32),
          "ddt_d": _EI(nc, "ddt_d", [L0_HPC, 2, 128, L0_BS], BF16),
          "qdb_d": _EI(nc, "qdb_d", [128, 2, L0_BS], BF16),
          "kdc_d": _EI(nc, "kdc_d", [128, 2, L0_HPC], F32),
          "ident": _EI(nc, "ident", [128, 64], BF16),
          "bd4": _EI(nc, "bd4", [64, L0_HPC], F32),
          "attn": _EO(nc, "attn", [L0_HPC, 64, NTOK], BF16),
          "nc": nc, "pfx": "aB_"}
    build_l0_attn(mega=io)
    return nc


def build_launchC():
    nc = bass.Bass("TRN2", target_bir_lowering=False)
    T = TPC
    XN0 = _EI(nc, "xn0", [DC, 128, T], F32)
    X0 = _EI(nc, "x0", [DC, 128, T], F32)
    S0 = _IN(nc, "S0", [DC, 128, T], F32)
    G0 = _IN(nc, "G0", [DC, 128, T], F32)
    XN1 = _IN(nc, "XN1", [DC, 128, T], F32)
    VTE1 = _EO(nc, "vte1", [T, L1_H, 64], BF16)
    io = {"attn": _EI(nc, "attn", [DC, 128, T], BF16), "xn": XN0,
          "nw": _EI(nc, "nw", [128, DC], F32),
          "wg": _EI(nc, "wg", [DC, 128, D], F32),
          "wo": _EI(nc, "wo0", [DC, 128, D], F32),
          "s0": S0, "nc": nc, "pfx": "pC_"}
    build_l0_post(mega=io)
    io = {"xn": XN0, "idf": _EI(nc, "idf", [128, 128], F32),
          "idb": _EI(nc, "idb", [128, 128], BF16),
          "wgu": _EI(nc, "wgu", [16, DC, 128, 1024], BF16),
          "wdq": _EI(nc, "wdq", [64, 128, 2048], BF16),
          "g0": G0, "nc": nc, "pfx": "fC_"}
    build_l0_ffn(mega=io)
    io = {"x": X0, "s": S0, "g": G0,
          "xnew": _EO(nc, "x1", [DC, 128, T], F32), "xn": XN1,
          "nc": nc, "pfx": "cC_"}
    build_combine_ln(True, mega=io)
    QF1 = _EO(nc, "qf1", [L1_H, 64, T], BF16)
    io = {"xn": XN1, "wq": _EI(nc, "wq1", [DC, 128, D], F32),
          "wk": _EI(nc, "wk1", [DC, 128, D], F32),
          "wv": _EI(nc, "wv1", [DC, 128, D], F32),
          "qf": QF1.rearrange("(c h2) d t -> c (h2 d) t", h2=2),
          "kf": _EO(nc, "kf1", [DC, 128, T], BF16),
          "vt": VTE1.rearrange("t h e -> t (h e)"),
          "nc": nc, "pfx": "jC_"}
    build_l1_proj(mega=io)
    io = {"xn": XN1, "cls": _EI(nc, "cls", [DC, 128, 12], F32),
          "ebias": _EI(nc, "ebias", [1, 12], F32),
          "idf": _EI(nc, "idf2", [128, 128], F32),
          "wgu_e": _EI(nc, "wgu_e", [8, DC, 128, 2048], F32),
          "wd_e": _EI(nc, "wd_e", [8, 8, 128, 2048], F32),
          "g1": _EO(nc, "g1", [DC, 128, T], F32),
          "nc": nc, "pfx": "mC_"}
    build_l1_moe(mega=io)
    return nc


def build_launchD():
    nc = bass.Bass("TRN2", target_bir_lowering=False)
    T = TPC
    S1 = _IN(nc, "S1", [DC, 128, T], F32)
    XN2 = _IN(nc, "XN2", [DC, 128, T], F32)
    io = {"qf": _EI(nc, "qf1", [L1_H, 64, T], BF16),
          "kf": _EI(nc, "kf1f", [DC, 128, S], BF16),
          "vte": _EI(nc, "vte1f", [S, L1_H, 64], BF16),
          "qdv": _EI(nc, "qdv", [1, 128 * QF_F], BF16),
          "idb": _EI(nc, "flip", [128, 128], BF16),
          "ones512": _EI(nc, "ones512", [1, 512], BF16),
          "wo": _EI(nc, "wo1", [DC, 128, D], F32),
          "s1": S1, "nc": nc, "pfx": "aD_"}
    build_l1_attn(mega=io)
    io = {"x": _EI(nc, "x1", [DC, 128, T], F32), "s": S1,
          "g": _EI(nc, "g1", [DC, 128, T], F32),
          "xnew": _EO(nc, "x2", [DC, 128, T], F32), "xn": XN2,
          "nc": nc, "pfx": "cD_"}
    build_combine_ln(True, mega=io)
    io = {"xn": XN2, "nc": nc, "pfx": "jD_",
          "qf": _EO(nc, "qf2", [DC, 128, T], BF16),
          "rqr": _EO(nc, "rq2", [64, 16, T], BF16),
          "kf": _EO(nc, "kf2", [DC, 128, T], BF16),
          "rkr": _EO(nc, "rk2", [128, T], BF16),
          "vt": _EO(nc, "vt2", [T, D], BF16)}
    for nm, shp in [("w1", [DC + 1, 128, 512]), ("w2", [DC + 1, 128, 1536]),
                    ("w3", [DC + 1, 128, 128]), ("w4", [5, 128, D]),
                    ("w5", [5, 128, D]), ("w6", [13, 128, D]),
                    ("w7", [13, 128, 1024])]:
        io[nm] = _EI(nc, "m2" + nm, shp, F32)
    build_mla_proj(512, 1536, mega=io)
    io = {"xn": XN2, "kqw": _EI(nc, "kqw", [DC, 128, D], BF16),
          "bqc": _EI(nc, "bqc", [128, DC], F32),
          "kscc": _EI(nc, "kscc", [128, DC], F32),
          "idf": _EI(nc, "idf", [128, 128], F32),
          "g2": _EO(nc, "g2", [DC, 128, T], F32),
          "nc": nc, "pfx": "fD_"}
    build_l2_ffn(mega=io)
    return nc


def _launch_ca_tail(nc, tag, final):
    T = TPC
    Sx = _IN(nc, "Sx", [DC, 128, T], F32)
    XNn = _IN(nc, "XNn", [DC, 128, T], F32)
    io = {"qf": _EI(nc, "qf", [DC, 128, T], BF16),
          "rqr": _EI(nc, "rqr", [64, 16, T], BF16),
          "kf": _EI(nc, "kff", [DC, 128, S], BF16),
          "rkr_f": _EI(nc, "rkf", [64, S], BF16),
          "vt": _EI(nc, "vtf", [S, D], BF16),
          "mt": _EI(nc, "mt", [32, 2, 128, 512], BF16),
          "csq": _EI(nc, "csq", [64, T], BF16),
          "ssq": _EI(nc, "ssq", [64, T], BF16),
          "csk": _EI(nc, "csk", [64, S], BF16),
          "ssk": _EI(nc, "ssk", [64, S], BF16),
          "ones512": _EI(nc, "ones512", [1, 512], BF16),
          "idb": _EI(nc, "idb", [128, 128], BF16),
          "wo": _EI(nc, "woc", [DC, 128, D], F32),
          "boc": _EI(nc, "boc", [128, DC], F32),
          "s2": Sx, "nc": nc, "pfx": "a%s_" % tag}
    build_causal_attn(mega=io)
    io = {"x": _EI(nc, "xin", [DC, 128, T], F32), "s": Sx,
          "g": _EI(nc, "gin", [DC, 128, T], F32),
          "xnew": _EO(nc, "xout", [DC, 128, T], F32), "xn": XNn,
          "nc": nc, "pfx": "c%s_" % tag}
    build_combine_ln(True, mega=io)
    return XNn


def build_launchE():
    nc = bass.Bass("TRN2", target_bir_lowering=False)
    XN3 = _launch_ca_tail(nc, "E", False)
    io = {"xn": XN3, "nc": nc, "pfx": "jE_",
          "qf": _EO(nc, "qf3", [DC, 128, TPC], BF16),
          "rqr": _EO(nc, "rq3", [64, 16, TPC], BF16),
          "kf": _EO(nc, "kf3", [DC, 128, TPC], BF16),
          "rkr": _EO(nc, "rk3", [128, TPC], BF16),
          "vt": _EO(nc, "vt3", [TPC, D], BF16)}
    for nm, shp in [("w1", [DC + 1, 128, 512]), ("w2", [DC + 1, 128, 512]),
                    ("w3", [DC + 1, 128, 128]), ("w4", [5, 128, D]),
                    ("w5", [5, 128, D]), ("w6", [5, 128, D]),
                    ("w7", [5, 128, 1024])]:
        io[nm] = _EI(nc, "m3" + nm, shp, F32)
    build_mla_proj(512, 512, mega=io)
    io = {"xn": XN3, "wr": _EI(nc, "wr", [DC + 1, 128, 8], F32),
          "wdp": _EI(nc, "wdp", [DC + 1, 128, 8, 16], F32),
          "wup": _EI(nc, "wup", [16, 8, D], F32),
          "bup": _EI(nc, "bup", [8, D], F32),
          "idf": _EI(nc, "idf", [128, 128], F32),
          "g3": _EO(nc, "g3", [DC, 128, TPC], F32),
          "nc": nc, "pfx": "fE_"}
    build_l3_ffn(mega=io)
    return nc


def build_launchF():
    nc = bass.Bass("TRN2", target_bir_lowering=False)
    _launch_ca_tail(nc, "F", True)
    return nc


def _ident():
    return np.eye(128, dtype=np.float32)


def _padw(w, b, ncolpad=None):
    K, M = w.shape
    Mp = ncolpad or M
    out = np.zeros((K // 128 + 1, 128, Mp), np.float32)
    for c in range(K // 128):
        out[c, :, 0:M] = w[c * 128:(c + 1) * 128, :]
    out[K // 128, 0, 0:M] = b
    return out


def _l2_perm():
    p = np.zeros(64, np.int64)
    for j in range(32):
        p[j] = 2 * j
        p[32 + j] = 2 * j + 1
    return p


def _rope_tabs_l2(pos):
    import math
    inv = np.exp(np.arange(0, 128, 2, dtype=np.float32) * (-math.log(10000.0) / 128.0))
    ang = pos[:, None].astype(np.float32) * inv[None, :]
    c = np.cos(ang)[:, 0:32].astype(np.float32)
    s = np.sin(ang)[:, 0:32].astype(np.float32)
    return c, s


def _rope_tabs_l3(pos):
    freqs = (1.0 / 10000.0 ** (np.arange(0, 64, 2, dtype=np.float32) / 64.0)).astype(np.float32)
    ang = np.outer(pos.astype(np.float32), freqs)
    return np.cos(ang).astype(np.float32), np.sin(ang).astype(np.float32)


def _cs_tiles(c, s):
    csq = np.concatenate([c, c], 1).T.copy()
    ssq = np.concatenate([-s, s], 1).T.copy()
    return csq, ssq


def _mask_tiles(q0):
    mt = np.zeros((32, 2, 128, 512), np.float32)
    ki = np.arange(128)[:, None]
    for kt in range(32):
        for qb in range(2):
            qj = q0 + qb * 512 + np.arange(512)[None, :]
            mt[kt, qb] = np.where(qj >= kt * 128 + ki, 0.0, -1e30)
    return mt


def _run(nc, maps):
    res = run_bass_kernel_spmd(nc, maps, core_ids=list(range(NCORES)))
    return res.results


_NCCACHE = {}


def _bld(key, fn, *a):
    if key not in _NCCACHE:
        _NCCACHE[key] = fn(*a)
    return _NCCACHE[key]


def kernel18(**inputs):
    I = {k: np.asarray(v, dtype=np.float32) for k, v in inputs.items()}
    ident = _ident()
    identb = ident.astype(bf)
    ones512 = np.ones((1, 512), np.float32).astype(bf)

    x_fm = to_fm(I["L0_attn_hidden_states"])

    nc = _bld("prep", build_prep)
    maps = []
    kT = np.ascontiguousarray(I["L2_ffn_kernel"].T)
    for c in range(NCORES):
        maps.append({
            "gsl": I["L0_ffn_gate_w"][c * 1024:(c + 1) * 1024],
            "usl": I["L0_ffn_up_w"][c * 1024:(c + 1) * 1024],
            "dsl": I["L0_ffn_down_w"][c * 256:(c + 1) * 256],
            "ksl": kT[c * 256:(c + 1) * 256],
            "bias2": I["L2_ffn_bias"][None, :],
            "relp": I["L1_attn_rel_pos"],
        })
    r = _run(nc, maps)
    gq = np.concatenate([r[c]["gq"] for c in range(8)], 0)
    uq = np.concatenate([r[c]["uq"] for c in range(8)], 0)
    dq = np.concatenate([r[c]["dq"] for c in range(8)], 0)
    kqi = np.concatenate([r[c]["kq"] for c in range(8)], 0)
    ksc = np.concatenate([r[c]["ksc"].reshape(-1) for c in range(8)], 0)
    bqv = r[0]["bqv"][0]
    bv8 = r[0]["bv8"][:, 0]

    nc = _bld("ln0", build_combine_ln, False)
    r = _run(nc, [{"x": s} for s in shard_tok(x_fm)])
    xn0 = np.concatenate([r[c]["xn"] for c in range(8)], 2)

    nc = _bld("l0a", build_l0_attn)
    maps = l0_attn_host_inputs(xn0)
    for c in range(NCORES):
        heads = list(range(c * L0_HPC, (c + 1) * L0_HPC))
        rows = []
        for part in range(3):
            for h in heads:
                rows.extend(range(h * 192 + part * 64, h * 192 + part * 64 + 64))
        maps[c]["wq"] = to_fm(I["L0_attn_qkv_w"][rows, :])
    r = _run(nc, maps)
    attn_fm = np.zeros((2048, NTOK), np.float32)
    for c in range(8):
        at = r[c]["attn"].astype(np.float32)
        for i in range(4):
            attn_fm[(c * 4 + i) * 64:(c * 4 + i) * 64 + 64, :] = at[i]
    attn_fm = attn_fm.reshape(DC, 128, NTOK)

    nc = _bld("l0p", build_l0_post)
    wg = to_fm(I["L0_attn_gate_w"])
    wo0 = to_fm(I["L0_attn_out_w"])
    nw = I["L0_attn_norm_w"].reshape(DC, 128).T.copy()
    xn0sh = shard_tok(xn0)
    maps = [{"attn": np.ascontiguousarray(attn_fm[:, :, c * TPC:(c + 1) * TPC]).astype(bf),
             "xn": xn0sh[c], "nw_s" if False else "nw": nw, "wg": wg, "wo": wo0}
            for c in range(8)]
    r = _run(nc, maps)
    s0 = [r[c]["s0"] for c in range(8)]

    nc = _bld("l0f", build_l0_ffn)
    wgu = np.zeros((16, DC, 128, 1024), bf)
    for j in range(16):
        for hc in range(DC):
            wgu[j, hc, :, 0:512] = gq[j * 512:(j + 1) * 512, hc * 128:(hc + 1) * 128].T
            wgu[j, hc, :, 512:1024] = uq[j * 512:(j + 1) * 512, hc * 128:(hc + 1) * 128].T
    wdq = np.zeros((64, 128, 2048), bf)
    for ic in range(64):
        wdq[ic] = dq[:, ic * 128:(ic + 1) * 128].T
    maps = [{"xn": xn0sh[c], "idf": ident, "idb": identb, "wgu": wgu, "wdq": wdq}
            for c in range(8)]
    r = _run(nc, maps)
    g0 = [r[c]["g0"] for c in range(8)]

    nc = _bld("cl", build_combine_ln, True)
    xsh = shard_tok(x_fm)
    r = _run(nc, [{"x": xsh[c], "s": s0[c], "g": g0[c]} for c in range(8)])
    x1 = [r[c]["xnew"] for c in range(8)]
    xn1 = [r[c]["xn"] for c in range(8)]
    xn1f = np.concatenate(xn1, 2)

    nc = _bld("l1p", build_l1_proj)
    wq1, wk1, wv1 = to_fm(I["L1_attn_wq"]), to_fm(I["L1_attn_wk"]), to_fm(I["L1_attn_wv"])
    r = _run(nc, [{"xn": xn1[c], "wq": wq1, "wk": wk1, "wv": wv1} for c in range(8)])
    qf1 = [r[c]["qf"] for c in range(8)]
    kf1 = np.concatenate([r[c]["kf"] for c in range(8)], 2)
    vt1 = np.concatenate([r[c]["vt"] for c in range(8)], 0)

    nc = _bld("l1a", build_l1_attn)
    wo1 = to_fm(I["L1_attn_wo"])
    clipidx = np.clip(np.arange(-4095, 4096), -(1023 - 512), 511)
    maps = []
    flip = np.zeros((128, 128), np.float32)
    flip[np.arange(128), 127 - np.arange(128)] = 1.0
    nidx = np.arange(128 * QF_F)
    for c in range(8):
        sq_, q0 = c // 4, (c % 4) * TPC
        kfs = np.ascontiguousarray(kf1[:, :, sq_ * S:(sq_ + 1) * S])
        vte = vt1[sq_ * S:(sq_ + 1) * S].reshape(S, L1_H, 64)
        qdv = bv8[np.clip(4606 - q0 - nidx, 0, 1022)][None, :]
        maps.append({"qf": qf1[c], "kf": kfs, "vte": vte.astype(bf), "qdv": qdv.astype(bf),
                     "idb": flip.astype(bf), "ones512": ones512, "wo": wo1})
    r = _run(nc, maps)
    s1 = [r[c]["s1"] for c in range(8)]

    nc = _bld("l1m", build_l1_moe)
    cls = to_fm(I["L1_ffn_classifier_w"])
    wgu_e = np.zeros((8, DC, 128, 2048), np.float32)
    wd_e = np.zeros((8, 8, 128, 2048), np.float32)
    for e in range(8):
        for hc in range(DC):
            wgu_e[e, hc, :, 0:1024] = I["L1_ffn_gate_w"][e][:, hc * 128:(hc + 1) * 128].T
            wgu_e[e, hc, :, 1024:2048] = I["L1_ffn_up_w"][e][:, hc * 128:(hc + 1) * 128].T
        for ic in range(8):
            wd_e[e, ic] = I["L1_ffn_down_w"][e][:, ic * 128:(ic + 1) * 128].T
    r = _run(nc, [{"xn": xn1[c], "cls": cls, "ebias": I["L1_ffn_e_score_correction_bias"][None, :],
                   "idf": ident, "wgu_e": wgu_e, "wd_e": wd_e} for c in range(8)])
    g1 = [r[c]["g1"] for c in range(8)]

    nc = _bld("cl", build_combine_ln, True)
    r = _run(nc, [{"x": x1[c], "s": s1[c], "g": g1[c]} for c in range(8)])
    x2 = [r[c]["xnew"] for c in range(8)]
    xn2 = [r[c]["xn"] for c in range(8)]

    nc = _bld("mla2", build_mla_proj, 512, 1536)
    perm = _l2_perm()
    w_rk2 = I["L2_attn_w_rk"][:, perm]
    b_rk2 = I["L2_attn_b_rk"][perm]
    w_rq2 = I["L2_attn_w_rq"].reshape(1536, 16, 64)[:, :, perm].reshape(1536, 1024)
    b_rq2 = I["L2_attn_b_rq"].reshape(16, 64)[:, perm].reshape(1024)
    m2 = {"w1": _padw(I["L2_attn_w_dkv"], I["L2_attn_b_dkv"]),
          "w2": _padw(I["L2_attn_w_dq"], I["L2_attn_b_dq"]),
          "w3": _padw(w_rk2, b_rk2, 128),
          "w4": _padw(I["L2_attn_w_uk"], I["L2_attn_b_uk"]),
          "w5": _padw(I["L2_attn_w_uv"], I["L2_attn_b_uv"]),
          "w6": _padw(I["L2_attn_w_uq"], I["L2_attn_b_uq"]),
          "w7": _padw(w_rq2, b_rq2)}
    r = _run(nc, [dict(m2, xn=xn2[c]) for c in range(8)])
    qf2 = [r[c]["qf"] for c in range(8)]
    rq2 = [r[c]["rqr"] for c in range(8)]
    kf2 = np.concatenate([r[c]["kf"] for c in range(8)], 2)
    rk2 = np.concatenate([r[c]["rkr"][0:64] for c in range(8)], 1)
    vt2 = np.concatenate([r[c]["vt"] for c in range(8)], 0)

    nc = _bld("ca", build_causal_attn)
    wo2 = to_fm(I["L2_attn_w_o"].T)
    bo2 = I["L2_attn_b_o"].reshape(DC, 128).T.copy()
    maps = []
    for c in range(8):
        sq_, q0 = c // 4, (c % 4) * TPC
        posq = np.arange(q0, q0 + TPC)
        posk = np.arange(S)
        cq, sq2 = _rope_tabs_l2(posq)
        ck, sk2 = _rope_tabs_l2(posk)
        csq, ssq = _cs_tiles(cq, sq2)
        csk, ssk = _cs_tiles(ck, sk2)
        maps.append({"qf": qf2[c], "rqr": rq2[c],
                     "kf": np.ascontiguousarray(kf2[:, :, sq_ * S:(sq_ + 1) * S]),
                     "rkr_f": np.ascontiguousarray(rk2[:, sq_ * S:(sq_ + 1) * S]).astype(bf),
                     "vt": vt2[sq_ * S:(sq_ + 1) * S].astype(bf),
                     "mt": _mask_tiles(q0).astype(bf),
                     "csq": csq.astype(bf), "ssq": ssq.astype(bf),
                     "csk": csk.astype(bf), "ssk": ssk.astype(bf),
                     "ones512": ones512, "idb": identb, "wo": wo2, "boc": bo2})
    r = _run(nc, maps)
    s2 = [r[c]["s2"] for c in range(8)]

    nc = _bld("l2f", build_l2_ffn)
    kqw = to_fm(kqi.astype(np.float32)).astype(bf)
    bqc = bqv.reshape(DC, 128).T.copy()
    kscc = ksc.reshape(DC, 128).T.copy()
    r = _run(nc, [{"xn": xn2[c], "kqw": kqw, "bqc": bqc, "kscc": kscc, "idf": ident}
                  for c in range(8)])
    g2 = [r[c]["g2"] for c in range(8)]

    nc = _bld("cl", build_combine_ln, True)
    r = _run(nc, [{"x": x2[c], "s": s2[c], "g": g2[c]} for c in range(8)])
    x3 = [r[c]["xnew"] for c in range(8)]
    xn3 = [r[c]["xn"] for c in range(8)]

    nc = _bld("mla3", build_mla_proj, 512, 512)
    z512 = np.zeros(512, np.float32)
    zD = np.zeros(D, np.float32)
    m3 = {"w1": _padw(I["L3_attn_Wkvd"], z512),
          "w2": _padw(I["L3_attn_Wqd"], z512),
          "w3": _padw(I["L3_attn_Wkr"], np.zeros(64, np.float32), 128),
          "w4": _padw(I["L3_attn_Wku"], zD),
          "w5": _padw(I["L3_attn_Wvu"], zD),
          "w6": _padw(I["L3_attn_Wqu"], zD),
          "w7": _padw(I["L3_attn_Wqr"], np.zeros(1024, np.float32))}
    r = _run(nc, [dict(m3, xn=xn3[c]) for c in range(8)])
    qf3 = [r[c]["qf"] for c in range(8)]
    rq3 = [r[c]["rqr"] for c in range(8)]
    kf3 = np.concatenate([r[c]["kf"] for c in range(8)], 2)
    rk3 = np.concatenate([r[c]["rkr"][0:64] for c in range(8)], 1)
    vt3 = np.concatenate([r[c]["vt"] for c in range(8)], 0)

    nc = _bld("ca", build_causal_attn)
    wo3 = to_fm(I["L3_attn_Wo"].T)
    bo3 = np.zeros((128, DC), np.float32)
    maps = []
    for c in range(8):
        sq_, q0 = c // 4, (c % 4) * TPC
        cq, sq2 = _rope_tabs_l3(np.arange(q0, q0 + TPC))
        ck, sk2 = _rope_tabs_l3(np.arange(S))
        csq, ssq = _cs_tiles(cq, sq2)
        csk, ssk = _cs_tiles(ck, sk2)
        maps.append({"qf": qf3[c], "rqr": rq3[c],
                     "kf": np.ascontiguousarray(kf3[:, :, sq_ * S:(sq_ + 1) * S]),
                     "rkr_f": np.ascontiguousarray(rk3[:, sq_ * S:(sq_ + 1) * S]).astype(bf),
                     "vt": vt3[sq_ * S:(sq_ + 1) * S].astype(bf),
                     "mt": _mask_tiles(q0).astype(bf),
                     "csq": csq.astype(bf), "ssq": ssq.astype(bf),
                     "csk": csk.astype(bf), "ssk": ssk.astype(bf),
                     "ones512": ones512, "idb": identb, "wo": wo3, "boc": bo3})
    r = _run(nc, maps)
    s3 = [r[c]["s2"] for c in range(8)]

    nc = _bld("l3f", build_l3_ffn)
    wrp = _padw(I["L3_ffn_Wr"], I["L3_ffn_br"])
    wdp = np.zeros((DC + 1, 128, 8, 16), np.float32)
    for e in range(8):
        for cc in range(DC):
            wdp[cc, :, e, :] = I["L3_ffn_Wd"][e][cc * 128:(cc + 1) * 128]
        wdp[DC, 0, e, :] = I["L3_ffn_bd"][e]
    wup = np.ascontiguousarray(np.transpose(I["L3_ffn_Wu"], (1, 0, 2)))
    r = _run(nc, [{"xn": xn3[c], "wr": wrp, "wdp": wdp, "wup": wup,
                   "bup": I["L3_ffn_bu"], "idf": ident} for c in range(8)])
    g3 = [r[c]["g3"] for c in range(8)]

    nc = _bld("cl", build_combine_ln, True)
    r = _run(nc, [{"x": x3[c], "s": s3[c], "g": g3[c]} for c in range(8)])
    out_fm = np.concatenate([r[c]["xnew"] for c in range(8)], 2)
    return from_fm(out_fm).reshape(B, S, D).astype(np.float32)


def kernel(**inputs):
    I = {k: np.asarray(v, dtype=np.float32) for k, v in inputs.items()}
    ident = _ident()
    identb = ident.astype(bf)
    ones512 = np.ones((1, 512), np.float32).astype(bf)
    flip = np.zeros((128, 128), np.float32)
    flip[np.arange(128), 127 - np.arange(128)] = 1.0
    flipb = flip.astype(bf)

    x_fm = to_fm(I["L0_attn_hidden_states"])
    xsh = shard_tok(x_fm)
    kT = np.ascontiguousarray(I["L2_ffn_kernel"].T)

    nc = _bld("LA", build_launchA)
    maps = []
    for c in range(NCORES):
        scl, scm = qscat_host(c)
        maps.append({
            "gsl": I["L0_ffn_gate_w"][c * 1024:(c + 1) * 1024],
            "usl": I["L0_ffn_up_w"][c * 1024:(c + 1) * 1024],
            "dsl": I["L0_ffn_down_w"][c * 256:(c + 1) * 256],
            "ksl": kT[c * 256:(c + 1) * 256],
            "bias2": I["L2_ffn_bias"][None, :],
            "relp": I["L1_attn_rel_pos"],
            "idp": identb, "scl": scl, "scm": scm, "x0": xsh[c],
        })
    rA = _run(nc, maps)
    wgu_full = np.concatenate([rA[c]["wguc"] for c in range(8)], 0)
    wdq_full = np.concatenate([rA[c]["wdqc"] for c in range(8)], 2)
    kqw_full = np.concatenate([rA[c]["kqwc"] for c in range(8)], 2)
    kscc = np.ascontiguousarray(np.concatenate([rA[c]["ksc"] for c in range(8)], 0).T)
    bqc = np.ascontiguousarray(rA[0]["bqv"][0].reshape(DC, 128).T)
    qdvs = [rA[c]["qdv"] for c in range(8)]
    xn0s = [rA[c]["xn0"] for c in range(8)]
    xn_full = np.concatenate(xn0s, 2)

    nc = _bld("LB", build_launchB)
    maps = l0_attn_host_inputs(xn_full)
    for c in range(NCORES):
        m = maps[c]
        m["xnf"] = m.pop("xn")
        heads = list(range(c * L0_HPC, (c + 1) * L0_HPC))
        rows = []
        for part in range(3):
            for h in heads:
                rows.extend(range(h * 192 + part * 64, h * 192 + part * 64 + 64))
        m["wq"] = to_fm(I["L0_attn_qkv_w"][rows, :])
    rB = _run(nc, maps)
    attn_fm = np.zeros((2048, NTOK), bf)
    for c in range(8):
        at = rB[c]["attn"]
        for i in range(4):
            attn_fm[(c * 4 + i) * 64:(c * 4 + i) * 64 + 64, :] = at[i]
    attn_fm = attn_fm.reshape(DC, 128, NTOK)

    nc = _bld("LC", build_launchC)
    wg = to_fm(I["L0_attn_gate_w"])
    wo0 = to_fm(I["L0_attn_out_w"])
    nw = np.ascontiguousarray(I["L0_attn_norm_w"].reshape(DC, 128).T)
    wq1, wk1, wv1 = to_fm(I["L1_attn_wq"]), to_fm(I["L1_attn_wk"]), to_fm(I["L1_attn_wv"])
    cls = to_fm(I["L1_ffn_classifier_w"])
    wgu_e = np.zeros((8, DC, 128, 2048), np.float32)
    wd_e = np.zeros((8, 8, 128, 2048), np.float32)
    for e in range(8):
        for hc in range(DC):
            wgu_e[e, hc, :, 0:1024] = I["L1_ffn_gate_w"][e][:, hc * 128:(hc + 1) * 128].T
            wgu_e[e, hc, :, 1024:2048] = I["L1_ffn_up_w"][e][:, hc * 128:(hc + 1) * 128].T
        for ic in range(8):
            wd_e[e, ic] = I["L1_ffn_down_w"][e][:, ic * 128:(ic + 1) * 128].T
    maps = [{"attn": np.ascontiguousarray(attn_fm[:, :, c * TPC:(c + 1) * TPC]),
             "xn0": xn0s[c], "x0": xsh[c], "nw": nw, "wg": wg, "wo0": wo0,
             "idf": ident, "idb": identb, "wgu": wgu_full, "wdq": wdq_full,
             "wq1": wq1, "wk1": wk1, "wv1": wv1,
             "cls": cls, "ebias": I["L1_ffn_e_score_correction_bias"][None, :],
             "idf2": ident, "wgu_e": wgu_e, "wd_e": wd_e} for c in range(8)]
    rC = _run(nc, maps)
    x1s = [rC[c]["x1"] for c in range(8)]
    qf1s = [rC[c]["qf1"] for c in range(8)]
    g1s = [rC[c]["g1"] for c in range(8)]
    kf1f = [np.concatenate([rC[c2]["kf1"] for c2 in range(sq * 4, sq * 4 + 4)], 2)
            for sq in range(2)]
    vte1f = [np.concatenate([rC[c2]["vte1"] for c2 in range(sq * 4, sq * 4 + 4)], 0)
             for sq in range(2)]

    nc = _bld("LD", build_launchD)
    wo1 = to_fm(I["L1_attn_wo"])
    perm = _l2_perm()
    w_rk2 = I["L2_attn_w_rk"][:, perm]
    b_rk2 = I["L2_attn_b_rk"][perm]
    w_rq2 = I["L2_attn_w_rq"].reshape(1536, 16, 64)[:, :, perm].reshape(1536, 1024)
    b_rq2 = I["L2_attn_b_rq"].reshape(16, 64)[:, perm].reshape(1024)
    m2 = {"m2w1": _padw(I["L2_attn_w_dkv"], I["L2_attn_b_dkv"]),
          "m2w2": _padw(I["L2_attn_w_dq"], I["L2_attn_b_dq"]),
          "m2w3": _padw(w_rk2, b_rk2, 128),
          "m2w4": _padw(I["L2_attn_w_uk"], I["L2_attn_b_uk"]),
          "m2w5": _padw(I["L2_attn_w_uv"], I["L2_attn_b_uv"]),
          "m2w6": _padw(I["L2_attn_w_uq"], I["L2_attn_b_uq"]),
          "m2w7": _padw(w_rq2, b_rq2)}
    maps = []
    for c in range(8):
        sq = c // 4
        maps.append(dict(m2, qf1=qf1s[c], kf1f=kf1f[sq], vte1f=vte1f[sq],
                         qdv=qdvs[c], flip=flipb, ones512=ones512, wo1=wo1,
                         x1=x1s[c], g1=g1s[c],
                         kqw=kqw_full, bqc=bqc, kscc=kscc, idf=ident))
    rD = _run(nc, maps)
    x2s = [rD[c]["x2"] for c in range(8)]
    qf2s = [rD[c]["qf2"] for c in range(8)]
    rq2s = [rD[c]["rq2"] for c in range(8)]
    g2s = [rD[c]["g2"] for c in range(8)]
    kf2f = [np.concatenate([rD[c2]["kf2"] for c2 in range(sq * 4, sq * 4 + 4)], 2)
            for sq in range(2)]
    rk2f = [np.ascontiguousarray(np.concatenate(
        [rD[c2]["rk2"] for c2 in range(sq * 4, sq * 4 + 4)], 1)[0:64]) for sq in range(2)]
    vt2f = [np.concatenate([rD[c2]["vt2"] for c2 in range(sq * 4, sq * 4 + 4)], 0)
            for sq in range(2)]

    nc = _bld("LE", build_launchE)
    wo2 = to_fm(I["L2_attn_w_o"].T)
    bo2 = np.ascontiguousarray(I["L2_attn_b_o"].reshape(DC, 128).T)
    z512 = np.zeros(512, np.float32)
    zD = np.zeros(D, np.float32)
    m3 = {"m3w1": _padw(I["L3_attn_Wkvd"], z512),
          "m3w2": _padw(I["L3_attn_Wqd"], z512),
          "m3w3": _padw(I["L3_attn_Wkr"], np.zeros(64, np.float32), 128),
          "m3w4": _padw(I["L3_attn_Wku"], zD),
          "m3w5": _padw(I["L3_attn_Wvu"], zD),
          "m3w6": _padw(I["L3_attn_Wqu"], zD),
          "m3w7": _padw(I["L3_attn_Wqr"], np.zeros(1024, np.float32))}
    wrp = _padw(I["L3_ffn_Wr"], I["L3_ffn_br"])
    wdp = np.zeros((DC + 1, 128, 8, 16), np.float32)
    for e in range(8):
        for cc in range(DC):
            wdp[cc, :, e, :] = I["L3_ffn_Wd"][e][cc * 128:(cc + 1) * 128]
        wdp[DC, 0, e, :] = I["L3_ffn_bd"][e]
    wup = np.ascontiguousarray(np.transpose(I["L3_ffn_Wu"], (1, 0, 2)))
    maps = []
    for c in range(8):
        sq, q0 = c // 4, (c % 4) * TPC
        cq, sq2 = _rope_tabs_l2(np.arange(q0, q0 + TPC))
        ck, sk2 = _rope_tabs_l2(np.arange(S))
        csq, ssq = _cs_tiles(cq, sq2)
        csk, ssk = _cs_tiles(ck, sk2)
        maps.append(dict(m3, qf=qf2s[c], rqr=rq2s[c], kff=kf2f[sq],
                         rkf=rk2f[sq], vtf=vt2f[sq],
                         mt=_mask_tiles(q0).astype(bf),
                         csq=csq.astype(bf), ssq=ssq.astype(bf),
                         csk=csk.astype(bf), ssk=ssk.astype(bf),
                         ones512=ones512, idb=identb, woc=wo2, boc=bo2,
                         xin=x2s[c], gin=g2s[c],
                         wr=wrp, wdp=wdp, wup=wup, bup=I["L3_ffn_bu"], idf=ident))
    rE = _run(nc, maps)
    x3s = [rE[c]["xout"] for c in range(8)]
    qf3s = [rE[c]["qf3"] for c in range(8)]
    rq3s = [rE[c]["rq3"] for c in range(8)]
    g3s = [rE[c]["g3"] for c in range(8)]
    kf3f = [np.concatenate([rE[c2]["kf3"] for c2 in range(sq * 4, sq * 4 + 4)], 2)
            for sq in range(2)]
    rk3f = [np.ascontiguousarray(np.concatenate(
        [rE[c2]["rk3"] for c2 in range(sq * 4, sq * 4 + 4)], 1)[0:64]) for sq in range(2)]
    vt3f = [np.concatenate([rE[c2]["vt3"] for c2 in range(sq * 4, sq * 4 + 4)], 0)
            for sq in range(2)]

    nc = _bld("LF", build_launchF)
    wo3 = to_fm(I["L3_attn_Wo"].T)
    bo3 = np.zeros((128, DC), np.float32)
    maps = []
    for c in range(8):
        sq, q0 = c // 4, (c % 4) * TPC
        cq, sq2 = _rope_tabs_l3(np.arange(q0, q0 + TPC))
        ck, sk2 = _rope_tabs_l3(np.arange(S))
        csq, ssq = _cs_tiles(cq, sq2)
        csk, ssk = _cs_tiles(ck, sk2)
        maps.append({"qf": qf3s[c], "rqr": rq3s[c], "kff": kf3f[sq],
                     "rkf": rk3f[sq], "vtf": vt3f[sq],
                     "mt": _mask_tiles(q0).astype(bf),
                     "csq": csq.astype(bf), "ssq": ssq.astype(bf),
                     "csk": csk.astype(bf), "ssk": ssk.astype(bf),
                     "ones512": ones512, "idb": identb, "woc": wo3, "boc": bo3,
                     "xin": x3s[c], "gin": g3s[c]})
    rF = _run(nc, maps)
    out_fm = np.concatenate([rF[c]["xout"] for c in range(8)], 2)
    return from_fm(out_fm).reshape(2, 4096, 2048)
